# version 08
# baseline (speedup 1.0000x reference)
import numpy as np
import concourse.bass as bass
import concourse.mybir as mybir
from concourse.bass_utils import run_bass_kernel_spmd

F32 = mybir.dt.float32
BF16 = mybir.dt.bfloat16
F16 = mybir.dt.float16
I32 = mybir.dt.int32
AF = mybir.ActivationFunctionType
ALU = mybir.AluOpType
AX = mybir.AxisListType

E = 1024
H = 16
D = 64
MLPD = 4096
LQ = 16384
NCORES = 8
TOK = 4096
TT = 512
HT = 256
NT = 8
HALO = 4
TOKH = TOK + HALO
EC = E // 128
MC = MLPD // 128
ROFF = 1536.0
EPS_LN = 1e-5
EPS_ATT = 1e-6

_TAPS = False
_STAGES = 5
_S1_TILES = None
_S1_FFN_ONLY = False


class Sem:
    def __init__(self, nc, name):
        self.h = nc.alloc_semaphore(name)
        self.n = 0

    def inc(self, instr, k=1):
        instr.then_inc(self.h, k)
        self.n += k
        return instr


class Builder:
    def __init__(self, taps=False, stages=None):
        self.taps = taps
        self.stages = _STAGES if stages is None else stages
        nc = bass.Bass("TRN2", target_bir_lowering=False)
        self.nc = nc

        di = {}
        di["x"] = nc.dram_tensor("x", [E, TOKH], F32, kind="ExternalInput").ap()
        for nm in ("wq", "wk", "wv", "wpq", "wpk"):
            di[nm] = nc.dram_tensor(nm, [E, E], F32, kind="ExternalInput").ap()
        di["w1"] = nc.dram_tensor("w1", [E, MLPD], F32, kind="ExternalInput").ap()
        di["w2"] = nc.dram_tensor("w2", [MLPD, E], F32, kind="ExternalInput").ap()
        di["wg"] = nc.dram_tensor("wg", [E, MLPD], F32, kind="ExternalInput").ap()
        di["wu"] = nc.dram_tensor("wu", [E, MLPD], F32, kind="ExternalInput").ap()
        di["wd"] = nc.dram_tensor("wd", [MLPD, E], F32, kind="ExternalInput").ap()
        di["bias"] = nc.dram_tensor("bias", [128, 160], F32, kind="ExternalInput").ap()
        di["b1p"] = nc.dram_tensor("b1p", [128, 32], F32, kind="ExternalInput").ap()
        di["convbd"] = nc.dram_tensor("convbd", [128, 5120], F32, kind="ExternalInput").ap()
        di["eye"] = nc.dram_tensor("eye", [128, 128], F32, kind="ExternalInput").ap()
        di["selp"] = nc.dram_tensor("selp", [16, 264], F32, kind="ExternalInput").ap()
        di["mask"] = nc.dram_tensor("mask", [1, 4], F32, kind="ExternalInput").ap()
        self.di = di
        self.out = nc.dram_tensor("out", [E, TOK], F32, kind="ExternalOutput").ap()

        ds = {}
        for nm in ("s_y1", "s_v", "s_pq", "s_kk", "s_fk", "s_vv"):
            ds[nm] = nc.dram_tensor(nm, [E, TOKH], F32).ap()
        for nm in ("q", "k", "v", "pq", "pk"):
            ds["wsh_" + nm] = nc.dram_tensor("wsh_" + nm, [128, 8192], BF16).ap()
            ds["wsl_" + nm] = nc.dram_tensor("wsl_" + nm, [128, 8192], BF16).ap()
        for nm in ("w1h", "w1l", "wgq", "wuq"):
            ds["ws_" + nm] = nc.dram_tensor("ws_" + nm, [128, 32768], BF16).ap()
        for nm in ("w2h", "w2l", "wdq"):
            ds["ws_" + nm] = nc.dram_tensor("ws_" + nm, [128, 32768], BF16).ap()
        ds["cb_qg_i"] = nc.dram_tensor("cb_qg_i", [128, 8], F32).ap()
        ds["cb_qg_o"] = nc.dram_tensor("cb_qg_o", [128, 8], F32).ap()
        ds["cb_ag_i"] = nc.dram_tensor("cb_ag_i", [128, 536], F32).ap()
        ds["cb_ag_o"] = nc.dram_tensor("cb_ag_o", [512, 536], F32).ap()
        self.ds = ds

        self.AR = nc.alloc_sbuf_tensor("AR", [128, 31808], F32).ap()
        self.SL_X = 0
        self.SL_Y1 = 4096
        self.SL_V = 8256
        self.SL_RAW = 12352
        self.SL_NORM = 16448
        self.SL_GU = 20544
        self.SL_BC = 28736
        self.SL_MISC = 30784
        self.AB = nc.alloc_sbuf_tensor("AB", [128, 22528], BF16).ap()
        self.BL_UQ = 0
        self.BL_VH = 4096
        self.BL_VL = 8192
        self.BL_WP = 12288
        self.BL_T = 20480
        self.T0 = nc.alloc_sbuf_tensor("T0", [128, 512], F32).ap()
        self.T1 = nc.alloc_sbuf_tensor("T1", [128, 512], F32).ap()
        self.TH = nc.alloc_sbuf_tensor("TH", [128, 512], F16).ap()
        self.TH2 = nc.alloc_sbuf_tensor("TH2", [128, 512], F16).ap()
        self.FEN = nc.alloc_sbuf_tensor("FEN", [128, 8], F32).ap()
        self.TI = nc.alloc_sbuf_tensor("TI", [128, 512], I32).ap()
        self.ST = nc.alloc_sbuf_tensor("ST", [128, 320], F32).ap()
        self.WS = nc.alloc_sbuf_tensor("WS", [1, 16], F32).ap()
        self.SR0 = nc.alloc_sbuf_tensor("SR0", [1, 512], F32).ap()
        self.SR1 = nc.alloc_sbuf_tensor("SR1", [1, 512], F32).ap()
        self.REPB = nc.alloc_sbuf_tensor("REPB", [128, 1024], F32).ap()
        self.ONES128 = nc.alloc_sbuf_tensor("ONES128", [128, 128], F32).ap()
        self.MROW = nc.alloc_sbuf_tensor("MROW", [1, 4], F32).ap()
        self.KVSB = nc.alloc_sbuf_tensor("KVSB", [128, 536], F32).ap()
        self.QG = nc.alloc_sbuf_tensor("QG", [128, 16], F32).ap()
        self.MX = nc.alloc_sbuf_tensor("MX", [2, 64], F32).ap()
        self.ETM = nc.alloc_sbuf_tensor("ETM", [128, 64], F32).ap()
        self.QGE = nc.alloc_sbuf_tensor("QGE", [128, 16], F32).ap()
        self.KSE = nc.alloc_sbuf_tensor("KSE", [128, 16], F32).ap()
        self.EYE = nc.alloc_sbuf_tensor("EYE", [128, 128], F32).ap()
        self.CONST = nc.alloc_sbuf_tensor("CONST", [128, 12], F32).ap()
        self.ONESR = nc.alloc_sbuf_tensor("ONESR", [1, 128], F32).ap()
        self.SELP = nc.alloc_sbuf_tensor("SELP", [16, 264], F32).ap()
        self.BIAS = nc.alloc_sbuf_tensor("BIAS", [128, 160], F32).ap()
        self.B1P = nc.alloc_sbuf_tensor("B1P", [128, 32], F32).ap()
        self.MASKB = nc.alloc_sbuf_tensor("MASKB", [128, 4], F32).ap()
        self.MINIA = nc.alloc_sbuf_tensor("MINIA", [128, 32], F32).ap()
        self.PK0 = nc.alloc_sbuf_tensor("PK0", [128, 16], F32).ap()
        self.COMB = nc.alloc_sbuf_tensor("COMB", [2, 64], F32).ap()
        self.ESB = self.AR[0:2, self.SL_GU:self.SL_GU + 4096]
        self.DEN = self.AR[0:2, self.SL_GU + 4096:self.SL_GU + 5120 + 1024]
        self.GCB = nc.alloc_sbuf_tensor("GCB", [128, 8], F32).ap()
        self.VTMO = None

        self.PS = [nc.alloc_psum_tensor(f"ps{i}", [128, 512], F32).ap() for i in range(8)]
        self.guard = [None] * 8
        self.wp_pe = 0

        nc_ = nc
        self.s_sy = Sem(nc_, "s_sy")
        self.s_sc = Sem(nc_, "s_sc")
        self.s_gp = Sem(nc_, "s_gp")
        self.s_st = Sem(nc_, "s_st")
        self.c_pe = Sem(nc_, "c_pe")
        self.c_ve = Sem(nc_, "c_ve")
        self.c_ac = Sem(nc_, "c_ac")
        self.c_gc = Sem(nc_, "c_gc")
        self.c_cc = Sem(nc_, "c_cc")
        self.esem = {"sync": Sem(nc_, "b_sy"), "vector": Sem(nc_, "b_ve"),
                     "scalar": Sem(nc_, "b_sc"), "tensor": Sem(nc_, "b_pe"),
                     "gpsimd": Sem(nc_, "b_gp")}

        self._tmv = {}
        self._tmn = 0

        self.tap_t = {}
        if taps:
            for nm in ("s_y1", "s_v", "s_pq", "s_kk", "s_fk", "s_vv"):
                self.tap_t[nm] = nc.dram_tensor("tap_" + nm, [E, TOKH], F32,
                                                kind="ExternalOutput").ap()
            self.tap_t["qg"] = nc.dram_tensor("tap_qg", [128, 16], F32,
                                              kind="ExternalOutput").ap()
            self.tap_t["kv"] = nc.dram_tensor("tap_kv", [128, 536], F32,
                                              kind="ExternalOutput").ap()
            self.tap_t["ws_wgq"] = nc.dram_tensor("tap_ws_wgq", [128, 32768], BF16,
                                                  kind="ExternalOutput").ap()
            self.tap_t["ws_wdq"] = nc.dram_tensor("tap_ws_wdq", [128, 32768], BF16,
                                                  kind="ExternalOutput").ap()
            self.tap_t["wsh_q"] = nc.dram_tensor("tap_wsh_q", [128, 8192], BF16,
                                                 kind="ExternalOutput").ap()
            self.tap_t["wsl_q"] = nc.dram_tensor("tap_wsl_q", [128, 8192], BF16,
                                                 kind="ExternalOutput").ap()
            self.tap_t["ws_w2h"] = nc.dram_tensor("tap_ws_w2h", [128, 32768], BF16,
                                                  kind="ExternalOutput").ap()
            self.tap_t["wsr"] = nc.dram_tensor("tap_wsr", [1, 16], F32,
                                               kind="ExternalOutput").ap()
            self.tap_t["uq"] = nc.dram_tensor("tap_uq", [128, 4096], BF16,
                                              kind="ExternalOutput").ap()
            self.tap_t["gu"] = nc.dram_tensor("tap_gu", [128, 8192], F32,
                                              kind="ExternalOutput").ap()
            self.tap_t["bc"] = nc.dram_tensor("tap_bc", [128, 3072], F32,
                                              kind="ExternalOutput").ap()
            self.tap_t["st"] = nc.dram_tensor("tap_st", [128, 320], F32,
                                              kind="ExternalOutput").ap()
            self.tap_t["const"] = nc.dram_tensor("tap_const", [128, 12], F32,
                                                 kind="ExternalOutput").ap()

    def sec(self, eng, fn):
        getattr(self.block, eng)(fn)

    def vfence(self, vector):
        vector.drain()
        return vector.memset(self.FEN[:, 0:1], 0.0)

    def tm(self, name, nblk=4):
        if name not in self._tmv:
            self._tmv[name] = self._tmn
            self._tmn += 4
            assert self._tmn <= 320
        o = self._tmv[name]
        return self.ST[:, o:o + nblk]

    def bank(self, i, eng):
        g = self.guard[i]
        if g is not None:
            eng.wait_ge(g[0].h, g[1])
            self.guard[i] = None
        return self.PS[i]

    def set_guard(self, i, sem):
        self.guard[i] = (sem, sem.n)

    def barrier(self, engs=("sync", "vector", "scalar", "tensor", "gpsimd")):
        counts = {}
        dma_waits = {"sync": [self.s_sy], "scalar": [self.s_sc],
                     "gpsimd": [self.s_gp, self.s_st]}
        for e in engs:
            s = self.esem[e]

            def f(eng, s=s, e=e):
                for dsem in dma_waits.get(e, ()):
                    eng.wait_ge(dsem.h, dsem.n)
                if e == "tensor":
                    eng.drain()
                    eng.sem_inc(s.h, 1)
                elif e == "vector":
                    eng.drain()
                    eng.memset(self.FEN[:, 1:2], 0.0).then_inc(s.h, 1)
                elif e == "gpsimd":
                    eng.memset(self.FEN[:, 2:3], 0.0).then_inc(s.h, 1)
                elif e == "scalar":
                    eng.activation(self.FEN[:, 3:4], self.FEN[:, 3:4],
                                   AF.Identity).then_inc(s.h, 1)
                else:
                    eng.sem_inc(s.h, 1)
            self.sec(e, f)
            s.n += 1
            counts[e] = s.n
        for e in engs:
            def f(eng, e=e):
                for o, c in counts.items():
                    if o != e:
                        eng.wait_ge(self.esem[o].h, c)
            self.sec(e, f)

    def fm(self, dram, c0, w):
        return dram.rearrange("(c p) t -> p c t", p=128)[:, :, c0:c0 + w]

    def arv(self, sl, w8):
        return self.AR[:, sl:sl + 8 * w8].rearrange("p (c t) -> p c t", t=w8)

    def chunk(self, sl, c, w, off=0):
        base = sl + c * w if off == 0 else sl + off + c * w
        return self.AR[:, base:base + w]

    def mm3(self, te, out, wh, wl, xh, xl, start, stop):
        te.matmul(out, wh, xh, start=start, stop=False)
        te.matmul(out, wl, xh, start=False, stop=False)
        te.matmul(out, wh, xl, start=False, stop=stop)

    def build(self):
        nc = self.nc
        with nc.Block() as block:
            self.block = block
            self.emit_preamble()
            if self.stages >= 1:
                self.emit_weight_prep()
            if self.stages >= 2:
                self.emit_sweep1()
            if self.stages >= 3:
                self.emit_qg_collective()
            if self.stages >= 4:
                self.emit_sweep2()
            if self.stages >= 5:
                self.emit_sweep3()
            self.emit_taps_and_close()
        return nc

    def emit_preamble(self):
        di = self.di

        def f(sync):
            self.s_sy.inc(sync.dma_start(out=self.EYE[:], in_=di["eye"][:]), 16)
            self.s_sy.inc(sync.dma_start(out=self.SELP[:], in_=di["selp"][:]), 16)
            self.s_sy.inc(sync.dma_start(out=self.BIAS[:], in_=di["bias"][:]), 16)
            self.s_sy.inc(sync.dma_start(out=self.B1P[:], in_=di["b1p"][:]), 16)
            self.s_sy.inc(sync.dma_start(out=self.MROW[:], in_=di["mask"][:]), 16)
        self.sec("sync", f)
        ld = self.s_sy.n

        def f(vector):
            vector.memset(self.CONST[:, 0:1], 1.0)
            vector.memset(self.ONESR[:], 1.0)
            vector.memset(self.ONES128[:], 1.0)
            vector.memset(self.QG[:], 0.0)
            vector.wait_ge(self.s_sy.h, self.s_sy.n)
            vector.tensor_scalar(self.GCB[:], self.BIAS[:, 112:120],
                                 self.BIAS[:, 120:121], None, ALU.mult)
            self.c_ve.inc(vector.memset(self.MX[:, 2:3], 0.0))
        self.sec("vector", f)

        def f(tensor):
            tensor.wait_ge(self.s_sy.h, ld)
            tensor.wait_ge(self.c_ve.h, self.c_ve.n)
            self.c_pe.inc(tensor.matmul(self.PS[0][:, 0:4], self.ONESR[:],
                                        self.MROW[:], start=True, stop=True))
        self.sec("tensor", f)

        def f(scalar):
            scalar.wait_ge(self.s_sy.h, ld)
            self.c_ac.inc(scalar.activation(self.PK0[:, 8:16], self.BIAS[:, 32:40], AF.Tanh))
        self.sec("scalar", f)

        def f(vector):
            vector.wait_ge(self.c_pe.h, self.c_pe.n)
            vector.tensor_copy(self.MASKB[:], self.PS[0][:, 0:4])
            vector.wait_ge(self.c_ac.h, self.c_ac.n)
            self.c_ve.inc(vector.tensor_scalar(self.PK0[:, 0:8], self.PK0[:, 8:16],
                                               1.0, None, ALU.add))
        self.sec("vector", f)

    def emit_weight_prep(self):
        di, ds = self.di, self.ds
        GU0 = self.SL_GU
        GU1 = self.SL_GU + 4096
        NORM = self.SL_NORM

        specs = [("wg", 8, 4096, 0), ("wu", 8, 4096, 1), ("wd", 32, 1024, 2)]
        for wname, nchunk, outw, rowname in specs:
            ldm, acm, vem = [], [], []
            ac0, ve0 = self.c_ac.n, self.c_ve.n
            for c in range(nchunk):
                half = GU0 if c % 2 == 0 else GU1

                def f(sync, c=c, half=half):
                    if c < 2:
                        sync.wait_ge(self.c_ac.h, ac0)
                        sync.wait_ge(self.c_ve.h, ve0)
                    if c >= 2:
                        sync.wait_ge(self.c_ac.h, acm[c - 2])
                    self.s_sy.inc(sync.dma_start(
                        out=self.AR[:, half:half + outw],
                        in_=di[wname][c * 128:(c + 1) * 128, :]), 16)
                self.sec("sync", f)
                ldm.append(self.s_sy.n)

                def f(scalar, c=c, half=half):
                    scalar.wait_ge(self.s_sy.h, ldm[c])
                    if c == 0:
                        scalar.wait_ge(self.c_ve.h, ve0)
                    if c >= 1:
                        scalar.wait_ge(self.c_ve.h, vem[c - 1])
                    self.c_ac.inc(scalar.activation(
                        self.AR[:, NORM:NORM + outw], self.AR[:, half:half + outw], AF.Abs))
                self.sec("scalar", f)
                acm.append(self.c_ac.n)

                def f(vector, c=c):
                    nseg = outw // 512
                    vector.wait_ge(self.c_ac.h, acm[c])
                    for j in range(nseg):
                        vector.reduce_sum(self.T0[:, j:j + 1],
                                          self.AR[:, NORM + j * 512:NORM + (j + 1) * 512], AX.X)
                    vector.drain()
                    self.c_ve.inc(vector.reduce_sum(
                        self.T0[:, 16 + (c % 32):17 + (c % 32)], self.T0[:, 0:nseg], AX.X))
                self.sec("vector", f)
                vem.append(self.c_ve.n)

            def f(vector, nchunk=nchunk, vem=vem):
                vector.wait_ge(self.c_ve.h, vem[-1])
                vector.wait_ge(self.c_pe.h, self.c_pe.n)
                vector.drain()
                self.c_ve.inc(vector.reduce_sum(self.T0[:, 60:61],
                                                self.T0[:, 16:16 + nchunk], AX.X))
            self.sec("vector", f)

            def f(tensor):
                ps = self.bank(1, tensor)
                tensor.wait_ge(self.c_ve.h, self.c_ve.n)
                self.c_pe.inc(tensor.matmul(ps[0:1, 0:1], self.CONST[:, 0:1],
                                            self.T0[:, 60:61], start=True, stop=True))
            self.sec("tensor", f)

            def f(vector):
                vector.wait_ge(self.c_pe.h, self.c_pe.n)
                self.c_ve.inc(vector.tensor_copy(self.WS[0:1, rowname:rowname + 1],
                                                 self.PS[1][0:1, 0:1]))
            self.sec("vector", f)
            self.set_guard(1, self.c_ve)

        def f(tensor):
            ps = self.bank(1, tensor)
            tensor.wait_ge(self.c_ve.h, self.c_ve.n)
            tensor.matmul(ps[:, 0:1], self.ONESR[:], self.WS[0:1, 0:1], start=True, stop=True)
            tensor.matmul(ps[:, 1:2], self.ONESR[:], self.WS[0:1, 1:2], start=True, stop=True)
            self.c_pe.inc(tensor.matmul(ps[:, 2:3], self.ONESR[:], self.WS[0:1, 2:3],
                                        start=True, stop=True))
        self.sec("tensor", f)

        def f(vector):
            vector.wait_ge(self.c_pe.h, self.c_pe.n)
            C = self.CONST
            specs2 = [(0, 8 * 128 * 4096, 2, 5), (1, 8 * 128 * 4096, 3, 6),
                      (2, 32 * 128 * 1024, 4, 7)]
            last = None
            for i, nelem, c127, csw in specs2:
                vector.tensor_scalar(C[:, 8:9], self.PS[1][:, i:i + 1],
                                     1.0 / nelem, None, ALU.mult)
                vector.drain()
                vector.tensor_scalar(C[:, 9:10], C[:, 8:9], 1e-5, None, ALU.max)
                vector.drain()
                fac = (0.5 / 127.0) if i == 2 else (1.0 / 127.0)
                vector.tensor_scalar(C[:, c127:c127 + 1], C[:, 9:10], fac, None, ALU.mult)
                last = vector.reciprocal(C[:, csw:csw + 1], C[:, 9:10])
                vector.drain()
            self.c_ve.inc(last)
        self.sec("vector", f)
        self.set_guard(1, self.c_ve)

        qspecs = [("wg", "ws_wgq", 8, 4096, 5, True),
                  ("wu", "ws_wuq", 8, 4096, 6, True),
                  ("wd", "ws_wdq", 32, 1024, 7, False)]
        qstm = []
        for wname, sname, nchunk, outw, swcol, fo_outer in qspecs:
            swmark = self.c_ve.n

            ldm, vem, stm = [], [], []
            ve0 = self.c_ve.n
            ac0 = self.c_ac.n
            for c in range(nchunk):
                half = GU0 if c % 2 == 0 else GU1
                g = len(qstm)

                def f(sync, c=c, half=half):
                    if c < 2:
                        sync.wait_ge(self.c_ve.h, max(swmark, ve0))
                        sync.wait_ge(self.c_ac.h, ac0)
                    if c >= 2:
                        sync.wait_ge(self.c_ve.h, vem[c - 2])
                    self.s_sy.inc(sync.dma_start(
                        out=self.AR[:, half:half + outw],
                        in_=di[wname][c * 128:(c + 1) * 128, :]), 16)
                self.sec("sync", f)
                ldm.append(self.s_sy.n)

                def f(vector, c=c, half=half, g=g):
                    nseg = outw // 512
                    vector.wait_ge(self.s_sy.h, ldm[c])
                    if g >= 2:
                        vector.wait_ge(self.s_st.h, qstm[g - 2])
                    wp = self.BL_WP + ((c % 2) * 4096 if outw == 4096
                                       else (c % 4) * 2048)
                    tba = self.TH
                    tbb = self.TH2
                    for j in range(nseg):
                        src = self.AR[:, half + j * 512:half + (j + 1) * 512]
                        dst = self.AB[:, wp + j * 512:wp + (j + 1) * 512]
                        vector.tensor_scalar(self.T0[:], src,
                                             self.CONST[:, swcol:swcol + 1], None, ALU.mult)
                        vector.tensor_scalar(tba, self.T0[:], ROFF, None, ALU.add)
                        vector.tensor_scalar(tbb, tba, -ROFF, 1.0, ALU.add, ALU.min)
                        vector.tensor_scalar(dst, tbb, -1.0, None, ALU.max)
                    self.c_ve.inc(self.vfence(vector))
                self.sec("vector", f)
                vem.append(self.c_ve.n)

                def f(gpsimd, c=c):
                    gpsimd.wait_ge(self.c_ve.h, vem[c])
                    wp = self.BL_WP + ((c % 2) * 4096 if outw == 4096
                                       else (c % 4) * 2048)
                    buf = self.AB[:, wp:wp + outw]
                    if fo_outer:
                        dst = self.ds[sname].rearrange("p (a b) -> p a b", b=2048)[
                            :, :, c * 256:(c + 1) * 256]
                        src = buf.rearrange("p (a b) -> p a b", b=256)
                    else:
                        dst = self.ds[sname][:, c * 1024:(c + 1) * 1024]
                        src = buf
                    self.s_st.inc(gpsimd.dma_start(out=dst, in_=src), 16)
                self.sec("gpsimd", f)
                stm.append(self.s_st.n)
                qstm.append(self.s_st.n)

        sspecs = [("wq", "wsh_q", "wsl_q", 8, 1024, True),
                  ("wk", "wsh_k", "wsl_k", 8, 1024, True),
                  ("wv", "wsh_v", "wsl_v", 8, 1024, True),
                  ("wpq", "wsh_pq", "wsl_pq", 8, 1024, True),
                  ("wpk", "wsh_pk", "wsl_pk", 8, 1024, True),
                  ("w1", "ws_w1h", "ws_w1l", 8, 4096, True),
                  ("w2", "ws_w2h", "ws_w2l", 32, 1024, False)]
        for wname, hname, lname, nchunk, outw, fo_outer in sspecs:
            ldm, vem, stm = [], [], []
            ve0 = self.c_ve.n
            st0 = self.s_st.n
            for c in range(nchunk):
                half = GU0 if c % 2 == 0 else GU1
                hb = self.AB[:, (c % 2) * 4096:(c % 2) * 4096 + outw]
                lb = self.AB[:, 8192 + (c % 2) * 4096:8192 + (c % 2) * 4096 + outw]

                def f(sync, c=c, half=half):
                    if c < 2:
                        sync.wait_ge(self.c_ve.h, ve0)
                    if c >= 2:
                        sync.wait_ge(self.c_ve.h, vem[c - 2])
                    self.s_sy.inc(sync.dma_start(
                        out=self.AR[:, half:half + outw],
                        in_=di[wname][c * 128:(c + 1) * 128, :]), 16)
                self.sec("sync", f)
                ldm.append(self.s_sy.n)

                def f(vector, c=c, half=half, hb=hb, lb=lb):
                    vector.wait_ge(self.s_sy.h, ldm[c])
                    if c < 2:
                        vector.wait_ge(self.s_st.h, st0)
                    if c >= 2:
                        vector.wait_ge(self.s_st.h, stm[2 * (c - 2) + 1])
                    src = self.AR[:, half:half + outw]
                    vector.tensor_copy(hb, src)
                    self.c_ve.inc(vector.tensor_sub(lb, src, hb))
                self.sec("vector", f)
                vem.append(self.c_ve.n)

                def f(gpsimd, c=c, hb=hb, lb=lb):
                    gpsimd.wait_ge(self.c_ve.h, vem[c])
                    for name, buf in ((hname, hb), (lname, lb)):
                        if fo_outer:
                            nfo2 = outw * 8 // 2048
                            dst = self.ds[name].rearrange("p (a b) -> p a b", b=2048)[
                                :, :, c * 256:(c + 1) * 256]
                            src = buf.rearrange("p (a b) -> p a b", b=256)
                        else:
                            dst = self.ds[name][:, c * 1024:(c + 1) * 1024]
                            src = buf
                        self.s_st.inc(gpsimd.dma_start(out=dst, in_=src), 16)
                        stm.append(self.s_st.n)
                self.sec("gpsimd", f)
        self.barrier()

    def emit_sums(self, src_sl, w, tmpsq_sl, bank, bank2=5, waits=(), nch=8):
        acm = []

        def fsq(scalar):
            for sm, cnt in waits:
                scalar.wait_ge(sm.h, cnt)
            for c in range(nch):
                s = self.AR[:, src_sl + c * w:src_sl + c * w + w]
                d = self.AR[:, tmpsq_sl + c * w:tmpsq_sl + c * w + w]
                self.c_ac.inc(scalar.activation(d, s, AF.Square))
                acm.append(self.c_ac.n)
        self.sec("scalar", fsq)

        def fmm(tensor):
            ps = self.bank(bank, tensor)
            ps2 = self.bank(bank2, tensor)
            for sm, cnt in waits:
                tensor.wait_ge(sm.h, cnt)
            for c in range(nch):
                s = self.AR[:, src_sl + c * w:src_sl + c * w + w]
                tensor.matmul(ps[0:1, 0:w], self.CONST[:, 0:1], s,
                              start=(c == 0), stop=(c == nch - 1))
            mm = None
            for c in range(nch):
                tensor.wait_ge(self.c_ac.h, acm[c])
                d = self.AR[:, tmpsq_sl + c * w:tmpsq_sl + c * w + w]
                mm = tensor.matmul(ps2[0:1, 0:w], self.CONST[:, 0:1], d,
                                   start=(c == 0), stop=(c == nch - 1))
            self.c_pe.inc(mm)
        self.sec("tensor", fmm)
        return self.c_pe.n

    def emit_minmax(self, src_sl, w, bank, mxname, mnname, waits=(), nch=8):
        BCT = self.SL_BC

        def f(vector):
            for sm, cnt in waits:
                vector.wait_ge(sm.h, cnt)
            last = None
            for which in (0, 1):
                op = ALU.max if which == 0 else ALU.min
                b0 = BCT + (0 + 2 * which) * 512
                b1 = BCT + (1 + 2 * which) * 512
                cur = src_sl
                dsts = [b0, b1]
                for c in range(1, nch):
                    sc = self.AR[:, src_sl + c * w:src_sl + c * w + w]
                    d = dsts[(c - 1) % 2]
                    last = vector.tensor_tensor(self.AR[:, d:d + w],
                                                self.AR[:, cur:cur + w], sc, op)
                    cur = d
            self.c_ve.inc(last)
        self.sec("vector", f)
        fin = BCT + ((nch - 2) % 2) * 512
        self.emit_minmax_rows(fin, fin + 1024, w, bank, mxname, mnname)

    def emit_bcast(self, row, w, bank, waits=()):
        def f(tensor):
            ps = self.bank(bank, tensor)
            for sm, cnt in waits:
                tensor.wait_ge(sm.h, cnt)
            self.c_pe.inc(tensor.matmul(ps[:, 0:w], self.ONESR[:], row,
                                        start=True, stop=True))
        self.sec("tensor", f)
        return self.c_pe.n

    def emit_bcasts(self, specs, w, waits=()):
        nblk = (w + 127) // 128
        assert len(specs) * nblk <= 8

        def fv(vector):
            for sm, cnt in waits:
                vector.wait_ge(sm.h, cnt)
            vector.wait_ge(self.c_pe.h, self.c_pe.n)
            last = None
            for i, (tmt, bank) in enumerate(specs):
                for b in range(nblk):
                    last = vector.tensor_scalar(
                        self.REPB[:, (i * nblk + b) * 128:(i * nblk + b) * 128 + 128],
                        self.ONES128[:], tmt[:, b:b + 1], None, ALU.mult)
            self.c_ve.inc(last)
        self.sec("vector", fv)
        vem = self.c_ve.n

        def ft(tensor):
            mm = None
            for i, (tmt, bank) in enumerate(specs):
                ps = self.bank(bank, tensor)
                if i == 0:
                    tensor.wait_ge(self.c_ve.h, vem)
                for b in range(nblk):
                    mm = tensor.matmul(
                        ps[:, b * 128:b * 128 + 128],
                        self.REPB[:, (i * nblk + b) * 128:(i * nblk + b) * 128 + 128],
                        self.EYE[:], is_transpose=True)
            self.c_pe.inc(mm)
        self.sec("tensor", ft)
        return self.c_pe.n

    def emit_ln_stats(self, bank, w, pfx, minmax=False, bank2=5):
        nblk = (w + 127) // 128
        mu = self.tm(pfx + "_mu", nblk)
        rs = self.tm(pfx + "_rs", nblk)
        mrs = self.tm(pfx + "_mrs", nblk)
        t1 = self.tm(pfx + "_t1", nblk)
        t2 = self.tm(pfx + "_t2", nblk)

        def f(vector):
            vector.wait_ge(self.c_pe.h, self.c_pe.n)
            vector.tensor_copy(self.SR0[0:1, 0:w], self.PS[bank][0:1, 0:w])
            self.c_ve.inc(vector.tensor_copy(self.SR1[0:1, 0:w],
                                             self.PS[bank2][0:1, 0:w]))
        self.sec("vector", f)
        vem = self.c_ve.n

        def f(tensor):
            ps = self.PS[bank]
            tensor.wait_ge(self.c_ve.h, vem)
            mm = None
            for b in range(nblk):
                bw = min(128, w - b * 128)
                tensor.matmul(ps[0:bw, b:b + 1],
                              self.SR0[0:1, b * 128:b * 128 + bw],
                              self.EYE[0:1, 0:1], is_transpose=True)
                mm = tensor.matmul(ps[0:bw, 8 + b:9 + b],
                                   self.SR1[0:1, b * 128:b * 128 + bw],
                                   self.EYE[0:1, 0:1], is_transpose=True)
            self.c_pe.inc(mm)
        self.sec("tensor", f)
        pem = self.c_pe.n

        t3 = self.tm(pfx + "_t3", nblk)
        t4 = self.tm(pfx + "_t4", nblk)

        def f(vector):
            vector.wait_ge(self.c_pe.h, pem)
            vector.tensor_scalar(mu, self.PS[bank][:, 0:nblk], 1.0 / E, None, ALU.mult)
            vector.tensor_scalar(t3, self.PS[bank][:, 8:8 + nblk], 1.0 / E, None, ALU.mult)
            vector.drain()
            vector.tensor_tensor(t4, mu, mu, ALU.mult)
            vector.drain()
            vector.tensor_tensor(t1, t3, t4, ALU.subtract)
            vector.drain()
            vector.tensor_scalar(t2, t1, EPS_LN, None, ALU.add)
            vector.drain()
            self.c_ve.inc(vector.reciprocal(rs, t2))
        self.sec("vector", f)
        self.set_guard(bank, self.c_ve)
        self.set_guard(bank2, self.c_ve)

        def f(scalar):
            scalar.wait_ge(self.c_ve.h, self.c_ve.n)
            self.c_ac.inc(scalar.activation(rs, rs, AF.Sqrt))
        self.sec("scalar", f)

        def f(vector):
            vector.wait_ge(self.c_ac.h, self.c_ac.n)
            last = vector.tensor_tensor(mrs, mu, rs, ALU.mult)
            if minmax:
                mx = self.tm(pfx + "_mx", nblk)
                mn = self.tm(pfx + "_mn", nblk)
                cl = self.tm(pfx + "_cl", nblk)
                cc = self.tm(pfx + "_c", nblk)
                mc = self.tm(pfx + "_mc", nblk)
                vector.tensor_tensor(t1, mx, mu, ALU.subtract)
                vector.tensor_tensor(t2, mu, mn, ALU.subtract)
                vector.drain()
                vector.tensor_tensor(t3, t1, t2, ALU.max)
                vector.drain()
                vector.tensor_tensor(t4, t3, rs, ALU.mult)
                vector.drain()
                vector.tensor_scalar(cl, t4, 1e-5, None, ALU.max)
                vector.drain()
                vector.reciprocal(t1, cl)
                vector.drain()
                vector.tensor_tensor(t2, rs, t1, ALU.mult)
                vector.drain()
                vector.tensor_scalar(cc, t2, 127.0, None, ALU.mult)
                vector.drain()
                last = vector.tensor_tensor(mc, mu, cc, ALU.mult)
            self.c_ve.inc(last)
        self.sec("vector", f)

    def emit_quant(self, src_sl, w, uq_off, pfx, bankc, bankmc, nch=8):
        nblk = (w + 127) // 128
        self.emit_bcasts([(self.tm(pfx + "_c", nblk), bankc),
                          (self.tm(pfx + "_mc", nblk), bankmc)], w,
                         waits=[(self.c_ve, self.c_ve.n)])
        pem = self.c_pe.n

        def f(vector):
            vector.wait_ge(self.c_pe.h, pem)
            dr = w < 128
            tb = self.TH
            last = None
            for c in range(nch):
                sc = self.AR[:, src_sl + c * w:src_sl + c * w + w]
                uqc = self.AB[:, uq_off + c * w:uq_off + c * w + w]
                vector.tensor_tensor(self.T0[:, 0:w], sc, self.PS[bankc][:, 0:w], ALU.mult)
                if dr:
                    vector.drain()
                vector.scalar_tensor_tensor(tb[:, 0:w], self.T0[:, 0:w], ROFF,
                                            self.PS[bankmc][:, 0:w], ALU.add, ALU.subtract)
                if dr:
                    vector.drain()
                last = vector.tensor_scalar(uqc, tb[:, 0:w], -ROFF, 127.0,
                                            ALU.add, ALU.min)
            self.c_ve.inc(last)
        self.sec("vector", f)
        self.set_guard(bankc, self.c_ve)
        self.set_guard(bankmc, self.c_ve)
        return self.c_ve.n

    def emit_minmax_rows(self, trx_sl, trn_sl, w, bank, mxname, mnname, waits=()):
        nblk = (w + 127) // 128
        for sl, name, op in ((trx_sl, mxname, ALU.max), (trn_sl, mnname, ALU.min)):
            def ftr(tensor, sl=sl):
                ps = self.bank(bank, tensor)
                for sm, cnt in waits:
                    tensor.wait_ge(sm.h, cnt)
                tensor.wait_ge(self.c_ve.h, self.c_ve.n)
                mm = None
                for b in range(nblk):
                    bw = min(128, w - b * 128)
                    mm = tensor.matmul(ps[0:bw, b * 128:b * 128 + 128],
                                       self.AR[:, sl + b * 128:sl + b * 128 + bw],
                                       self.EYE[:], is_transpose=True)
                self.c_pe.inc(mm)
            self.sec("tensor", ftr)
            pem = self.c_pe.n

            def frd(vector, name=name, op=op):
                vector.wait_ge(self.c_pe.h, pem)
                dst = self.tm(name, nblk)
                for b in range(nblk):
                    bw = min(128, w - b * 128)
                    vector.tensor_reduce(dst[0:bw, b:b + 1],
                                         self.PS[bank][0:bw, b * 128:b * 128 + 128],
                                         AX.X, op)
                vector.drain()
                self.c_ve.inc(self.vfence(vector))
            self.sec("vector", frd)
            self.set_guard(bank, self.c_ve)

    def emit_ffn(self, uq_off, uq_stride, hv, w, cl_name, ydst_sl, ysrc_sl, ystride):
        BC = self.SL_BC
        TRXa, TRXb = BC + 1024, BC + 1280
        TRNa, TRNb = BC + 1536, BC + 1792
        GU = self.SL_GU
        nblk = (w + 127) // 128
        cl = self.tm(cl_name + "_cl")[:, 2 * hv:2 * hv + nblk]

        rg = self.tm("ffn_rg", nblk)
        ru = self.tm("ffn_ru", nblk)

        def f(vector):
            vector.wait_ge(self.c_ac.h, self.c_ac.n)
            vector.tensor_scalar(rg, cl, self.CONST[:, 2:3], None, ALU.mult)
            vector.tensor_scalar(ru, cl, self.CONST[:, 3:4], None, ALU.mult)
            vector.drain()
            self.c_ve.inc(self.vfence(vector))
        self.sec("vector", f)
        self.emit_bcasts([(rg, 4), (ru, 5)], w, waits=[(self.c_ve, self.c_ve.n)])
        pem = self.c_pe.n

        def f(vector):
            vector.wait_ge(self.c_pe.h, pem)
            vector.tensor_copy(self.AR[:, BC:BC + w], self.PS[4][:, 0:w])
            self.c_ve.inc(vector.tensor_copy(self.AR[:, BC + 256:BC + 256 + w],
                                             self.PS[5][:, 0:w]))
        self.sec("vector", f)
        self.set_guard(4, self.c_ve)
        self.set_guard(5, self.c_ve)

        ldm = {}
        pemk = {}
        vemk = {}
        gcmk = {}
        acmk = {}
        wp0 = self.wp_pe
        ve_uq = self.c_ve.n
        for fo2 in range(16):
            wpg = self.BL_WP + (fo2 % 2) * 2048
            wpu = self.BL_WP + 4096 + (fo2 % 2) * 2048

            def f(sync, fo2=fo2, wpg=wpg, wpu=wpu):
                if fo2 < 2:
                    sync.wait_ge(self.c_pe.h, wp0)
                if fo2 >= 2:
                    sync.wait_ge(self.c_pe.h, pemk[(fo2 - 2, 1)])
                blk = self.ds["ws_wgq"][:, fo2 * 2048:(fo2 + 1) * 2048]
                self.s_sy.inc(sync.dma_start(out=self.AB[:, wpg:wpg + 2048], in_=blk), 16)
                blk = self.ds["ws_wuq"][:, fo2 * 2048:(fo2 + 1) * 2048]
                self.s_sy.inc(sync.dma_start(out=self.AB[:, wpu:wpu + 2048], in_=blk), 16)
            self.sec("sync", f)
            ldm[fo2] = self.s_sy.n

            MI = self.SL_MISC

            def mmx(vector, fo):
                guc = self.AR[:, GU + fo * w:GU + fo * w + w]
                xd = (TRXa, TRXb)[fo % 2]
                xs = (TRXa, TRXb)[(fo - 1) % 2]
                nd = (TRNa, TRNb)[fo % 2]
                ns = (TRNa, TRNb)[(fo - 1) % 2]
                if fo == 0:
                    vector.tensor_copy(self.AR[:, xd:xd + w], guc)
                    return vector.tensor_copy(self.AR[:, nd:nd + w], guc)
                vector.tensor_tensor(self.AR[:, xd:xd + w],
                                     self.AR[:, xs:xs + w], guc, ALU.max)
                return vector.tensor_tensor(self.AR[:, nd:nd + w],
                                            self.AR[:, ns:ns + w], guc, ALU.min)

            for fop in range(2):
                fo = fo2 * 2 + fop
                bg, bu = fo % 4, 4 + fo % 4
                j = fo % 2

                def f(tensor, fo2=fo2, fop=fop, fo=fo, bg=bg, bu=bu, wpg=wpg, wpu=wpu):
                    psg = self.bank(bg, tensor)
                    psu = self.bank(bu, tensor)
                    tensor.wait_ge(self.s_sy.h, ldm[fo2])
                    if fo == 0:
                        tensor.wait_ge(self.c_ve.h, ve_uq)
                    for fi in range(8):
                        rhs = self.AB[:, uq_off + fi * uq_stride + hv * w:
                                      uq_off + fi * uq_stride + hv * w + w]
                        tensor.matmul(psg[:, 0:w],
                                      self.AB[:, wpg + fi * 256 + fop * 128:
                                              wpg + fi * 256 + fop * 128 + 128],
                                      rhs, start=(fi == 0), stop=(fi == 7))
                    mm = None
                    for fi in range(8):
                        rhs = self.AB[:, uq_off + fi * uq_stride + hv * w:
                                      uq_off + fi * uq_stride + hv * w + w]
                        mm = tensor.matmul(psu[:, 0:w],
                                           self.AB[:, wpu + fi * 256 + fop * 128:
                                                   wpu + fi * 256 + fop * 128 + 128],
                                           rhs, start=(fi == 0), stop=(fi == 7))
                    self.c_pe.inc(mm)
                self.sec("tensor", f)
                pemk[(fo2, fop)] = self.c_pe.n

                def f(vector, fo=fo, bg=bg, bu=bu, j=j):
                    vector.wait_ge(self.c_pe.h, pemk[(fo // 2, fo % 2)])
                    if fo >= 2:
                        vector.wait_ge(self.c_gc.h, gcmk[fo - 2])
                    vector.tensor_tensor(self.AR[:, MI + j * 512:MI + j * 512 + w],
                                         self.PS[bg][:, 0:w],
                                         self.AR[:, BC:BC + w], ALU.mult)
                    last = vector.tensor_tensor(
                        self.AR[:, MI + j * 512 + 256:MI + j * 512 + 256 + w],
                        self.PS[bu][:, 0:w],
                        self.AR[:, BC + 256:BC + 256 + w], ALU.mult)
                    if fo >= 2:
                        last = mmx(vector, fo - 2)
                    self.c_ve.inc(last)
                self.sec("vector", f)
                vemk[fo] = self.c_ve.n
                self.set_guard(bg, self.c_ve)
                self.set_guard(bu, self.c_ve)

                if fop == 1:
                    def f(scalar, fo=fo):
                        scalar.wait_ge(self.c_ve.h, vemk[fo])
                        if fo >= 3:
                            scalar.wait_ge(self.c_gc.h, gcmk[fo - 2])
                        scalar.activation(self.AR[:, BC + 512:BC + 512 + w],
                                          self.AR[:, MI:MI + w], AF.Silu)
                        self.c_ac.inc(scalar.activation(
                            self.AR[:, BC + 768:BC + 768 + w],
                            self.AR[:, MI + 512:MI + 512 + w], AF.Silu))
                    self.sec("scalar", f)
                    acmk[fo - 1] = acmk[fo] = self.c_ac.n

                    def f(gpsimd, fo=fo, acm=self.c_ac.n):
                        gpsimd.wait_ge(self.c_ac.h, acm)
                        gpsimd.tensor_tensor(
                            self.AR[:, GU + (fo - 1) * w:GU + (fo - 1) * w + w],
                            self.AR[:, BC + 512:BC + 512 + w],
                            self.AR[:, MI + 256:MI + 256 + w], ALU.mult)
                        self.c_gc.inc(gpsimd.tensor_tensor(
                            self.AR[:, GU + fo * w:GU + fo * w + w],
                            self.AR[:, BC + 768:BC + 768 + w],
                            self.AR[:, MI + 512 + 256:MI + 512 + 256 + w], ALU.mult))
                    self.sec("gpsimd", f)
                    gcmk[fo - 1] = gcmk[fo] = self.c_gc.n

        self.wp_pe = pemk[(15, 1)]

        def f(vector):
            vector.wait_ge(self.c_gc.h, self.c_gc.n)
            mmx(vector, 30)
            mmx(vector, 31)
            self.c_ve.inc(self.vfence(vector))
        self.sec("vector", f)
        self.emit_minmax_rows((TRXa, TRXb)[31 % 2], (TRNa, TRNb)[31 % 2], w, 6,
                              "ffn_gmx", "ffn_gmn")
        gmx = self.tm("ffn_gmx", nblk)
        gmn = self.tm("ffn_gmn", nblk)
        t1 = self.tm("ffn_t1", nblk)
        cl2 = self.tm("ffn_cl2", nblk)
        s2 = self.tm("ffn_s2", nblk)
        rd = self.tm("ffn_rd", nblk)

        t2f = self.tm("ffn_t2", nblk)

        def f(vector):
            vector.wait_ge(self.c_ve.h, self.c_ve.n)
            vector.tensor_scalar(t1, gmn, -1.0, None, ALU.mult)
            vector.drain()
            vector.tensor_tensor(t2f, gmx, t1, ALU.max)
            vector.drain()
            vector.tensor_scalar(cl2, t2f, 1e-5, None, ALU.max)
            vector.drain()
            vector.reciprocal(t1, cl2)
            vector.drain()
            vector.tensor_scalar(s2, t1, 127.0, None, ALU.mult)
            vector.tensor_scalar(rd, cl2, self.CONST[:, 4:5], None, ALU.mult)
            vector.drain()
            self.c_ve.inc(self.vfence(vector))
        self.sec("vector", f)
        self.emit_bcasts([(s2, 4), (rd, 5)], w, waits=[(self.c_ve, self.c_ve.n)])
        pem = self.c_pe.n

        def f(vector):
            vector.wait_ge(self.c_pe.h, pem)
            vector.tensor_copy(self.AR[:, BC + 512:BC + 512 + w], self.PS[4][:, 0:w])
            self.c_ve.inc(vector.tensor_copy(self.AR[:, BC + 768:BC + 768 + w],
                                             self.PS[5][:, 0:w]))
        self.sec("vector", f)
        self.set_guard(4, self.c_ve)
        self.set_guard(5, self.c_ve)

        dldm = {}
        dvem = {}
        dpem = {}
        wp0d = self.wp_pe
        for fi in range(32):
            wpd = self.BL_WP + (fi % 2) * 2048

            def f(sync, fi=fi, wpd=wpd):
                if fi < 2:
                    sync.wait_ge(self.c_pe.h, wp0d)
                if fi >= 2:
                    sync.wait_ge(self.c_pe.h, dpem[fi - 2])
                blk = self.ds["ws_wdq"][:, fi * 1024:(fi + 1) * 1024]
                self.s_sy.inc(sync.dma_start(out=self.AB[:, wpd:wpd + 1024], in_=blk), 16)
            self.sec("sync", f)
            dldm[fi] = self.s_sy.n

            def f(vector, fi=fi):
                if fi >= 4:
                    vector.wait_ge(self.c_pe.h, dpem[fi - 4])
                dr = w < 128
                guc = self.AR[:, GU + fi * w:GU + fi * w + w]
                vector.tensor_tensor(self.T0[:, 0:w], guc,
                                     self.AR[:, BC + 512:BC + 512 + w], ALU.mult)
                if dr:
                    vector.drain()
                th4 = self.TH if (fi // 2) % 2 == 0 else self.TH2
                tb = th4[:, (fi % 2) * 256:(fi % 2) * 256 + w]
                vector.tensor_scalar(tb, self.T0[:, 0:w], ROFF, None, ALU.add)
                if dr:
                    vector.drain()
                b3 = self.AB[:, self.BL_T + (fi % 4) * 256:self.BL_T + (fi % 4) * 256 + w]
                self.c_ve.inc(vector.tensor_scalar(b3, tb, -ROFF, 127.0,
                                                   ALU.add, ALU.min))
            self.sec("vector", f)
            dvem[fi] = self.c_ve.n

            def f(tensor, fi=fi, wpd=wpd):
                if fi == 0:
                    for b in range(8):
                        self.bank(b, tensor)
                tensor.wait_ge(self.s_sy.h, dldm[fi])
                tensor.wait_ge(self.c_ve.h, dvem[fi])
                b3 = self.AB[:, self.BL_T + (fi % 4) * 256:self.BL_T + (fi % 4) * 256 + w]
                mm = None
                for fo in range(8):
                    ps = self.PS[fo]
                    mm = tensor.matmul(ps[:, 0:w],
                                       self.AB[:, wpd + fo * 128:wpd + (fo + 1) * 128],
                                       b3, start=(fi == 0), stop=(fi == 31))
                self.c_pe.inc(mm)
            self.sec("tensor", f)
            dpem[fi] = self.c_pe.n

        self.wp_pe = dpem[31]

        def f(vector):
            vector.wait_ge(self.c_pe.h, self.c_pe.n)
            dr = w < 128
            last = None
            for fo in range(8):
                ps = self.PS[fo][:, 0:w]
                vector.tensor_tensor(self.T0[:, 0:w], ps,
                                     self.AR[:, BC + 768:BC + 768 + w], ALU.mult)
                if dr:
                    vector.drain()
                ysrc = self.AR[:, ysrc_sl + fo * ystride + hv * w:
                               ysrc_sl + fo * ystride + hv * w + w]
                ydst = self.AR[:, ydst_sl + fo * ystride + hv * w:
                               ydst_sl + fo * ystride + hv * w + w]
                last = vector.tensor_tensor(ydst, self.T0[:, 0:w], ysrc, ALU.add)
            self.c_ve.inc(last)
        self.sec("vector", f)
        for b in range(8):
            self.set_guard(b, self.c_ve)

    def emit_proj(self, hname, lname, rhsh_off, rhsl_off, w, out_cb, waits=()):
        ldm = {}
        pemk = {}
        wp0 = self.wp_pe
        for fo2 in range(4):
            wph = self.BL_WP + (fo2 % 2) * 2048
            wpl = self.BL_WP + 4096 + (fo2 % 2) * 2048

            def f(sync, fo2=fo2, wph=wph, wpl=wpl):
                if fo2 < 2:
                    sync.wait_ge(self.c_pe.h, wp0)
                else:
                    sync.wait_ge(self.c_pe.h, pemk[(fo2 - 2, 1)])
                blk = self.ds[hname][:, fo2 * 2048:(fo2 + 1) * 2048]
                self.s_sy.inc(sync.dma_start(out=self.AB[:, wph:wph + 2048], in_=blk), 16)
                blk = self.ds[lname][:, fo2 * 2048:(fo2 + 1) * 2048]
                self.s_sy.inc(sync.dma_start(out=self.AB[:, wpl:wpl + 2048], in_=blk), 16)
            self.sec("sync", f)
            ldm[fo2] = self.s_sy.n

            for fop in range(2):
                fo = fo2 * 2 + fop
                bk = fo % 2

                def f(tensor, fo2=fo2, fop=fop, bk=bk, wph=wph, wpl=wpl):
                    ps = self.bank(bk, tensor)
                    tensor.wait_ge(self.s_sy.h, ldm[fo2])
                    if fo2 == 0 and fop == 0:
                        for sm, cnt in waits:
                            tensor.wait_ge(sm.h, cnt)
                    mm = None
                    for fi in range(8):
                        xh = self.AB[:, rhsh_off + fi * w:rhsh_off + fi * w + w]
                        xl = self.AB[:, rhsl_off + fi * w:rhsl_off + fi * w + w]
                        wh = self.AB[:, wph + fi * 256 + fop * 128:
                                     wph + fi * 256 + fop * 128 + 128]
                        wl = self.AB[:, wpl + fi * 256 + fop * 128:
                                     wpl + fi * 256 + fop * 128 + 128]
                        tensor.matmul(ps[:, 0:w], wh, xh, start=(fi == 0), stop=False)
                        tensor.matmul(ps[:, 0:w], wl, xh, start=False, stop=False)
                        mm = tensor.matmul(ps[:, 0:w], wh, xl, start=False,
                                           stop=(fi == 7))
                    self.c_pe.inc(mm)
                self.sec("tensor", f)
                pemk[(fo2, fop)] = self.c_pe.n
                out_cb(fo, bk, self.c_pe.n)
        self.wp_pe = pemk[(3, 1)]

    def emit_ln_apply(self, src_sl, dst_sl, w, rw, gcol=None, bcol=None, nch=8,
                      extra_cb=None):
        nblk = (w + 127) // 128
        self.emit_bcasts([(self.tm(rw + "_rs", nblk), 2),
                          (self.tm(rw + "_mrs", nblk), 3)], w,
                         waits=[(self.c_ve, self.c_ve.n)])
        pem = self.c_pe.n

        def f(vector):
            vector.wait_ge(self.c_pe.h, pem)
            dr = w < 128
            last = None
            for c in range(nch):
                s = self.AR[:, src_sl + c * w:src_sl + c * w + w]
                d = self.AR[:, dst_sl + c * w:dst_sl + c * w + w]
                vector.tensor_tensor(self.T0[:, 0:w], s, self.PS[2][:, 0:w], ALU.mult)
                if dr:
                    vector.drain()
                if gcol is None:
                    last = vector.tensor_tensor(d, self.T0[:, 0:w], self.PS[3][:, 0:w],
                                                ALU.subtract)
                else:
                    vector.tensor_tensor(self.T1[:, 0:w], self.T0[:, 0:w],
                                         self.PS[3][:, 0:w], ALU.subtract)
                    if dr:
                        vector.drain()
                    last = vector.tensor_scalar(d, self.T1[:, 0:w],
                                                gcol[:, c:c + 1], bcol[:, c:c + 1],
                                                ALU.mult, ALU.add)
                if extra_cb is not None:
                    extra_cb(vector, c, d)
            self.c_ve.inc(last)
        self.sec("vector", f)
        self.set_guard(2, self.c_ve)
        self.set_guard(3, self.c_ve)
        return self.c_ve.n

    def emit_split(self, src_sl, hi_off, lo_off, w, nch=8):
        vem = []

        def f(vector):
            for c in range(nch):
                s = self.AR[:, src_sl + c * w:src_sl + c * w + w]
                hb = self.AB[:, hi_off + c * w:hi_off + c * w + w]
                self.c_ve.inc(vector.tensor_copy(hb, s))
                vem.append(self.c_ve.n)
        self.sec("vector", f)

        def f(gpsimd):
            last = None
            for c in range(nch):
                gpsimd.wait_ge(self.c_ve.h, vem[c])
                s = self.AR[:, src_sl + c * w:src_sl + c * w + w]
                hb = self.AB[:, hi_off + c * w:hi_off + c * w + w]
                lb = self.AB[:, lo_off + c * w:lo_off + c * w + w]
                last = gpsimd.tensor_sub(lb, s, hb)
            self.c_gc.inc(last)
        self.sec("gpsimd", f)
        return self.c_gc.n

    def emit_store(self, slot, sname, c0, w, waits=()):
        def f(gpsimd):
            for sm, cnt in waits:
                gpsimd.wait_ge(sm.h, cnt)
            src = self.AR[:, slot:slot + 8 * w].rearrange("p (c t) -> p c t", t=w)
            dst = self.fm(self.ds[sname], c0, w)
            self.s_st.inc(gpsimd.dma_start(out=dst, in_=src), 16)
        self.sec("gpsimd", f)
        return self.s_st.n

    def emit_load(self, sname_or_ap, slot, c0, w, eng="gpsimd", is_input=False,
                  waits=()):
        sem = {"gpsimd": self.s_gp, "sync": self.s_sy, "scalar": self.s_sc}[eng]

        def f(e):
            for sm, cnt in waits:
                e.wait_ge(sm.h, cnt)
            dram = self.di[sname_or_ap] if is_input else self.ds[sname_or_ap]
            src = self.fm(dram, c0, w)
            dst = self.AR[:, slot:slot + 8 * w].rearrange("p (c t) -> p c t", t=w)
            sem.inc(e.dma_start(out=dst, in_=src), 16)
        self.sec(eng, f)
        return sem.n

    def emit_sweep1(self):
        tiles = range(NT + 1) if _S1_TILES is None else _S1_TILES
        for t in tiles:
            mini = (t == NT)
            w = HALO if mini else TT
            c0 = NT * TT if mini else t * TT
            self.emit_s1_tile(t, c0, w, mini)

            def f(vector):
                vector.wait_ge(self.s_st.h, self.s_st.n)
                vector.wait_ge(self.s_gp.h, self.s_gp.n)
            self.sec("vector", f)
        self.barrier()

    def emit_s1_tile(self, t, c0, w, mini):
        SX, SY, SV = self.SL_X, self.SL_Y1, self.SL_V
        RAW, NORM, GU = self.SL_RAW, self.SL_NORM, self.SL_GU
        B = self.BIAS

        ldx = self.emit_load("x", SX, c0, w, eng="scalar", is_input=True,
                             waits=[(self.c_ve, self.c_ve.n),
                                    (self.c_pe, self.c_pe.n)])

        self.emit_sums(SX, w, GU, 6, waits=[(self.s_sc, ldx)])
        self.emit_minmax(SX, w, 7, "ln1_mx", "ln1_mn", waits=[(self.s_sc, ldx)])
        self.emit_ln_stats(6, w, "ln1", minmax=True)
        self.emit_quant(SX, w, self.BL_UQ, "ln1", 2, 3)

        if mini:
            self.emit_ffn(self.BL_UQ, w, 0, w, "ln1", SY, SX, w)
        else:
            for hv in range(2):
                self.emit_ffn(self.BL_UQ, w, hv, HT, "ln1", SY, SX, w)
        if not mini:
            st_y1 = self.emit_store(SY, "s_y1", c0, w, waits=[(self.c_ve, self.c_ve.n)])
        if _S1_FFN_ONLY:
            return

        self.emit_sums(SY, w, GU, 6, waits=[(self.c_ve, self.c_ve.n)])
        self.emit_ln_stats(6, w, "lnv", minmax=False)
        self.emit_ln_apply(SY, SV, w, "lnv")
        if not mini:
            st_v = self.emit_store(SV, "s_v", c0, w, waits=[(self.c_ve, self.c_ve.n)])
        self.emit_split(SV, self.BL_VH, self.BL_VL, w)

        def mkevac(dst_sl, bias_c):
            def cb(fo, bk, pem):
                def f(vector):
                    vector.wait_ge(self.c_pe.h, pem)
                    vector.wait_ge(self.c_gc.h, self.c_gc.n)
                    d = self.AR[:, dst_sl + fo * w:dst_sl + fo * w + w]
                    self.c_ve.inc(vector.tensor_scalar(
                        d, self.PS[bk][:, 0:w], B[:, bias_c + fo:bias_c + fo + 1],
                        None, ALU.add))
                self.sec("vector", f)
                self.set_guard(bk, self.c_ve)
            return cb

        self.emit_proj("wsh_q", "wsl_q", self.BL_VH, self.BL_VL, w, mkevac(RAW, 0),
                       waits=[(self.c_ve, self.c_ve.n), (self.c_gc, self.c_gc.n)])
        self.emit_sums(RAW, w, GU, 6, waits=[(self.c_ve, self.c_ve.n)])
        self.emit_ln_stats(6, w, "lnq", minmax=False)
        self.emit_ln_apply(RAW, NORM, w, "lnq", gcol=B[:, 40:48], bcol=B[:, 48:56])
        if not mini:
            def f(vector, t=t):
                srch = 8 * (t % 2)
                dsth = 8 * ((t + 1) % 2)
                for c in range(8):
                    s = self.AR[:, NORM + c * w:NORM + c * w + w]
                    vector.reduce_sum(self.T0[:, 0:1], s, AX.X)
                    vector.drain()
                    vector.tensor_tensor(self.QG[:, dsth + c:dsth + c + 1],
                                         self.QG[:, srch + c:srch + c + 1],
                                         self.T0[:, 0:1], ALU.add)
                vector.drain()
                self.c_ve.inc(self.vfence(vector))
            self.sec("vector", f)

        if not mini:
            self.emit_proj("wsh_k", "wsl_k", self.BL_VH, self.BL_VL, w, mkevac(RAW, 8))
            self.emit_sums(RAW, w, GU, 6, waits=[(self.c_ve, self.c_ve.n)])
            self.emit_ln_stats(6, w, "lnk", minmax=False)
            self.emit_ln_apply(RAW, GU + 4096, w, "lnk", gcol=B[:, 56:64], bcol=B[:, 64:72])
            st_k = self.emit_store(GU + 4096, "s_kk", c0, w, waits=[(self.c_ve, self.c_ve.n)])

            self.emit_proj("wsh_v", "wsl_v", self.BL_VH, self.BL_VL, w, mkevac(RAW, 16))
            self.emit_sums(RAW, w, GU, 6, waits=[(self.c_ve, self.c_ve.n)])
            self.emit_ln_stats(6, w, "lnu", minmax=False)
            self.emit_ln_apply(RAW, RAW, w, "lnu", gcol=B[:, 72:80], bcol=B[:, 80:88])
            st_vv = self.emit_store(RAW, "s_vv", c0, w, waits=[(self.c_ve, self.c_ve.n)])

        self.emit_split(NORM, self.BL_VH, self.BL_VL, w)
        stv = self.s_st.n

        def mkphi(act_sl, dst_sl, bias_c):
            def cb(fo, bk, pem):
                def f(scalar):
                    scalar.wait_ge(self.c_pe.h, pem)
                    scalar.wait_ge(self.s_st.h, stv)
                    a_ = self.AR[:, act_sl + fo * w:act_sl + fo * w + w]
                    self.c_ac.inc(scalar.activation(
                        a_, self.PS[bk][:, 0:w], AF.Tanh,
                        bias=B[:, bias_c + fo:bias_c + fo + 1], scale=1.0))
                self.sec("scalar", f)
                acm = self.c_ac.n

                def f(vector, acm=acm):
                    vector.wait_ge(self.c_ac.h, acm)
                    a_ = self.AR[:, act_sl + fo * w:act_sl + fo * w + w]
                    d = self.AR[:, dst_sl + fo * w:dst_sl + fo * w + w]
                    self.c_ve.inc(vector.tensor_scalar(d, a_, 1.0, None, ALU.add))
                self.sec("vector", f)
                self.set_guard(bk, self.c_ve)
            return cb

        self.emit_proj("wsh_pq", "wsl_pq", self.BL_VH, self.BL_VL, w,
                       mkphi(NORM, RAW, 24),
                       waits=[(self.c_ve, self.c_ve.n), (self.c_gc, self.c_gc.n)])
        self.emit_store(RAW, "s_pq", c0, w, waits=[(self.c_ve, self.c_ve.n)])

        if not mini:
            self.emit_split(GU + 4096, self.BL_VH, self.BL_VL, w)
            self.emit_proj("wsh_pk", "wsl_pk", self.BL_VH, self.BL_VL, w,
                           mkphi(GU + 4096, NORM, 32),
                           waits=[(self.c_ve, self.c_ve.n), (self.c_gc, self.c_gc.n)])
            self.emit_store(NORM, "s_fk", c0, w, waits=[(self.c_ve, self.c_ve.n)])

    def emit_qg_collective(self):
        groups = [[0, 1, 2, 3], [4, 5, 6, 7]]

        def f(gpsimd):
            self.s_st.inc(gpsimd.dma_start(out=self.ds["cb_qg_i"][:],
                                           in_=self.QG[:, 0:8]), 16)
            gpsimd.wait_ge(self.s_st.h, self.s_st.n)
            self.c_cc.inc(gpsimd.collective_compute(
                "AllReduce", ALU.add, replica_groups=groups,
                ins=[self.ds["cb_qg_i"][:]], outs=[self.ds["cb_qg_o"][:]]))
            gpsimd.wait_ge(self.c_cc.h, self.c_cc.n)
            self.s_gp.inc(gpsimd.dma_start(out=self.QG[:, 0:8],
                                           in_=self.ds["cb_qg_o"][:]), 16)
        self.sec("gpsimd", f)
        gpm = self.s_gp.n

        def f(vector):
            vector.wait_ge(self.s_gp.h, gpm)
            vector.tensor_scalar(self.QG[:, 8:16], self.QG[:, 0:8],
                                 1.0 / (LQ * 8.0), None, ALU.mult)
            vector.drain()
            self.c_ve.inc(self.vfence(vector))
        self.sec("vector", f)
        self.barrier()

    def kvcol(self, s):
        return (0 if s < 4 else 260) + 65 * (s % 4)

    def emit_scores_all(self, ldmark, first, dst_off):
        for c in range(8):
            bk = c % 2

            def f(tensor, c=c, bk=bk):
                ps = self.bank(bk, tensor)
                tensor.wait_ge(self.s_sy.h, ldmark)
                if first and c == 0:
                    tensor.wait_ge(self.c_ve.h, self.c_ve.n)
                self.c_pe.inc(tensor.matmul(
                    ps[0:2, :], self.QGE[:, 2 * c:2 * c + 2],
                    self.AR[:, self.SL_X + 512 * c:self.SL_X + 512 * c + 512],
                    start=True, stop=True))
            self.sec("tensor", f)
            pem = self.c_pe.n

            def f(vector, c=c, bk=bk, pem=pem):
                vector.wait_ge(self.c_pe.h, pem)
                self.c_ve.inc(vector.tensor_copy(
                    self.ESB[0:2, dst_off + 512 * c:dst_off + 512 * c + 512],
                    self.PS[bk][0:2, :]))
            self.sec("vector", f)
            self.set_guard(bk, self.c_ve)
        return self.c_ve.n

    def emit_sweep2(self):
        def f(vector):
            vector.memset(self.QGE[:], 0.0)
            vector.drain()
            for c in range(8):
                vector.tensor_copy(self.QGE[0:64, 2 * c:2 * c + 1],
                                   self.QG[0:64, 8 + c:9 + c])
                vector.tensor_copy(self.QGE[64:128, 2 * c + 1:2 * c + 2],
                                   self.QG[64:128, 8 + c:9 + c])
            vector.drain()
            self.c_ve.inc(self.vfence(vector))
        self.sec("vector", f)

        for t in range(NT):
            ld = self.emit_load("s_kk", self.SL_X, t * TT, TT, eng="sync")
            vem = self.emit_scores_all(ld, t == 0, 0)

            def f(vector, t=t, vem=vem):
                vector.wait_ge(self.c_ve.h, vem)
                srch = (0, 32)[t % 2]
                dsth = (0, 32)[(t + 1) % 2]
                for c in range(8):
                    vector.reduce_max(self.MX[:, 24:25],
                                      self.ESB[0:2, 512 * c:512 * c + 512], AX.X)
                    vector.drain()
                    if t == 0:
                        vector.tensor_copy(self.MX[:, dsth + c:dsth + c + 1],
                                           self.MX[:, 24:25])
                    else:
                        vector.tensor_tensor(self.MX[:, dsth + c:dsth + c + 1],
                                             self.MX[:, srch + c:srch + c + 1],
                                             self.MX[:, 24:25], ALU.max)
                vector.drain()
                self.c_ve.inc(self.vfence(vector))
            self.sec("vector", f)
            self.barrier(("sync", "vector", "tensor"))

        def f(vector):
            vector.tensor_scalar(self.MX[:, 8:16], self.MX[:, 0:8], -1.0, None, ALU.mult)
            vector.memset(self.MX[:, 16:24], 0.0)
            vector.memset(self.MX[:, 40:48], 0.0)
            vector.drain()
            self.c_ve.inc(self.vfence(vector))
        self.sec("vector", f)
        self.barrier()

        VTM = self.SL_NORM
        PKT = self.SL_NORM + 1040

        def f(vector):
            last = None
            for h in range(16):
                last = vector.memset(self.AR[:, VTM + 65 * h + 64:VTM + 65 * h + 65], 1.0)
            self.c_ve.inc(last)
        self.sec("vector", f)

        KVA = self.SL_RAW
        KVB = self.SL_RAW + 520

        def f(vector):
            self.c_ve.inc(vector.memset(self.AR[:, KVA:KVA + 520], 0.0))
        self.sec("vector", f)

        for t in range(NT):
            c0 = t * TT
            ldk = self.emit_load("s_kk", self.SL_X, c0, TT, eng="sync")
            ldf = self.emit_load("s_fk", self.SL_Y1, c0, TT, eng="scalar")
            ldv = self.emit_load("s_vv", self.SL_V, c0, TT, eng="gpsimd")
            vem_s = self.emit_scores_all(ldk, False, 0)

            def f(scalar, vem_s=vem_s):
                scalar.wait_ge(self.c_ve.h, vem_s)
                last = None
                for c in range(8):
                    sl = self.ESB[0:2, 512 * c:512 * c + 512]
                    last = scalar.activation(sl, sl, AF.Exp,
                                             bias=self.MX[:, 8 + c:9 + c], scale=1.0)
                self.c_ac.inc(last)
            self.sec("scalar", f)
            acm = self.c_ac.n

            def f(vector, acm=acm, t=t):
                vector.wait_ge(self.c_ac.h, acm)
                srch = (16, 40)[t % 2]
                dsth = (16, 40)[(t + 1) % 2]
                for c in range(8):
                    vector.reduce_sum(self.MX[:, 24:25],
                                      self.ESB[0:2, 512 * c:512 * c + 512], AX.X)
                    vector.drain()
                    vector.tensor_tensor(self.MX[:, dsth + c:dsth + c + 1],
                                         self.MX[:, srch + c:srch + c + 1],
                                         self.MX[:, 24:25], ALU.add)
                vector.drain()
                self.c_ve.inc(self.vfence(vector))
            self.sec("vector", f)
            vem_e = self.c_ve.n

            def f(tensor, vem_e=vem_e):
                ps = self.bank(1, tensor)
                tensor.wait_ge(self.c_ve.h, vem_e)
                mm = None
                for j in range(4):
                    for c in range(8):
                        mm = tensor.matmul(
                            ps[:, j * 16 + 2 * c:j * 16 + 2 * c + 2],
                            self.ESB[0:2, 512 * c + 128 * j:512 * c + 128 * j + 128],
                            self.EYE[0:2, 0:2], is_transpose=True)
                self.c_pe.inc(mm)
            self.sec("tensor", f)
            pem_t = self.c_pe.n

            def f(vector, pem_t=pem_t):
                vector.wait_ge(self.c_pe.h, pem_t)
                vector.tensor_copy(self.ETM[:], self.PS[1][:, 0:64])
                vector.drain()
                self.c_ve.inc(self.vfence(vector))
            self.sec("vector", f)
            self.set_guard(1, self.c_ve)

            for j in range(4):
                def f(tensor, j=j, ldf=ldf):
                    ps2 = self.bank(2, tensor)
                    ps3 = self.bank(3, tensor)
                    tensor.wait_ge(self.s_sc.h, ldf)
                    mm = None
                    for c in range(8):
                        ps = ps2 if c < 4 else ps3
                        mm = tensor.matmul(
                            ps[:, (c % 4) * 128:(c % 4) * 128 + 128],
                            self.AR[:, self.SL_Y1 + 512 * c + 128 * j:
                                    self.SL_Y1 + 512 * c + 128 * j + 128],
                            self.EYE[:], is_transpose=True)
                    self.c_pe.inc(mm)
                self.sec("tensor", f)
                pem_k = self.c_pe.n

                def f(vector, j=j, pem_k=pem_k):
                    vector.wait_ge(self.c_pe.h, pem_k)
                    last = None
                    for h in range(16):
                        c = h // 2
                        ps = self.PS[2 + c // 4]
                        sl = ps[:, 128 * (c % 4) + 64 * (h % 2):
                                128 * (c % 4) + 64 * (h % 2) + 64]
                        last = vector.tensor_scalar(
                            self.AR[:, PKT + 64 * h:PKT + 64 * h + 64], sl,
                            self.ETM[:, 16 * j + h:16 * j + h + 1], None, ALU.mult)
                    self.c_ve.inc(last)
                self.sec("vector", f)
                self.set_guard(2, self.c_ve)
                self.set_guard(3, self.c_ve)
                vem_pk = self.c_ve.n

                def f(tensor, j=j, ldv=ldv, vem_pk=vem_pk):
                    ps2 = self.bank(2, tensor)
                    ps3 = self.bank(3, tensor)
                    tensor.wait_ge(self.s_gp.h, ldv)
                    mm = None
                    for c in range(8):
                        ps = ps2 if c < 4 else ps3
                        mm = tensor.matmul(
                            ps[:, (c % 4) * 128:(c % 4) * 128 + 128],
                            self.AR[:, self.SL_V + 512 * c + 128 * j:
                                    self.SL_V + 512 * c + 128 * j + 128],
                            self.EYE[:], is_transpose=True)
                    self.c_pe.inc(mm)
                self.sec("tensor", f)
                pem_v = self.c_pe.n

                def f(vector, pem_v=pem_v):
                    vector.wait_ge(self.c_pe.h, pem_v)
                    last = None
                    for h in range(16):
                        c = h // 2
                        ps = self.PS[2 + c // 4]
                        sl = ps[:, 128 * (c % 4) + 64 * (h % 2):
                                128 * (c % 4) + 64 * (h % 2) + 64]
                        last = vector.tensor_copy(
                            self.AR[:, VTM + 65 * h:VTM + 65 * h + 64], sl)
                    self.c_ve.inc(last)
                self.sec("vector", f)
                self.set_guard(2, self.c_ve)
                self.set_guard(3, self.c_ve)
                vem_v = self.c_ve.n

                idx = t * 4 + j
                src_kv = (KVA, KVB)[idx % 2]
                dst_kv = (KVA, KVB)[(idx + 1) % 2]
                for rnd in range(2):
                    def f(tensor, j=j, vem_v=vem_v, rnd=rnd):
                        mm = None
                        for hh in range(8):
                            h = rnd * 8 + hh
                            pb = 64 * (h % 2)
                            ps = self.bank(hh, tensor)
                            if hh == 0 and rnd == 0:
                                tensor.wait_ge(self.c_ve.h, vem_v)
                            mm = tensor.matmul(
                                ps[pb:pb + 64, 0:65],
                                self.AR[:, PKT + 64 * h:PKT + 64 * h + 64],
                                self.AR[:, VTM + 65 * h:VTM + 65 * h + 65],
                                start=True, stop=True)
                        self.c_pe.inc(mm)
                    self.sec("tensor", f)
                    pem_kv = self.c_pe.n

                    def f(vector, rnd=rnd, src_kv=src_kv, dst_kv=dst_kv,
                          pem_kv=pem_kv):
                        vector.wait_ge(self.c_pe.h, pem_kv)
                        last = None
                        for hh in range(8):
                            h = rnd * 8 + hh
                            pb = 64 * (h % 2)
                            a_src = self.AR[pb:pb + 64,
                                            src_kv + self.kvcol(h // 2):
                                            src_kv + self.kvcol(h // 2) + 65]
                            a_dst = self.AR[pb:pb + 64,
                                            dst_kv + self.kvcol(h // 2):
                                            dst_kv + self.kvcol(h // 2) + 65]
                            last = vector.tensor_tensor(a_dst,
                                                        self.PS[hh][pb:pb + 64, 0:65],
                                                        a_src, ALU.add)
                        self.c_ve.inc(last)
                    self.sec("vector", f)
                    for hh in range(8):
                        self.set_guard(hh, self.c_ve)
            self.barrier(("sync", "vector", "scalar", "tensor", "gpsimd"))

        fin_kv = (KVA, KVB)[(NT * 4) % 2]

        def f(vector):
            vector.wait_ge(self.c_pe.h, self.c_pe.n)
            vector.tensor_copy(self.KVSB[:, 0:520], self.AR[:, fin_kv:fin_kv + 520])
            vector.tensor_copy(self.KVSB[0:2, 520:528], self.MX[:, 0:8])
            self.c_ve.inc(vector.tensor_copy(self.KVSB[0:2, 528:536], self.MX[:, 16:24]))
        self.sec("vector", f)

        groups = [[0, 1, 2, 3], [4, 5, 6, 7]]

        def f(gpsimd):
            gpsimd.wait_ge(self.c_ve.h, self.c_ve.n)
            self.s_st.inc(gpsimd.dma_start(out=self.ds["cb_ag_i"][:],
                                           in_=self.KVSB[:, 0:536]), 16)
            gpsimd.wait_ge(self.s_st.h, self.s_st.n)
            self.c_cc.inc(gpsimd.collective_compute(
                "AllGather", ALU.bypass, replica_groups=groups,
                ins=[self.ds["cb_ag_i"][:]], outs=[self.ds["cb_ag_o"][:]]))
            gpsimd.wait_ge(self.c_cc.h, self.c_cc.n)
            K4 = self.AR[:, self.SL_RAW:self.SL_RAW + 4 * 536]
            self.s_gp.inc(gpsimd.dma_start(
                out=K4.rearrange("p (a b) -> p a b", b=536),
                in_=self.ds["cb_ag_o"].rearrange("(a p) b -> p a b", p=128)), 16)
        self.sec("gpsimd", f)
        gpm = self.s_gp.n

        RAWs = self.SL_RAW
        C = self.COMB

        def f(vector):
            vector.wait_ge(self.s_gp.h, gpm)
            for i in range(4):
                vector.tensor_copy(C[:, 8 * i:8 * i + 8],
                                   self.AR[0:2, RAWs + 536 * i + 520:RAWs + 536 * i + 528])
            vector.drain()
            vector.tensor_tensor(self.MX[:, 48:56], C[:, 0:8], C[:, 8:16], ALU.max)
            vector.drain()
            vector.tensor_tensor(self.MX[:, 56:64], self.MX[:, 48:56], C[:, 16:24], ALU.max)
            vector.drain()
            vector.tensor_tensor(self.MX[:, 48:56], self.MX[:, 56:64], C[:, 24:32], ALU.max)
            vector.drain()
            vector.tensor_scalar(C[:, 40:48], self.MX[:, 48:56], 0.0, None, ALU.max)
            vector.drain()
            for i in range(4):
                vector.tensor_tensor(self.MX[:, 32:40], C[:, 8 * i:8 * i + 8],
                                     C[:, 40:48], ALU.subtract)
                vector.drain()
                vector.tensor_copy(C[:, 8 * i:8 * i + 8], self.MX[:, 32:40])
                vector.drain()
            self.c_ve.inc(self.vfence(vector))
        self.sec("vector", f)

        def f(scalar):
            scalar.wait_ge(self.c_ve.h, self.c_ve.n)
            scalar.activation(C[:, 0:32], C[:, 0:32], AF.Exp)
            self.c_ac.inc(scalar.activation(C[:, 48:56], C[:, 40:48], AF.Exp, scale=-1.0))
        self.sec("scalar", f)

        def f(vector):
            vector.wait_ge(self.c_ac.h, self.c_ac.n)
            zp = [C[:, 32:40], self.MX[:, 56:64]]
            vector.tensor_copy(zp[0], C[:, 48:56])
            vector.drain()
            for i in range(4):
                vector.tensor_tensor(self.MX[:, 48:56],
                                     self.AR[0:2, RAWs + 536 * i + 528:RAWs + 536 * i + 536],
                                     C[:, 8 * i:8 * i + 8], ALU.mult)
                vector.drain()
                vector.tensor_tensor(zp[(i + 1) % 2], zp[i % 2], self.MX[:, 48:56], ALU.add)
                vector.drain()
            vector.reciprocal(self.MX[:, 48:56], zp[0])
            vector.drain()
            vector.tensor_scalar(C[:, 56:64], self.MX[:, 48:56], float(LQ), None, ALU.mult)
            vector.drain()
            self.c_ve.inc(self.vfence(vector))
        self.sec("vector", f)

        def f(tensor):
            ps = self.bank(1, tensor)
            tensor.wait_ge(self.c_ve.h, self.c_ve.n)
            sel2 = self.SELP[0:2, 128:256]
            tensor.matmul(ps[:, 0:32], sel2, C[:, 0:32], start=True, stop=True)
            tensor.matmul(ps[:, 32:40], sel2, C[:, 48:56], start=True, stop=True)
            self.c_pe.inc(tensor.matmul(ps[:, 40:48], sel2, C[:, 56:64],
                                        start=True, stop=True))
        self.sec("tensor", f)
        pem_f = self.c_pe.n

        NB0 = self.SL_NORM
        NB1 = self.SL_NORM + 1040

        def f(vector):
            vector.wait_ge(self.c_pe.h, pem_f)
            F = self.PS[1]
            last = None
            for c in range(8):
                col = self.kvcol(c)
                a0 = self.AR[:, NB0 + 65 * c:NB0 + 65 * c + 65]
                a1 = self.AR[:, NB1 + 65 * c:NB1 + 65 * c + 65]
                k4 = [self.AR[:, RAWs + 536 * i + col:RAWs + 536 * i + col + 65]
                      for i in range(4)]
                vector.tensor_scalar(a0, k4[0], F[:, c:c + 1], None, ALU.mult)
                vector.drain()
                vector.scalar_tensor_tensor(a1, k4[1], F[:, 8 + c:9 + c], a0,
                                            ALU.mult, ALU.add)
                vector.drain()
                vector.scalar_tensor_tensor(a0, k4[2], F[:, 16 + c:17 + c], a1,
                                            ALU.mult, ALU.add)
                vector.drain()
                vector.scalar_tensor_tensor(a1, k4[3], F[:, 24 + c:25 + c], a0,
                                            ALU.mult, ALU.add)
                vector.drain()
                vector.scalar_tensor_tensor(a0[:, 64:65], self.PK0[:, c:c + 1],
                                            F[:, 32 + c:33 + c], a1[:, 64:65],
                                            ALU.mult, ALU.add)
                vector.drain()
                vector.tensor_scalar(self.KVSB[:, col:col + 64], a1[:, 0:64],
                                     F[:, 40 + c:41 + c], None, ALU.mult)
                last = vector.tensor_scalar(self.KVSB[:, col + 64:col + 65],
                                            a0[:, 64:65],
                                            F[:, 40 + c:41 + c], None, ALU.mult)
            vector.drain()
            self.c_ve.inc(self.vfence(vector))
        self.sec("vector", f)
        self.set_guard(1, self.c_ve)

        def f(vector):
            vector.memset(self.KSE[:], 0.0)
            vector.drain()
            for c in range(8):
                col = self.kvcol(c) + 64
                vector.tensor_copy(self.KSE[0:64, 2 * c:2 * c + 1],
                                   self.KVSB[0:64, col:col + 1])
                vector.tensor_copy(self.KSE[64:128, 2 * c + 1:2 * c + 2],
                                   self.KVSB[64:128, col:col + 1])
            vector.drain()
            self.c_ve.inc(self.vfence(vector))
        self.sec("vector", f)
        self.barrier()

    def emit_attn(self, pq_sl, pq_stride, pq_off, w, dst_fn, use_mask=False,
                  waits=()):
        for c in range(8):
            bn = c % 2
            bd = 2 + c % 2

            def f(tensor, c=c, bn=bn, bd=bd):
                ps5 = self.bank(5, tensor)
                if c == 0:
                    for sm, cnt in waits:
                        tensor.wait_ge(sm.h, cnt)
                self.c_pe.inc(tensor.matmul(
                    ps5[0:2, 0:w], self.KSE[:, 2 * c:2 * c + 2],
                    self.AR[:, pq_sl + c * pq_stride + pq_off:
                            pq_sl + c * pq_stride + pq_off + w],
                    start=True, stop=True))
            self.sec("tensor", f)
            pem_d = self.c_pe.n

            def f(vector, c=c, pem_d=pem_d):
                vector.wait_ge(self.c_pe.h, pem_d)
                vector.tensor_scalar(self.DEN[:, 0:w], self.PS[5][0:2, 0:w], EPS_ATT,
                                     None, ALU.add)
                if w < 128:
                    vector.drain()
                vector.reciprocal(self.DEN[:, 512:512 + w], self.DEN[:, 0:w])
                vector.drain()
                self.c_ve.inc(self.vfence(vector))
            self.sec("vector", f)
            self.set_guard(5, self.c_ve)
            vem_d = self.c_ve.n

            def f(tensor, c=c, bn=bn, bd=bd, vem_d=vem_d):
                psn = self.bank(bn, tensor)
                psd = self.bank(bd, tensor)
                tensor.wait_ge(self.c_ve.h, vem_d)
                for h in (2 * c, 2 * c + 1):
                    pb = 64 * (h % 2)
                    col = self.kvcol(c)
                    lhs = self.KVSB[pb:pb + 64, col:col + 64]
                    rhs = self.AR[pb:pb + 64, pq_sl + c * pq_stride + pq_off:
                                  pq_sl + c * pq_stride + pq_off + w]
                    tensor.matmul(psn[pb:pb + 64, 0:w], lhs, rhs, start=True, stop=True)
                self.c_pe.inc(tensor.matmul(psd[:, 0:w], self.SELP[0:2, 128:256],
                                            self.DEN[:, 512:512 + w],
                                            start=True, stop=True))
            self.sec("tensor", f)
            pem_n = self.c_pe.n

            def f(vector, c=c, bn=bn, bd=bd, pem_n=pem_n):
                vector.wait_ge(self.c_pe.h, pem_n)
                d = dst_fn(c)
                vector.tensor_copy(self.T1[:, 0:w], self.PS[bd][:, 0:w])
                if w < 128:
                    vector.drain()
                if use_mask:
                    vector.tensor_tensor(self.T0[:, 0:w], self.PS[bn][:, 0:w],
                                         self.T1[:, 0:w], ALU.mult)
                    vector.drain()
                    last = vector.tensor_tensor(d, self.T0[:, 0:w],
                                                self.MASKB[:, 0:w], ALU.mult)
                else:
                    last = vector.tensor_tensor(d, self.PS[bn][:, 0:w],
                                                self.T1[:, 0:w], ALU.mult)
                self.c_ve.inc(last)
            self.sec("vector", f)
            self.set_guard(bn, self.c_ve)
            self.set_guard(bd, self.c_ve)

    def emit_w1(self, w):
        ldm = {}
        pemk = {}
        self.acm_w1 = {}
        wp0 = self.wp_pe
        ve_rdy = self.c_ve.n
        gc_rdy = self.c_gc.n
        for fo2 in range(16):
            wph = self.BL_WP + (fo2 % 2) * 2048
            wpl = self.BL_WP + 4096 + (fo2 % 2) * 2048

            def f(sync, fo2=fo2, wph=wph, wpl=wpl):
                if fo2 < 2:
                    sync.wait_ge(self.c_pe.h, wp0)
                else:
                    sync.wait_ge(self.c_pe.h, pemk[(fo2 - 2, 1)])
                blk = self.ds["ws_w1h"][:, fo2 * 2048:(fo2 + 1) * 2048]
                self.s_sy.inc(sync.dma_start(out=self.AB[:, wph:wph + 2048], in_=blk), 16)
                blk = self.ds["ws_w1l"][:, fo2 * 2048:(fo2 + 1) * 2048]
                self.s_sy.inc(sync.dma_start(out=self.AB[:, wpl:wpl + 2048], in_=blk), 16)
            self.sec("sync", f)
            ldm[fo2] = self.s_sy.n

            for fop in range(2):
                fo = fo2 * 2 + fop
                bk = fo % 4

                def f(tensor, fo2=fo2, fop=fop, bk=bk, wph=wph, wpl=wpl):
                    ps = self.bank(bk, tensor)
                    tensor.wait_ge(self.s_sy.h, ldm[fo2])
                    if fo2 == 0 and fop == 0:
                        tensor.wait_ge(self.c_ve.h, ve_rdy)
                        tensor.wait_ge(self.c_gc.h, gc_rdy)
                    mm = None
                    for fi in range(8):
                        xh = self.AB[:, self.BL_VH + fi * w:self.BL_VH + fi * w + w]
                        xl = self.AB[:, self.BL_VL + fi * w:self.BL_VL + fi * w + w]
                        wh = self.AB[:, wph + fi * 256 + fop * 128:
                                     wph + fi * 256 + fop * 128 + 128]
                        wl = self.AB[:, wpl + fi * 256 + fop * 128:
                                     wpl + fi * 256 + fop * 128 + 128]
                        tensor.matmul(ps[:, 0:w], wh, xh, start=(fi == 0), stop=False)
                        tensor.matmul(ps[:, 0:w], wl, xh, start=False, stop=False)
                        mm = tensor.matmul(ps[:, 0:w], wh, xl, start=False, stop=(fi == 7))
                    self.c_pe.inc(mm)
                self.sec("tensor", f)
                pemk[(fo2, fop)] = self.c_pe.n

            if fo2 % 2 == 1:
                def f(scalar, fo2=fo2):
                    scalar.wait_ge(self.c_pe.h, pemk[(fo2, 1)])
                    last = None
                    for fo in range(fo2 * 2 - 2, fo2 * 2 + 2):
                        d = self.AR[:, self.SL_GU + fo * w:self.SL_GU + fo * w + w]
                        last = scalar.activation(d, self.PS[fo % 4][:, 0:w], AF.Gelu,
                                                 bias=self.B1P[:, fo:fo + 1], scale=1.0)
                    self.c_ac.inc(last)
                self.sec("scalar", f)
                for b in range(4):
                    self.set_guard(b, self.c_ac)
                self.acm_w1[fo2 * 2 - 2] = self.acm_w1[fo2 * 2 - 1] = \
                    self.acm_w1[fo2 * 2] = self.acm_w1[fo2 * 2 + 1] = self.c_ac.n
        self.wp_pe = pemk[(15, 1)]

    def emit_w2(self, w):
        dldm = {}
        dvem = {}
        dpem = {}
        wp0 = self.wp_pe
        for fi in range(32):
            wph = self.BL_WP + (fi % 2) * 1024
            wpl = self.BL_WP + 4096 + (fi % 2) * 1024

            def f(sync, fi=fi, wph=wph, wpl=wpl):
                if fi < 2:
                    sync.wait_ge(self.c_pe.h, wp0)
                else:
                    sync.wait_ge(self.c_pe.h, dpem[fi - 2])
                blk = self.ds["ws_w2h"][:, fi * 1024:(fi + 1) * 1024]
                self.s_sy.inc(sync.dma_start(out=self.AB[:, wph:wph + 1024], in_=blk), 16)
                blk = self.ds["ws_w2l"][:, fi * 1024:(fi + 1) * 1024]
                self.s_sy.inc(sync.dma_start(out=self.AB[:, wpl:wpl + 1024], in_=blk), 16)
            self.sec("sync", f)
            dldm[fi] = self.s_sy.n

            def f(vector, fi=fi):
                if fi >= 2:
                    vector.wait_ge(self.c_pe.h, dpem[fi - 2])
                vector.wait_ge(self.c_ac.h, self.acm_w1[fi])
                hc = self.AR[:, self.SL_GU + fi * w:self.SL_GU + fi * w + w]
                hh = self.AB[:, self.BL_T + (fi % 2) * 512:self.BL_T + (fi % 2) * 512 + w]
                hl = self.AB[:, self.BL_T + 1024 + (fi % 2) * 512:
                             self.BL_T + 1024 + (fi % 2) * 512 + w]
                vector.tensor_copy(hh, hc)
                self.c_ve.inc(vector.tensor_sub(hl, hc, hh))
            self.sec("vector", f)
            dvem[fi] = self.c_ve.n

            def f(tensor, fi=fi, wph=wph, wpl=wpl):
                if fi == 0:
                    for b in range(8):
                        self.bank(b, tensor)
                tensor.wait_ge(self.s_sy.h, dldm[fi])
                tensor.wait_ge(self.c_ve.h, dvem[fi])
                hh = self.AB[:, self.BL_T + (fi % 2) * 512:self.BL_T + (fi % 2) * 512 + w]
                hl = self.AB[:, self.BL_T + 1024 + (fi % 2) * 512:
                             self.BL_T + 1024 + (fi % 2) * 512 + w]
                mm = None
                for fo in range(8):
                    ps = self.PS[fo][:, 0:w]
                    wh = self.AB[:, wph + fo * 128:wph + (fo + 1) * 128]
                    wl = self.AB[:, wpl + fo * 128:wpl + (fo + 1) * 128]
                    tensor.matmul(ps, wh, hh, start=(fi == 0), stop=False)
                    tensor.matmul(ps, wl, hh, start=False, stop=False)
                    mm = tensor.matmul(ps, wh, hl, start=False, stop=(fi == 31))
                self.c_pe.inc(mm)
            self.sec("tensor", f)
            dpem[fi] = self.c_pe.n
        self.wp_pe = dpem[31]

    def emit_finish(self, tau):
        CIN = self.SL_Y1
        w = HT
        B = self.BIAS
        for hv in range(2):
            def f(sync, hv=hv):
                sync.wait_ge(self.c_ve.h, self.c_ve.n)
                sync.wait_ge(self.c_ac.h, self.c_ac.n)
                self.s_sy.inc(sync.dma_start(out=self.AR[:, self.SL_GU:self.SL_GU + 5120],
                                             in_=self.di["convbd"][:]), 16)
            self.sec("sync", f)
            ldcv = self.s_sy.n

            pemc = {}
            for p in range(8):
                bk = 6 + p % 2

                def f(tensor, p=p, bk=bk, hv=hv):
                    ps = self.bank(bk, tensor)
                    tensor.wait_ge(self.s_sy.h, ldcv)
                    if p < 2:
                        tensor.wait_ge(self.c_ve.h, self.c_ve.n)
                    mm = None
                    for k in range(5):
                        lhs = self.AR[:, self.SL_GU + (p * 5 + k) * 128:
                                      self.SL_GU + (p * 5 + k) * 128 + 128]
                        rhs = self.AR[:, CIN + 516 * p + 256 * hv + k:
                                      CIN + 516 * p + 256 * hv + k + w]
                        mm = tensor.matmul(ps[:, 0:w], lhs, rhs,
                                           start=(k == 0), stop=(k == 4))
                    self.c_pe.inc(mm)
                self.sec("tensor", f)
                pemc[p] = self.c_pe.n

                def f(vector, p=p, bk=bk, hv=hv):
                    vector.wait_ge(self.c_pe.h, pemc[p])
                    vector.tensor_scalar(self.T0[:, 0:w], self.PS[bk][:, 0:w],
                                         B[:, 120:121], self.GCB[:, p:p + 1],
                                         ALU.mult, ALU.add)
                    d = self.AR[:, self.SL_V + p * w:self.SL_V + p * w + w]
                    self.c_ve.inc(vector.tensor_tensor(
                        d, self.T0[:, 0:w],
                        self.AR[:, CIN + 516 * p + 2 + 256 * hv:
                                CIN + 516 * p + 2 + 256 * hv + w], ALU.add))
                self.sec("vector", f)
                self.set_guard(bk, self.c_ve)

            self.emit_sums(self.SL_V, w, self.SL_RAW + 2048, 6,
                           waits=[(self.c_ve, self.c_ve.n)])
            self.emit_ln_stats(6, w, "lna", minmax=False)
            self.emit_ln_apply(self.SL_V, self.SL_V, w, "lna",
                               gcol=B[:, 88:96], bcol=B[:, 96:104])
            self.emit_split(self.SL_V, self.BL_VH, self.BL_VL, w)

            ldy1 = self.emit_load("s_y1", self.SL_V + 2048, tau * TT + hv * w, w,
                                  eng="gpsimd", waits=[(self.c_ve, self.c_ve.n)])
            ldv2 = self.emit_load("s_v", self.SL_RAW, tau * TT + hv * w, w,
                                  eng="gpsimd")

            self.emit_w1(w)
            self.emit_w2(w)

            def f(vector, hv=hv, ldy1=ldy1, ldv2=ldv2):
                vector.wait_ge(self.c_pe.h, self.c_pe.n)
                vector.wait_ge(self.s_gp.h, max(ldy1, ldv2))
                last = None
                for fo in range(8):
                    ps = self.PS[fo][:, 0:w]
                    vector.tensor_scalar(self.T0[:, 0:w], ps, B[:, 104 + fo:105 + fo],
                                         None, ALU.add)
                    vector.tensor_tensor(self.T1[:, 0:w], self.T0[:, 0:w],
                                         self.AR[:, self.SL_V + fo * w:
                                                 self.SL_V + fo * w + w], ALU.add)
                    vector.tensor_tensor(self.T0[:, 0:w], self.T1[:, 0:w],
                                         self.AR[:, self.SL_RAW + fo * w:
                                                 self.SL_RAW + fo * w + w], ALU.add)
                    last = vector.tensor_tensor(
                        self.AR[:, self.SL_V + fo * w:self.SL_V + fo * w + w],
                        self.T0[:, 0:w],
                        self.AR[:, self.SL_V + 2048 + fo * w:
                                self.SL_V + 2048 + fo * w + w], ALU.add)
                self.c_ve.inc(last)
            self.sec("vector", f)
            for b in range(8):
                self.set_guard(b, self.c_ve)

            self.emit_sums(self.SL_V, w, self.SL_RAW + 2048, 6,
                           waits=[(self.c_ve, self.c_ve.n)])
            self.emit_minmax(self.SL_V, w, 7, "ln2_mx", "ln2_mn",
                             waits=[(self.c_ve, self.c_ve.n)])
            self.emit_ln_stats(6, w, "ln2", minmax=True)
            self.emit_quant(self.SL_V, w, self.BL_UQ, "ln2", 2, 3)
            self.emit_ffn(self.BL_UQ, w, 0, w, "ln2", self.SL_RAW, self.SL_V, w)

            def f(gpsimd, hv=hv):
                gpsimd.wait_ge(self.c_ve.h, self.c_ve.n)
                src = self.AR[:, self.SL_RAW:self.SL_RAW + 8 * w].rearrange(
                    "p (c t) -> p c t", t=w)
                dst = self.fm(self.out, tau * TT + hv * w, w)
                self.s_st.inc(gpsimd.dma_start(out=dst, in_=src), 16)
            self.sec("gpsimd", f)

    def emit_sweep3(self):
        CIN = self.SL_Y1

        ldm = self.emit_load("s_pq", self.SL_X, TOK, HALO, eng="gpsimd")
        self.emit_attn(self.SL_X, HALO, 0, HALO,
                       lambda s: self.MINIA[:, 4 * s:4 * s + 4], use_mask=True,
                       waits=[(self.s_gp, ldm)])

        def f(vector):
            vector.wait_ge(self.c_ve.h, self.c_ve.n)
            last = None
            for p in range(8):
                last = vector.tensor_copy(self.AR[:, CIN + 516 * p:CIN + 516 * p + 2],
                                          self.MINIA[:, 4 * p:4 * p + 2])
            self.c_ve.inc(last)
        self.sec("vector", f)
        self.barrier()

        for t in range(NT):
            ldp = self.emit_load("s_pq", self.SL_X, t * TT, TT, eng="gpsimd",
                                 waits=[(self.c_pe, self.c_pe.n)])
            if t >= 1:
                self.emit_attn(self.SL_X, TT, 0, 2,
                               lambda s: self.AR[:, CIN + 516 * s + 514:CIN + 516 * s + 516],
                               waits=[(self.s_gp, ldp)])
                self.emit_finish(t - 1)

                def f(vector):
                    vector.wait_ge(self.c_ve.h, self.c_ve.n)
                    vector.wait_ge(self.c_ac.h, self.c_ac.n)
                    last = None
                    for p in range(8):
                        last = vector.tensor_copy(
                            self.AR[:, CIN + 516 * p:CIN + 516 * p + 2],
                            self.AR[:, CIN + 516 * p + 512:CIN + 516 * p + 514])
                    self.c_ve.inc(last)
                self.sec("vector", f)
            self.emit_attn(self.SL_X, TT, 0, TT,
                           lambda s: self.AR[:, CIN + 516 * s + 2:CIN + 516 * s + 514],
                           waits=[(self.s_gp, ldp), (self.c_ve, self.c_ve.n)])

            def f(vector):
                vector.wait_ge(self.s_st.h, self.s_st.n)
                vector.wait_ge(self.s_gp.h, self.s_gp.n)
            self.sec("vector", f)
        self.barrier()

        def f(vector):
            last = None
            for p in range(8):
                last = vector.tensor_copy(
                    self.AR[:, CIN + 516 * p + 514:CIN + 516 * p + 516],
                    self.MINIA[:, 4 * p + 2:4 * p + 4])
            self.c_ve.inc(last)
        self.sec("vector", f)
        self.emit_finish(NT - 1)
        self.barrier()

    def emit_taps_and_close(self):
        if self.taps:
            def f(gpsimd):
                gpsimd.wait_ge(self.c_ve.h, self.c_ve.n)
                gpsimd.wait_ge(self.c_pe.h, self.c_pe.n)
                gpsimd.wait_ge(self.c_ac.h, self.c_ac.n)
                gpsimd.wait_ge(self.c_gc.h, self.c_gc.n)
                gpsimd.wait_ge(self.s_st.h, self.s_st.n)
                gpsimd.wait_ge(self.s_sy.h, self.s_sy.n)
                for nm in ("s_y1", "s_v", "s_pq", "s_kk", "s_fk", "s_vv",
                           "ws_wgq", "ws_wdq", "wsh_q", "wsl_q", "ws_w2h"):
                    self.s_st.inc(gpsimd.dma_start(out=self.tap_t[nm][:],
                                                   in_=self.ds[nm][:]), 16)
                self.s_st.inc(gpsimd.dma_start(out=self.tap_t["qg"][:],
                                               in_=self.QG[:]), 16)
                self.s_st.inc(gpsimd.dma_start(out=self.tap_t["wsr"][:],
                                               in_=self.WS[:]), 16)
                self.s_st.inc(gpsimd.dma_start(out=self.tap_t["const"][:],
                                               in_=self.CONST[:]), 16)
                self.s_st.inc(gpsimd.dma_start(out=self.tap_t["uq"][:],
                                               in_=self.AB[:, self.BL_UQ:self.BL_UQ + 4096]), 16)
                self.s_st.inc(gpsimd.dma_start(out=self.tap_t["gu"][:],
                                               in_=self.AR[:, self.SL_GU:self.SL_GU + 8192]), 16)
                self.s_st.inc(gpsimd.dma_start(out=self.tap_t["bc"][:],
                                               in_=self.AR[:, self.SL_BC:self.SL_BC + 3072]), 16)
                self.s_st.inc(gpsimd.dma_start(out=self.tap_t["st"][:],
                                               in_=self.ST[:]), 16)
                self.s_st.inc(gpsimd.dma_start(out=self.tap_t["kv"][:],
                                               in_=self.KVSB[:]), 16)
            self.sec("gpsimd", f)

        def f(gpsimd):
            gpsimd.wait_ge(self.s_st.h, self.s_st.n)
        self.sec("gpsimd", f)


def _pack_col(v):
    return np.ascontiguousarray(np.asarray(v, dtype=np.float32).reshape(-1, 128).T)


def build_nc(taps=False):
    return Builder(taps=taps, stages=_STAGES).build()


def kernel(L0_attn_x, L0_attn_Wq, L0_attn_bq, L0_attn_lnq_g, L0_attn_lnq_b,
           L0_attn_Wk, L0_attn_bk, L0_attn_lnk_g, L0_attn_lnk_b,
           L0_attn_Wv, L0_attn_bv, L0_attn_lnv_g, L0_attn_lnv_b,
           L0_attn_Wphi_q, L0_attn_bphi_q, L0_attn_Wphi_k, L0_attn_bphi_k,
           L0_attn_lna_g, L0_attn_lna_b, L0_attn_W1, L0_attn_b1,
           L0_attn_W2, L0_attn_b2, L0_attn_gamma, L0_attn_conv_w, L0_attn_conv_b,
           L0_ffn_w_gate, L0_ffn_w_up, L0_ffn_w_down):
    x = np.asarray(L0_attn_x, dtype=np.float32)
    f32 = lambda a: np.ascontiguousarray(np.asarray(a, dtype=np.float32))

    bias = np.zeros((128, 160), dtype=np.float32)
    cols = [L0_attn_bq, L0_attn_bk, L0_attn_bv, L0_attn_bphi_q, L0_attn_bphi_k,
            L0_attn_lnq_g, L0_attn_lnq_b, L0_attn_lnk_g, L0_attn_lnk_b,
            L0_attn_lnv_g, L0_attn_lnv_b, L0_attn_lna_g, L0_attn_lna_b,
            L0_attn_b2, L0_attn_conv_b]
    for i, v in enumerate(cols):
        bias[:, 8 * i:8 * (i + 1)] = _pack_col(v)
    bias[:, 120] = np.float32(L0_attn_gamma)
    b1p = _pack_col(L0_attn_b1)

    convw = np.asarray(L0_attn_conv_w, dtype=np.float32)
    convbd = np.zeros((128, 5120), dtype=np.float32)
    for p in range(8):
        for k in range(5):
            c0 = (p * 5 + k) * 128
            convbd[0:64, c0:c0 + 64] = convw[128 * p:128 * p + 64, :, k].T
            convbd[64:128, c0 + 64:c0 + 128] = convw[128 * p + 64:128 * p + 128, :, k].T

    eye = np.eye(128, dtype=np.float32)
    selp = np.zeros((16, 264), dtype=np.float32)
    for h in range(16):
        for p in range(128):
            selp[h, p] = 1.0 if (p // 64) == (h % 2) else 0.0
    for r in range(2):
        for p in range(128):
            selp[r, 128 + p] = 1.0 if (p // 64) == r else 0.0
    for h in range(16):
        for s in range(8):
            selp[h, 256 + s] = 1.0 if (h // 2) == s else 0.0

    shared = {
        "wq": f32(L0_attn_Wq), "wk": f32(L0_attn_Wk), "wv": f32(L0_attn_Wv),
        "wpq": f32(L0_attn_Wphi_q), "wpk": f32(L0_attn_Wphi_k),
        "w1": f32(L0_attn_W1), "w2": f32(L0_attn_W2),
        "wg": f32(np.asarray(L0_ffn_w_gate).T), "wu": f32(np.asarray(L0_ffn_w_up).T),
        "wd": f32(np.asarray(L0_ffn_w_down).T),
        "bias": bias, "b1p": b1p, "convbd": convbd, "eye": eye, "selp": selp,
    }

    in_maps = []
    for c in range(NCORES):
        b, q = c // 4, c % 4
        s0 = q * TOK
        xcore = np.zeros((E, TOKH), dtype=np.float32)
        xcore[:, 0:TOK] = x[b, s0:s0 + TOK, :].T
        for j, idx in enumerate((s0 - 2, s0 - 1, s0 + TOK, s0 + TOK + 1)):
            if 0 <= idx < LQ:
                xcore[:, TOK + j] = x[b, idx, :]
        mask = np.array([[1.0 if q > 0 else 0.0, 1.0 if q > 0 else 0.0,
                          1.0 if q < 3 else 0.0, 1.0 if q < 3 else 0.0]],
                        dtype=np.float32)
        m = dict(shared)
        m["x"] = xcore
        m["mask"] = mask
        in_maps.append(m)

    nc = build_nc(taps=_TAPS)
    res = run_bass_kernel_spmd(nc, in_maps, core_ids=list(range(NCORES)))

    out = np.empty((2, LQ, E), dtype=np.float32)
    for c in range(NCORES):
        b, q = c // 4, c % 4
        out[b, q * TOK:(q + 1) * TOK, :] = res.results[c]["out"].T
    if _TAPS:
        kernel._last_results = res
    return out


# version 09
# speedup vs baseline: 1.0374x; 1.0374x over previous
import numpy as np
import concourse.bass as bass
import concourse.mybir as mybir
from concourse.bass_utils import run_bass_kernel_spmd

F32 = mybir.dt.float32
BF16 = mybir.dt.bfloat16
F16 = mybir.dt.float16
I32 = mybir.dt.int32
AF = mybir.ActivationFunctionType
ALU = mybir.AluOpType
AX = mybir.AxisListType

E = 1024
H = 16
D = 64
MLPD = 4096
LQ = 16384
NCORES = 8
TOK = 4096
TT = 512
HT = 256
NT = 8
HALO = 4
TOKH = TOK + HALO
EC = E // 128
MC = MLPD // 128
ROFF = 1536.0
EPS_LN = 1e-5
EPS_ATT = 1e-6

_TAPS = False
_STAGES = 5
_S1_TILES = None
_S1_FFN_ONLY = False


class Sem:
    def __init__(self, nc, name):
        self.h = nc.alloc_semaphore(name)
        self.n = 0

    def inc(self, instr, k=1):
        instr.then_inc(self.h, k)
        self.n += k
        return instr


class Builder:
    def __init__(self, taps=False, stages=None):
        self.taps = taps
        self.stages = _STAGES if stages is None else stages
        nc = bass.Bass("TRN2", target_bir_lowering=False)
        self.nc = nc

        di = {}
        di["x"] = nc.dram_tensor("x", [E, TOKH], F32, kind="ExternalInput").ap()
        for nm in ("wq", "wk", "wv", "wpq", "wpk"):
            di[nm] = nc.dram_tensor(nm, [E, E], F32, kind="ExternalInput").ap()
        di["w1"] = nc.dram_tensor("w1", [E, MLPD], F32, kind="ExternalInput").ap()
        di["w2"] = nc.dram_tensor("w2", [MLPD, E], F32, kind="ExternalInput").ap()
        di["wg"] = nc.dram_tensor("wg", [E, MLPD], F32, kind="ExternalInput").ap()
        di["wu"] = nc.dram_tensor("wu", [E, MLPD], F32, kind="ExternalInput").ap()
        di["wd"] = nc.dram_tensor("wd", [MLPD, E], F32, kind="ExternalInput").ap()
        di["bias"] = nc.dram_tensor("bias", [128, 160], F32, kind="ExternalInput").ap()
        di["b1p"] = nc.dram_tensor("b1p", [128, 32], F32, kind="ExternalInput").ap()
        di["convbd"] = nc.dram_tensor("convbd", [128, 5120], F32, kind="ExternalInput").ap()
        di["eye"] = nc.dram_tensor("eye", [128, 128], F32, kind="ExternalInput").ap()
        di["selp"] = nc.dram_tensor("selp", [16, 264], F32, kind="ExternalInput").ap()
        di["mask"] = nc.dram_tensor("mask", [1, 4], F32, kind="ExternalInput").ap()
        self.di = di
        self.out = nc.dram_tensor("out", [E, TOK], F32, kind="ExternalOutput").ap()

        ds = {}
        for nm in ("s_y1", "s_v", "s_pq", "s_kk", "s_fk", "s_vv"):
            ds[nm] = nc.dram_tensor(nm, [E, TOKH], F32).ap()
        for nm in ("q", "k", "v", "pq", "pk"):
            ds["wsh_" + nm] = nc.dram_tensor("wsh_" + nm, [128, 8192], BF16).ap()
            ds["wsl_" + nm] = nc.dram_tensor("wsl_" + nm, [128, 8192], BF16).ap()
        for nm in ("w1h", "w1l", "wgq", "wuq"):
            ds["ws_" + nm] = nc.dram_tensor("ws_" + nm, [128, 32768], BF16).ap()
        for nm in ("w2h", "w2l", "wdq"):
            ds["ws_" + nm] = nc.dram_tensor("ws_" + nm, [128, 32768], BF16).ap()
        ds["cb_qg_i"] = nc.dram_tensor("cb_qg_i", [128, 8], F32).ap()
        ds["cb_qg_o"] = nc.dram_tensor("cb_qg_o", [128, 8], F32).ap()
        ds["cb_ag_i"] = nc.dram_tensor("cb_ag_i", [128, 536], F32).ap()
        ds["cb_ag_o"] = nc.dram_tensor("cb_ag_o", [512, 536], F32).ap()
        self.ds = ds

        self.AR = nc.alloc_sbuf_tensor("AR", [128, 31808], F32).ap()
        self.SL_X = 0
        self.SL_Y1 = 4096
        self.SL_V = 8256
        self.SL_RAW = 12352
        self.SL_NORM = 16448
        self.SL_GU = 20544
        self.SL_BC = 28736
        self.SL_MISC = 30784
        self.AB = nc.alloc_sbuf_tensor("AB", [128, 22528], BF16).ap()
        self.BL_UQ = 0
        self.BL_VH = 4096
        self.BL_VL = 8192
        self.BL_WP = 12288
        self.BL_T = 20480
        self.T0 = nc.alloc_sbuf_tensor("T0", [128, 512], F32).ap()
        self.T1 = nc.alloc_sbuf_tensor("T1", [128, 512], F32).ap()
        self.TH = nc.alloc_sbuf_tensor("TH", [128, 512], F16).ap()
        self.TH2 = nc.alloc_sbuf_tensor("TH2", [128, 512], F16).ap()
        self.FEN = nc.alloc_sbuf_tensor("FEN", [128, 8], F32).ap()
        self.TI = nc.alloc_sbuf_tensor("TI", [128, 512], I32).ap()
        self.ST = nc.alloc_sbuf_tensor("ST", [128, 320], F32).ap()
        self.WS = nc.alloc_sbuf_tensor("WS", [1, 16], F32).ap()
        self.SR0 = nc.alloc_sbuf_tensor("SR0", [1, 512], F32).ap()
        self.SR1 = nc.alloc_sbuf_tensor("SR1", [1, 512], F32).ap()
        self.REPB = nc.alloc_sbuf_tensor("REPB", [128, 1024], F32).ap()
        self.ONES128 = nc.alloc_sbuf_tensor("ONES128", [128, 128], F32).ap()
        self.MROW = nc.alloc_sbuf_tensor("MROW", [1, 4], F32).ap()
        self.KVSB = nc.alloc_sbuf_tensor("KVSB", [128, 536], F32).ap()
        self.QG = nc.alloc_sbuf_tensor("QG", [128, 16], F32).ap()
        self.MX = nc.alloc_sbuf_tensor("MX", [2, 64], F32).ap()
        self.ETM = nc.alloc_sbuf_tensor("ETM", [128, 64], F32).ap()
        self.QGE = nc.alloc_sbuf_tensor("QGE", [128, 16], F32).ap()
        self.KSE = nc.alloc_sbuf_tensor("KSE", [128, 16], F32).ap()
        self.EYE = nc.alloc_sbuf_tensor("EYE", [128, 128], F32).ap()
        self.CONST = nc.alloc_sbuf_tensor("CONST", [128, 12], F32).ap()
        self.ONESR = nc.alloc_sbuf_tensor("ONESR", [1, 128], F32).ap()
        self.SELP = nc.alloc_sbuf_tensor("SELP", [16, 264], F32).ap()
        self.BIAS = nc.alloc_sbuf_tensor("BIAS", [128, 160], F32).ap()
        self.B1P = nc.alloc_sbuf_tensor("B1P", [128, 32], F32).ap()
        self.MASKB = nc.alloc_sbuf_tensor("MASKB", [128, 4], F32).ap()
        self.MINIA = nc.alloc_sbuf_tensor("MINIA", [128, 32], F32).ap()
        self.PK0 = nc.alloc_sbuf_tensor("PK0", [128, 16], F32).ap()
        self.COMB = nc.alloc_sbuf_tensor("COMB", [2, 64], F32).ap()
        self.ESB = self.AR[0:2, self.SL_GU:self.SL_GU + 4096]
        self.DEN = self.AR[0:2, self.SL_GU + 4096:self.SL_GU + 5120 + 1024]
        self.GCB = nc.alloc_sbuf_tensor("GCB", [128, 8], F32).ap()
        self.VTMO = None

        self.PS = [nc.alloc_psum_tensor(f"ps{i}", [128, 512], F32).ap() for i in range(8)]
        self.guard = [None] * 8
        self.wp_pe = 0

        nc_ = nc
        self.s_sy = Sem(nc_, "s_sy")
        self.s_sc = Sem(nc_, "s_sc")
        self.s_gp = Sem(nc_, "s_gp")
        self.s_st = Sem(nc_, "s_st")
        self.c_pe = Sem(nc_, "c_pe")
        self.c_ve = Sem(nc_, "c_ve")
        self.c_ac = Sem(nc_, "c_ac")
        self.c_gc = Sem(nc_, "c_gc")
        self.c_cc = Sem(nc_, "c_cc")
        self.esem = {"sync": Sem(nc_, "b_sy"), "vector": Sem(nc_, "b_ve"),
                     "scalar": Sem(nc_, "b_sc"), "tensor": Sem(nc_, "b_pe"),
                     "gpsimd": Sem(nc_, "b_gp")}

        self._tmv = {}
        self._tmn = 0

        self.tap_t = {}
        if taps:
            for nm in ("s_y1", "s_v", "s_pq", "s_kk", "s_fk", "s_vv"):
                self.tap_t[nm] = nc.dram_tensor("tap_" + nm, [E, TOKH], F32,
                                                kind="ExternalOutput").ap()
            self.tap_t["qg"] = nc.dram_tensor("tap_qg", [128, 16], F32,
                                              kind="ExternalOutput").ap()
            self.tap_t["kv"] = nc.dram_tensor("tap_kv", [128, 536], F32,
                                              kind="ExternalOutput").ap()
            self.tap_t["ws_wgq"] = nc.dram_tensor("tap_ws_wgq", [128, 32768], BF16,
                                                  kind="ExternalOutput").ap()
            self.tap_t["ws_wdq"] = nc.dram_tensor("tap_ws_wdq", [128, 32768], BF16,
                                                  kind="ExternalOutput").ap()
            self.tap_t["wsh_q"] = nc.dram_tensor("tap_wsh_q", [128, 8192], BF16,
                                                 kind="ExternalOutput").ap()
            self.tap_t["wsl_q"] = nc.dram_tensor("tap_wsl_q", [128, 8192], BF16,
                                                 kind="ExternalOutput").ap()
            self.tap_t["ws_w2h"] = nc.dram_tensor("tap_ws_w2h", [128, 32768], BF16,
                                                  kind="ExternalOutput").ap()
            self.tap_t["wsr"] = nc.dram_tensor("tap_wsr", [1, 16], F32,
                                               kind="ExternalOutput").ap()
            self.tap_t["uq"] = nc.dram_tensor("tap_uq", [128, 4096], BF16,
                                              kind="ExternalOutput").ap()
            self.tap_t["gu"] = nc.dram_tensor("tap_gu", [128, 8192], F32,
                                              kind="ExternalOutput").ap()
            self.tap_t["bc"] = nc.dram_tensor("tap_bc", [128, 3072], F32,
                                              kind="ExternalOutput").ap()
            self.tap_t["st"] = nc.dram_tensor("tap_st", [128, 320], F32,
                                              kind="ExternalOutput").ap()
            self.tap_t["const"] = nc.dram_tensor("tap_const", [128, 12], F32,
                                                 kind="ExternalOutput").ap()

    def sec(self, eng, fn):
        getattr(self.block, eng)(fn)

    def vfence(self, vector):
        vector.drain()
        return vector.memset(self.FEN[:, 0:1], 0.0)

    def tm(self, name, nblk=4):
        if name not in self._tmv:
            self._tmv[name] = self._tmn
            self._tmn += 4
            assert self._tmn <= 320
        o = self._tmv[name]
        return self.ST[:, o:o + nblk]

    def bank(self, i, eng):
        g = self.guard[i]
        if g is not None:
            eng.wait_ge(g[0].h, g[1])
            self.guard[i] = None
        return self.PS[i]

    def set_guard(self, i, sem):
        self.guard[i] = (sem, sem.n)

    def barrier(self, engs=("sync", "vector", "scalar", "tensor", "gpsimd")):
        counts = {}
        dma_waits = {"sync": [self.s_sy], "scalar": [self.s_sc],
                     "gpsimd": [self.s_gp, self.s_st]}
        for e in engs:
            s = self.esem[e]

            def f(eng, s=s, e=e):
                for dsem in dma_waits.get(e, ()):
                    eng.wait_ge(dsem.h, dsem.n)
                if e == "tensor":
                    eng.drain()
                    eng.sem_inc(s.h, 1)
                elif e == "vector":
                    eng.drain()
                    eng.memset(self.FEN[:, 1:2], 0.0).then_inc(s.h, 1)
                elif e == "gpsimd":
                    eng.memset(self.FEN[:, 2:3], 0.0).then_inc(s.h, 1)
                elif e == "scalar":
                    eng.activation(self.FEN[:, 3:4], self.FEN[:, 3:4],
                                   AF.Identity).then_inc(s.h, 1)
                else:
                    eng.sem_inc(s.h, 1)
            self.sec(e, f)
            s.n += 1
            counts[e] = s.n
        for e in engs:
            def f(eng, e=e):
                for o, c in counts.items():
                    if o != e:
                        eng.wait_ge(self.esem[o].h, c)
            self.sec(e, f)

    def fm(self, dram, c0, w):
        return dram.rearrange("(c p) t -> p c t", p=128)[:, :, c0:c0 + w]

    def arv(self, sl, w8):
        return self.AR[:, sl:sl + 8 * w8].rearrange("p (c t) -> p c t", t=w8)

    def chunk(self, sl, c, w, off=0):
        base = sl + c * w if off == 0 else sl + off + c * w
        return self.AR[:, base:base + w]

    def mm3(self, te, out, wh, wl, xh, xl, start, stop):
        te.matmul(out, wh, xh, start=start, stop=False)
        te.matmul(out, wl, xh, start=False, stop=False)
        te.matmul(out, wh, xl, start=False, stop=stop)

    def build(self):
        nc = self.nc
        with nc.Block() as block:
            self.block = block
            self.emit_preamble()
            if self.stages >= 1:
                self.emit_weight_prep()
            if self.stages >= 2:
                self.emit_sweep1()
            if self.stages >= 3:
                self.emit_qg_collective()
            if self.stages >= 4:
                self.emit_sweep2()
            if self.stages >= 5:
                self.emit_sweep3()
            self.emit_taps_and_close()
        return nc

    def emit_preamble(self):
        di = self.di

        def f(sync):
            self.s_sy.inc(sync.dma_start(out=self.EYE[:], in_=di["eye"][:]), 16)
            self.s_sy.inc(sync.dma_start(out=self.SELP[:], in_=di["selp"][:]), 16)
            self.s_sy.inc(sync.dma_start(out=self.BIAS[:], in_=di["bias"][:]), 16)
            self.s_sy.inc(sync.dma_start(out=self.B1P[:], in_=di["b1p"][:]), 16)
            self.s_sy.inc(sync.dma_start(out=self.MROW[:], in_=di["mask"][:]), 16)
        self.sec("sync", f)
        ld = self.s_sy.n

        def f(vector):
            vector.memset(self.CONST[:, 0:1], 1.0)
            vector.memset(self.ONESR[:], 1.0)
            vector.memset(self.ONES128[:], 1.0)
            vector.memset(self.QG[:], 0.0)
            vector.wait_ge(self.s_sy.h, self.s_sy.n)
            vector.tensor_scalar(self.GCB[:], self.BIAS[:, 112:120],
                                 self.BIAS[:, 120:121], None, ALU.mult)
            self.c_ve.inc(vector.memset(self.MX[:, 2:3], 0.0))
        self.sec("vector", f)

        def f(tensor):
            tensor.wait_ge(self.s_sy.h, ld)
            tensor.wait_ge(self.c_ve.h, self.c_ve.n)
            self.c_pe.inc(tensor.matmul(self.PS[0][:, 0:4], self.ONESR[:],
                                        self.MROW[:], start=True, stop=True))
        self.sec("tensor", f)

        def f(scalar):
            scalar.wait_ge(self.s_sy.h, ld)
            self.c_ac.inc(scalar.activation(self.PK0[:, 8:16], self.BIAS[:, 32:40], AF.Tanh))
        self.sec("scalar", f)

        def f(vector):
            vector.wait_ge(self.c_pe.h, self.c_pe.n)
            vector.tensor_copy(self.MASKB[:], self.PS[0][:, 0:4])
            vector.wait_ge(self.c_ac.h, self.c_ac.n)
            self.c_ve.inc(vector.tensor_scalar(self.PK0[:, 0:8], self.PK0[:, 8:16],
                                               1.0, None, ALU.add))
        self.sec("vector", f)

    def emit_weight_prep(self):
        di, ds = self.di, self.ds
        GU0 = self.SL_GU
        GU1 = self.SL_GU + 4096
        NORM = self.SL_NORM

        specs = [("wg", 8, 4096, 0), ("wu", 8, 4096, 1), ("wd", 32, 1024, 2)]
        for wname, nchunk, outw, rowname in specs:
            ldm, acm, vem = [], [], []
            ac0, ve0 = self.c_ac.n, self.c_ve.n
            for c in range(nchunk):
                half = GU0 if c % 2 == 0 else GU1

                def f(sync, c=c, half=half):
                    if c < 2:
                        sync.wait_ge(self.c_ac.h, ac0)
                        sync.wait_ge(self.c_ve.h, ve0)
                    if c >= 2:
                        sync.wait_ge(self.c_ac.h, acm[c - 2])
                    self.s_sy.inc(sync.dma_start(
                        out=self.AR[:, half:half + outw],
                        in_=di[wname][c * 128:(c + 1) * 128, :]), 16)
                self.sec("sync", f)
                ldm.append(self.s_sy.n)

                def f(scalar, c=c, half=half):
                    scalar.wait_ge(self.s_sy.h, ldm[c])
                    if c == 0:
                        scalar.wait_ge(self.c_ve.h, ve0)
                    if c >= 1:
                        scalar.wait_ge(self.c_ve.h, vem[c - 1])
                    self.c_ac.inc(scalar.activation(
                        self.AR[:, NORM:NORM + outw], self.AR[:, half:half + outw], AF.Abs))
                self.sec("scalar", f)
                acm.append(self.c_ac.n)

                def f(vector, c=c):
                    nseg = outw // 512
                    vector.wait_ge(self.c_ac.h, acm[c])
                    for j in range(nseg):
                        vector.reduce_sum(self.T0[:, j:j + 1],
                                          self.AR[:, NORM + j * 512:NORM + (j + 1) * 512], AX.X)
                    vector.drain()
                    self.c_ve.inc(vector.reduce_sum(
                        self.T0[:, 16 + (c % 32):17 + (c % 32)], self.T0[:, 0:nseg], AX.X))
                self.sec("vector", f)
                vem.append(self.c_ve.n)

            def f(vector, nchunk=nchunk, vem=vem):
                vector.wait_ge(self.c_ve.h, vem[-1])
                vector.wait_ge(self.c_pe.h, self.c_pe.n)
                vector.drain()
                self.c_ve.inc(vector.reduce_sum(self.T0[:, 60:61],
                                                self.T0[:, 16:16 + nchunk], AX.X))
            self.sec("vector", f)

            def f(tensor):
                ps = self.bank(1, tensor)
                tensor.wait_ge(self.c_ve.h, self.c_ve.n)
                self.c_pe.inc(tensor.matmul(ps[0:1, 0:1], self.CONST[:, 0:1],
                                            self.T0[:, 60:61], start=True, stop=True))
            self.sec("tensor", f)

            def f(vector):
                vector.wait_ge(self.c_pe.h, self.c_pe.n)
                self.c_ve.inc(vector.tensor_copy(self.WS[0:1, rowname:rowname + 1],
                                                 self.PS[1][0:1, 0:1]))
            self.sec("vector", f)
            self.set_guard(1, self.c_ve)

        def f(tensor):
            ps = self.bank(1, tensor)
            tensor.wait_ge(self.c_ve.h, self.c_ve.n)
            tensor.matmul(ps[:, 0:1], self.ONESR[:], self.WS[0:1, 0:1], start=True, stop=True)
            tensor.matmul(ps[:, 1:2], self.ONESR[:], self.WS[0:1, 1:2], start=True, stop=True)
            self.c_pe.inc(tensor.matmul(ps[:, 2:3], self.ONESR[:], self.WS[0:1, 2:3],
                                        start=True, stop=True))
        self.sec("tensor", f)

        def f(vector):
            vector.wait_ge(self.c_pe.h, self.c_pe.n)
            C = self.CONST
            specs2 = [(0, 8 * 128 * 4096, 2, 5), (1, 8 * 128 * 4096, 3, 6),
                      (2, 32 * 128 * 1024, 4, 7)]
            last = None
            for i, nelem, c127, csw in specs2:
                vector.tensor_scalar(C[:, 8:9], self.PS[1][:, i:i + 1],
                                     1.0 / nelem, None, ALU.mult)
                vector.drain()
                vector.tensor_scalar(C[:, 9:10], C[:, 8:9], 1e-5, None, ALU.max)
                vector.drain()
                fac = (0.5 / 127.0) if i == 2 else (1.0 / 127.0)
                vector.tensor_scalar(C[:, c127:c127 + 1], C[:, 9:10], fac, None, ALU.mult)
                last = vector.reciprocal(C[:, csw:csw + 1], C[:, 9:10])
                vector.drain()
            self.c_ve.inc(last)
        self.sec("vector", f)
        self.set_guard(1, self.c_ve)

        qspecs = [("wg", "ws_wgq", 8, 4096, 5, True),
                  ("wu", "ws_wuq", 8, 4096, 6, True),
                  ("wd", "ws_wdq", 32, 1024, 7, False)]
        qstm = []
        for wname, sname, nchunk, outw, swcol, fo_outer in qspecs:
            swmark = self.c_ve.n

            ldm, vem, stm = [], [], []
            ve0 = self.c_ve.n
            ac0 = self.c_ac.n
            for c in range(nchunk):
                half = GU0 if c % 2 == 0 else GU1
                g = len(qstm)

                def f(sync, c=c, half=half):
                    if c < 2:
                        sync.wait_ge(self.c_ve.h, max(swmark, ve0))
                        sync.wait_ge(self.c_ac.h, ac0)
                    if c >= 2:
                        sync.wait_ge(self.c_ve.h, vem[c - 2])
                    self.s_sy.inc(sync.dma_start(
                        out=self.AR[:, half:half + outw],
                        in_=di[wname][c * 128:(c + 1) * 128, :]), 16)
                self.sec("sync", f)
                ldm.append(self.s_sy.n)

                def f(vector, c=c, half=half, g=g):
                    nseg = outw // 512
                    vector.wait_ge(self.s_sy.h, ldm[c])
                    if g >= 2:
                        vector.wait_ge(self.s_st.h, qstm[g - 2])
                    wp = self.BL_WP + ((c % 2) * 4096 if outw == 4096
                                       else (c % 4) * 2048)
                    tba = self.TH
                    tbb = self.TH2
                    for j in range(nseg):
                        src = self.AR[:, half + j * 512:half + (j + 1) * 512]
                        dst = self.AB[:, wp + j * 512:wp + (j + 1) * 512]
                        vector.tensor_scalar(self.T0[:], src,
                                             self.CONST[:, swcol:swcol + 1], None, ALU.mult)
                        vector.tensor_scalar(tba, self.T0[:], ROFF, None, ALU.add)
                        vector.tensor_scalar(tbb, tba, -ROFF, 1.0, ALU.add, ALU.min)
                        vector.tensor_scalar(dst, tbb, -1.0, None, ALU.max)
                    self.c_ve.inc(self.vfence(vector))
                self.sec("vector", f)
                vem.append(self.c_ve.n)

                def f(gpsimd, c=c):
                    gpsimd.wait_ge(self.c_ve.h, vem[c])
                    wp = self.BL_WP + ((c % 2) * 4096 if outw == 4096
                                       else (c % 4) * 2048)
                    buf = self.AB[:, wp:wp + outw]
                    if fo_outer:
                        dst = self.ds[sname].rearrange("p (a b) -> p a b", b=2048)[
                            :, :, c * 256:(c + 1) * 256]
                        src = buf.rearrange("p (a b) -> p a b", b=256)
                    else:
                        dst = self.ds[sname][:, c * 1024:(c + 1) * 1024]
                        src = buf
                    self.s_st.inc(gpsimd.dma_start(out=dst, in_=src), 16)
                self.sec("gpsimd", f)
                stm.append(self.s_st.n)
                qstm.append(self.s_st.n)

        sspecs = [("wq", "wsh_q", "wsl_q", 8, 1024, True),
                  ("wk", "wsh_k", "wsl_k", 8, 1024, True),
                  ("wv", "wsh_v", "wsl_v", 8, 1024, True),
                  ("wpq", "wsh_pq", "wsl_pq", 8, 1024, True),
                  ("wpk", "wsh_pk", "wsl_pk", 8, 1024, True),
                  ("w1", "ws_w1h", "ws_w1l", 8, 4096, True),
                  ("w2", "ws_w2h", "ws_w2l", 32, 1024, False)]
        for wname, hname, lname, nchunk, outw, fo_outer in sspecs:
            ldm, vem, stm = [], [], []
            ve0 = self.c_ve.n
            st0 = self.s_st.n
            for c in range(nchunk):
                half = GU0 if c % 2 == 0 else GU1
                hb = self.AB[:, (c % 2) * 4096:(c % 2) * 4096 + outw]
                lb = self.AB[:, 8192 + (c % 2) * 4096:8192 + (c % 2) * 4096 + outw]

                def f(sync, c=c, half=half):
                    if c < 2:
                        sync.wait_ge(self.c_ve.h, ve0)
                    if c >= 2:
                        sync.wait_ge(self.c_ve.h, vem[c - 2])
                    self.s_sy.inc(sync.dma_start(
                        out=self.AR[:, half:half + outw],
                        in_=di[wname][c * 128:(c + 1) * 128, :]), 16)
                self.sec("sync", f)
                ldm.append(self.s_sy.n)

                def f(vector, c=c, half=half, hb=hb, lb=lb):
                    vector.wait_ge(self.s_sy.h, ldm[c])
                    if c < 2:
                        vector.wait_ge(self.s_st.h, st0)
                    if c >= 2:
                        vector.wait_ge(self.s_st.h, stm[2 * (c - 2) + 1])
                    src = self.AR[:, half:half + outw]
                    vector.tensor_copy(hb, src)
                    self.c_ve.inc(vector.tensor_sub(lb, src, hb))
                self.sec("vector", f)
                vem.append(self.c_ve.n)

                def f(gpsimd, c=c, hb=hb, lb=lb):
                    gpsimd.wait_ge(self.c_ve.h, vem[c])
                    for name, buf in ((hname, hb), (lname, lb)):
                        if fo_outer:
                            nfo2 = outw * 8 // 2048
                            dst = self.ds[name].rearrange("p (a b) -> p a b", b=2048)[
                                :, :, c * 256:(c + 1) * 256]
                            src = buf.rearrange("p (a b) -> p a b", b=256)
                        else:
                            dst = self.ds[name][:, c * 1024:(c + 1) * 1024]
                            src = buf
                        self.s_st.inc(gpsimd.dma_start(out=dst, in_=src), 16)
                        stm.append(self.s_st.n)
                self.sec("gpsimd", f)
        self.barrier()

    def emit_sums(self, src_sl, w, tmpsq_sl, bank, bank2=5, waits=(), nch=8):
        acm = []

        def fsq(scalar):
            for sm, cnt in waits:
                scalar.wait_ge(sm.h, cnt)
            for c in range(nch):
                s = self.AR[:, src_sl + c * w:src_sl + c * w + w]
                d = self.AR[:, tmpsq_sl + c * w:tmpsq_sl + c * w + w]
                self.c_ac.inc(scalar.activation(d, s, AF.Square))
                acm.append(self.c_ac.n)
        self.sec("scalar", fsq)

        def fmm(tensor):
            ps = self.bank(bank, tensor)
            ps2 = self.bank(bank2, tensor)
            for sm, cnt in waits:
                tensor.wait_ge(sm.h, cnt)
            for c in range(nch):
                s = self.AR[:, src_sl + c * w:src_sl + c * w + w]
                tensor.matmul(ps[0:1, 0:w], self.CONST[:, 0:1], s,
                              start=(c == 0), stop=(c == nch - 1))
            mm = None
            for c in range(nch):
                tensor.wait_ge(self.c_ac.h, acm[c])
                d = self.AR[:, tmpsq_sl + c * w:tmpsq_sl + c * w + w]
                mm = tensor.matmul(ps2[0:1, 0:w], self.CONST[:, 0:1], d,
                                   start=(c == 0), stop=(c == nch - 1))
            self.c_pe.inc(mm)
        self.sec("tensor", fmm)
        return self.c_pe.n

    def emit_minmax(self, src_sl, w, bank, mxname, mnname, waits=(), nch=8):
        BCT = self.SL_BC

        def f(vector):
            for sm, cnt in waits:
                vector.wait_ge(sm.h, cnt)
            last = None
            for which in (0, 1):
                op = ALU.max if which == 0 else ALU.min
                b0 = BCT + (0 + 2 * which) * 512
                b1 = BCT + (1 + 2 * which) * 512
                cur = src_sl
                dsts = [b0, b1]
                for c in range(1, nch):
                    sc = self.AR[:, src_sl + c * w:src_sl + c * w + w]
                    d = dsts[(c - 1) % 2]
                    last = vector.tensor_tensor(self.AR[:, d:d + w],
                                                self.AR[:, cur:cur + w], sc, op)
                    cur = d
            self.c_ve.inc(last)
        self.sec("vector", f)
        fin = BCT + ((nch - 2) % 2) * 512
        self.emit_minmax_rows(fin, fin + 1024, w, bank, mxname, mnname)

    def emit_bcast(self, row, w, bank, waits=()):
        def f(tensor):
            ps = self.bank(bank, tensor)
            for sm, cnt in waits:
                tensor.wait_ge(sm.h, cnt)
            self.c_pe.inc(tensor.matmul(ps[:, 0:w], self.ONESR[:], row,
                                        start=True, stop=True))
        self.sec("tensor", f)
        return self.c_pe.n

    def emit_bcasts(self, specs, w, waits=()):
        nblk = (w + 127) // 128
        assert len(specs) * nblk <= 8

        def fv(vector):
            for sm, cnt in waits:
                vector.wait_ge(sm.h, cnt)
            vector.wait_ge(self.c_pe.h, self.c_pe.n)
            last = None
            for i, (tmt, bank) in enumerate(specs):
                for b in range(nblk):
                    last = vector.tensor_scalar(
                        self.REPB[:, (i * nblk + b) * 128:(i * nblk + b) * 128 + 128],
                        self.ONES128[:], tmt[:, b:b + 1], None, ALU.mult)
            self.c_ve.inc(last)
        self.sec("vector", fv)
        vem = self.c_ve.n

        def ft(tensor):
            mm = None
            for i, (tmt, bank) in enumerate(specs):
                ps = self.bank(bank, tensor)
                if i == 0:
                    tensor.wait_ge(self.c_ve.h, vem)
                for b in range(nblk):
                    mm = tensor.matmul(
                        ps[:, b * 128:b * 128 + 128],
                        self.REPB[:, (i * nblk + b) * 128:(i * nblk + b) * 128 + 128],
                        self.EYE[:], is_transpose=True)
            self.c_pe.inc(mm)
        self.sec("tensor", ft)
        return self.c_pe.n

    def emit_ln_stats(self, bank, w, pfx, minmax=False, bank2=5):
        nblk = (w + 127) // 128
        mu = self.tm(pfx + "_mu", nblk)
        rs = self.tm(pfx + "_rs", nblk)
        mrs = self.tm(pfx + "_mrs", nblk)
        t1 = self.tm(pfx + "_t1", nblk)
        t2 = self.tm(pfx + "_t2", nblk)

        def f(vector):
            vector.wait_ge(self.c_pe.h, self.c_pe.n)
            vector.tensor_copy(self.SR0[0:1, 0:w], self.PS[bank][0:1, 0:w])
            self.c_ve.inc(vector.tensor_copy(self.SR1[0:1, 0:w],
                                             self.PS[bank2][0:1, 0:w]))
        self.sec("vector", f)
        vem = self.c_ve.n

        def f(tensor):
            ps = self.PS[bank]
            tensor.wait_ge(self.c_ve.h, vem)
            mm = None
            for b in range(nblk):
                bw = min(128, w - b * 128)
                tensor.matmul(ps[0:bw, b:b + 1],
                              self.SR0[0:1, b * 128:b * 128 + bw],
                              self.EYE[0:1, 0:1], is_transpose=True)
                mm = tensor.matmul(ps[0:bw, 8 + b:9 + b],
                                   self.SR1[0:1, b * 128:b * 128 + bw],
                                   self.EYE[0:1, 0:1], is_transpose=True)
            self.c_pe.inc(mm)
        self.sec("tensor", f)
        pem = self.c_pe.n

        t3 = self.tm(pfx + "_t3", nblk)
        t4 = self.tm(pfx + "_t4", nblk)

        def f(vector):
            vector.wait_ge(self.c_pe.h, pem)
            vector.tensor_scalar(mu, self.PS[bank][:, 0:nblk], 1.0 / E, None, ALU.mult)
            vector.tensor_scalar(t3, self.PS[bank][:, 8:8 + nblk], 1.0 / E, None, ALU.mult)
            vector.drain()
            vector.tensor_tensor(t4, mu, mu, ALU.mult)
            vector.drain()
            vector.tensor_tensor(t1, t3, t4, ALU.subtract)
            vector.drain()
            vector.tensor_scalar(t2, t1, EPS_LN, None, ALU.add)
            vector.drain()
            self.c_ve.inc(vector.reciprocal(rs, t2))
        self.sec("vector", f)
        self.set_guard(bank, self.c_ve)
        self.set_guard(bank2, self.c_ve)

        def f(scalar):
            scalar.wait_ge(self.c_ve.h, self.c_ve.n)
            self.c_ac.inc(scalar.activation(rs, rs, AF.Sqrt))
        self.sec("scalar", f)

        def f(vector):
            vector.wait_ge(self.c_ac.h, self.c_ac.n)
            last = vector.tensor_tensor(mrs, mu, rs, ALU.mult)
            if minmax:
                mx = self.tm(pfx + "_mx", nblk)
                mn = self.tm(pfx + "_mn", nblk)
                cl = self.tm(pfx + "_cl", nblk)
                cc = self.tm(pfx + "_c", nblk)
                mc = self.tm(pfx + "_mc", nblk)
                vector.tensor_tensor(t1, mx, mu, ALU.subtract)
                vector.tensor_tensor(t2, mu, mn, ALU.subtract)
                vector.drain()
                vector.tensor_tensor(t3, t1, t2, ALU.max)
                vector.drain()
                vector.tensor_tensor(t4, t3, rs, ALU.mult)
                vector.drain()
                vector.tensor_scalar(cl, t4, 1e-5, None, ALU.max)
                vector.drain()
                vector.reciprocal(t1, cl)
                vector.drain()
                vector.tensor_tensor(t2, rs, t1, ALU.mult)
                vector.drain()
                vector.tensor_scalar(cc, t2, 127.0, None, ALU.mult)
                vector.drain()
                last = vector.tensor_tensor(mc, mu, cc, ALU.mult)
            self.c_ve.inc(last)
        self.sec("vector", f)

    def emit_quant(self, src_sl, w, uq_off, pfx, bankc, bankmc, nch=8):
        nblk = (w + 127) // 128
        self.emit_bcasts([(self.tm(pfx + "_c", nblk), bankc),
                          (self.tm(pfx + "_mc", nblk), bankmc)], w,
                         waits=[(self.c_ve, self.c_ve.n)])
        pem = self.c_pe.n

        def f(vector):
            vector.wait_ge(self.c_pe.h, pem)
            dr = w < 128
            tb = self.TH
            last = None
            for c in range(nch):
                sc = self.AR[:, src_sl + c * w:src_sl + c * w + w]
                uqc = self.AB[:, uq_off + c * w:uq_off + c * w + w]
                vector.tensor_tensor(self.T0[:, 0:w], sc, self.PS[bankc][:, 0:w], ALU.mult)
                if dr:
                    vector.drain()
                vector.scalar_tensor_tensor(tb[:, 0:w], self.T0[:, 0:w], ROFF,
                                            self.PS[bankmc][:, 0:w], ALU.add, ALU.subtract)
                if dr:
                    vector.drain()
                last = vector.tensor_scalar(uqc, tb[:, 0:w], -ROFF, 127.0,
                                            ALU.add, ALU.min)
            self.c_ve.inc(last)
        self.sec("vector", f)
        self.set_guard(bankc, self.c_ve)
        self.set_guard(bankmc, self.c_ve)
        return self.c_ve.n

    def emit_minmax_rows(self, trx_sl, trn_sl, w, bank, mxname, mnname, waits=()):
        nblk = (w + 127) // 128
        for sl, name, op in ((trx_sl, mxname, ALU.max), (trn_sl, mnname, ALU.min)):
            def ftr(tensor, sl=sl):
                ps = self.bank(bank, tensor)
                for sm, cnt in waits:
                    tensor.wait_ge(sm.h, cnt)
                tensor.wait_ge(self.c_ve.h, self.c_ve.n)
                mm = None
                for b in range(nblk):
                    bw = min(128, w - b * 128)
                    mm = tensor.matmul(ps[0:bw, b * 128:b * 128 + 128],
                                       self.AR[:, sl + b * 128:sl + b * 128 + bw],
                                       self.EYE[:], is_transpose=True)
                self.c_pe.inc(mm)
            self.sec("tensor", ftr)
            pem = self.c_pe.n

            def frd(vector, name=name, op=op):
                vector.wait_ge(self.c_pe.h, pem)
                dst = self.tm(name, nblk)
                for b in range(nblk):
                    bw = min(128, w - b * 128)
                    vector.tensor_reduce(dst[0:bw, b:b + 1],
                                         self.PS[bank][0:bw, b * 128:b * 128 + 128],
                                         AX.X, op)
                vector.drain()
                self.c_ve.inc(self.vfence(vector))
            self.sec("vector", frd)
            self.set_guard(bank, self.c_ve)

    def emit_ffn(self, uq_off, uq_stride, hv, w, cl_name, ydst_sl, ysrc_sl, ystride):
        BC = self.SL_BC
        TRXa, TRXb = BC + 1024, BC + 1280
        TRNa, TRNb = BC + 1536, BC + 1792
        GU = self.SL_GU
        nblk = (w + 127) // 128
        cl = self.tm(cl_name + "_cl")[:, 2 * hv:2 * hv + nblk]

        rg = self.tm("ffn_rg", nblk)
        ru = self.tm("ffn_ru", nblk)

        def f(vector):
            vector.wait_ge(self.c_ac.h, self.c_ac.n)
            vector.tensor_scalar(rg, cl, self.CONST[:, 2:3], None, ALU.mult)
            vector.tensor_scalar(ru, cl, self.CONST[:, 3:4], None, ALU.mult)
            vector.drain()
            self.c_ve.inc(self.vfence(vector))
        self.sec("vector", f)
        self.emit_bcasts([(rg, 4), (ru, 5)], w, waits=[(self.c_ve, self.c_ve.n)])
        pem = self.c_pe.n

        def f(vector):
            vector.wait_ge(self.c_pe.h, pem)
            vector.tensor_copy(self.AR[:, BC:BC + w], self.PS[4][:, 0:w])
            self.c_ve.inc(vector.tensor_copy(self.AR[:, BC + 256:BC + 256 + w],
                                             self.PS[5][:, 0:w]))
        self.sec("vector", f)
        self.set_guard(4, self.c_ve)
        self.set_guard(5, self.c_ve)

        ldm = {}
        pemk = {}
        vemk = {}
        gcmk = {}
        acmk = {}
        wp0 = self.wp_pe
        ve_uq = self.c_ve.n
        for fo2 in range(16):
            wpg = self.BL_WP + (fo2 % 2) * 2048
            wpu = self.BL_WP + 4096 + (fo2 % 2) * 2048

            def f(sync, fo2=fo2, wpg=wpg, wpu=wpu):
                if fo2 < 2:
                    sync.wait_ge(self.c_pe.h, wp0)
                if fo2 >= 2:
                    sync.wait_ge(self.c_pe.h, pemk[(fo2 - 2, 1)])
                blk = self.ds["ws_wgq"][:, fo2 * 2048:(fo2 + 1) * 2048]
                self.s_sy.inc(sync.dma_start(out=self.AB[:, wpg:wpg + 2048], in_=blk), 16)
                blk = self.ds["ws_wuq"][:, fo2 * 2048:(fo2 + 1) * 2048]
                self.s_sy.inc(sync.dma_start(out=self.AB[:, wpu:wpu + 2048], in_=blk), 16)
            self.sec("sync", f)
            ldm[fo2] = self.s_sy.n

            MI = self.SL_MISC

            def mmx(vector, fo):
                guc = self.AR[:, GU + fo * w:GU + fo * w + w]
                xd = (TRXa, TRXb)[fo % 2]
                xs = (TRXa, TRXb)[(fo - 1) % 2]
                nd = (TRNa, TRNb)[fo % 2]
                ns = (TRNa, TRNb)[(fo - 1) % 2]
                if fo == 0:
                    vector.tensor_copy(self.AR[:, xd:xd + w], guc)
                    return vector.tensor_copy(self.AR[:, nd:nd + w], guc)
                vector.tensor_tensor(self.AR[:, xd:xd + w],
                                     self.AR[:, xs:xs + w], guc, ALU.max)
                return vector.tensor_tensor(self.AR[:, nd:nd + w],
                                            self.AR[:, ns:ns + w], guc, ALU.min)

            for fop in range(2):
                fo = fo2 * 2 + fop
                bg, bu = fo % 4, 4 + fo % 4
                j = fo % 2

                def f(tensor, fo2=fo2, fop=fop, fo=fo, bg=bg, bu=bu, wpg=wpg, wpu=wpu):
                    psg = self.bank(bg, tensor)
                    psu = self.bank(bu, tensor)
                    tensor.wait_ge(self.s_sy.h, ldm[fo2])
                    if fo == 0:
                        tensor.wait_ge(self.c_ve.h, ve_uq)
                    for fi in range(8):
                        rhs = self.AB[:, uq_off + fi * uq_stride + hv * w:
                                      uq_off + fi * uq_stride + hv * w + w]
                        tensor.matmul(psg[:, 0:w],
                                      self.AB[:, wpg + fi * 256 + fop * 128:
                                              wpg + fi * 256 + fop * 128 + 128],
                                      rhs, start=(fi == 0), stop=(fi == 7))
                    mm = None
                    for fi in range(8):
                        rhs = self.AB[:, uq_off + fi * uq_stride + hv * w:
                                      uq_off + fi * uq_stride + hv * w + w]
                        mm = tensor.matmul(psu[:, 0:w],
                                           self.AB[:, wpu + fi * 256 + fop * 128:
                                                   wpu + fi * 256 + fop * 128 + 128],
                                           rhs, start=(fi == 0), stop=(fi == 7))
                    self.c_pe.inc(mm)
                self.sec("tensor", f)
                pemk[(fo2, fop)] = self.c_pe.n

                def f(vector, fo=fo, bg=bg, bu=bu):
                    jq = fo % 4
                    vector.wait_ge(self.c_pe.h, pemk[(fo // 2, fo % 2)])
                    if fo >= 4:
                        vector.wait_ge(self.c_gc.h, gcmk[fo - 4])
                    if jq < 2:
                        usc = self.T0[:, jq * w:jq * w + w]
                    else:
                        usc = self.AR[:, BC + 512 + (jq - 2) * w:
                                      BC + 512 + (jq - 2) * w + w]
                    vector.tensor_tensor(self.AR[:, MI + jq * w:MI + jq * w + w],
                                         self.PS[bg][:, 0:w],
                                         self.AR[:, BC:BC + w], ALU.mult)
                    last = vector.tensor_tensor(usc, self.PS[bu][:, 0:w],
                                                self.AR[:, BC + 256:BC + 256 + w],
                                                ALU.mult)
                    if fo >= 4:
                        last = mmx(vector, fo - 4)
                    self.c_ve.inc(last)
                self.sec("vector", f)
                vemk[fo] = self.c_ve.n
                self.set_guard(bg, self.c_ve)
                self.set_guard(bu, self.c_ve)

                if fo % 4 == 3:
                    def f(scalar, fo=fo):
                        scalar.wait_ge(self.c_ve.h, vemk[fo])
                        if fo >= 7:
                            scalar.wait_ge(self.c_gc.h, gcmk[fo - 4])
                        self.c_ac.inc(scalar.activation(
                            self.AR[:, self.SL_NORM:self.SL_NORM + 4 * w],
                            self.AR[:, MI:MI + 4 * w], AF.Silu))
                    self.sec("scalar", f)
                    for k in range(4):
                        acmk[fo - 3 + k] = self.c_ac.n

                    def f(gpsimd, fo=fo, acm=self.c_ac.n):
                        gpsimd.wait_ge(self.c_ac.h, acm)
                        last = None
                        for k in range(4):
                            fk = fo - 3 + k
                            jk = fk % 4
                            if jk < 2:
                                usc = self.T0[:, jk * w:jk * w + w]
                            else:
                                usc = self.AR[:, BC + 512 + (jk - 2) * w:
                                              BC + 512 + (jk - 2) * w + w]
                            last = gpsimd.tensor_tensor(
                                self.AR[:, GU + fk * w:GU + fk * w + w],
                                self.AR[:, self.SL_NORM + jk * w:
                                        self.SL_NORM + jk * w + w],
                                usc, ALU.mult)
                        self.c_gc.inc(last)
                    self.sec("gpsimd", f)
                    for k in range(4):
                        gcmk[fo - 3 + k] = self.c_gc.n

        self.wp_pe = pemk[(15, 1)]

        def f(vector):
            vector.wait_ge(self.c_gc.h, self.c_gc.n)
            for k in (28, 29, 30, 31):
                mmx(vector, k)
            self.c_ve.inc(self.vfence(vector))
        self.sec("vector", f)
        self.emit_minmax_rows((TRXa, TRXb)[31 % 2], (TRNa, TRNb)[31 % 2], w, 6,
                              "ffn_gmx", "ffn_gmn")
        gmx = self.tm("ffn_gmx", nblk)
        gmn = self.tm("ffn_gmn", nblk)
        t1 = self.tm("ffn_t1", nblk)
        cl2 = self.tm("ffn_cl2", nblk)
        s2 = self.tm("ffn_s2", nblk)
        rd = self.tm("ffn_rd", nblk)

        t2f = self.tm("ffn_t2", nblk)

        def f(vector):
            vector.wait_ge(self.c_ve.h, self.c_ve.n)
            vector.tensor_scalar(t1, gmn, -1.0, None, ALU.mult)
            vector.drain()
            vector.tensor_tensor(t2f, gmx, t1, ALU.max)
            vector.drain()
            vector.tensor_scalar(cl2, t2f, 1e-5, None, ALU.max)
            vector.drain()
            vector.reciprocal(t1, cl2)
            vector.drain()
            vector.tensor_scalar(s2, t1, 127.0, None, ALU.mult)
            vector.tensor_scalar(rd, cl2, self.CONST[:, 4:5], None, ALU.mult)
            vector.drain()
            self.c_ve.inc(self.vfence(vector))
        self.sec("vector", f)
        self.emit_bcasts([(s2, 4), (rd, 5)], w, waits=[(self.c_ve, self.c_ve.n)])
        pem = self.c_pe.n

        def f(vector):
            vector.wait_ge(self.c_pe.h, pem)
            vector.tensor_copy(self.AR[:, BC + 512:BC + 512 + w], self.PS[4][:, 0:w])
            self.c_ve.inc(vector.tensor_copy(self.AR[:, BC + 768:BC + 768 + w],
                                             self.PS[5][:, 0:w]))
        self.sec("vector", f)
        self.set_guard(4, self.c_ve)
        self.set_guard(5, self.c_ve)

        dldm = {}
        dvem = {}
        dpem = {}
        wp0d = self.wp_pe
        for fi in range(32):
            wpd = self.BL_WP + (fi % 2) * 2048

            def f(sync, fi=fi, wpd=wpd):
                if fi < 2:
                    sync.wait_ge(self.c_pe.h, wp0d)
                if fi >= 2:
                    sync.wait_ge(self.c_pe.h, dpem[fi - 2])
                blk = self.ds["ws_wdq"][:, fi * 1024:(fi + 1) * 1024]
                self.s_sy.inc(sync.dma_start(out=self.AB[:, wpd:wpd + 1024], in_=blk), 16)
            self.sec("sync", f)
            dldm[fi] = self.s_sy.n

            def f(vector, fi=fi):
                if fi >= 4:
                    vector.wait_ge(self.c_pe.h, dpem[fi - 4])
                dr = w < 128
                guc = self.AR[:, GU + fi * w:GU + fi * w + w]
                vector.tensor_tensor(self.T0[:, 0:w], guc,
                                     self.AR[:, BC + 512:BC + 512 + w], ALU.mult)
                if dr:
                    vector.drain()
                th4 = self.TH if (fi // 2) % 2 == 0 else self.TH2
                tb = th4[:, (fi % 2) * 256:(fi % 2) * 256 + w]
                vector.tensor_scalar(tb, self.T0[:, 0:w], ROFF, None, ALU.add)
                if dr:
                    vector.drain()
                b3 = self.AB[:, self.BL_T + (fi % 4) * 256:self.BL_T + (fi % 4) * 256 + w]
                self.c_ve.inc(vector.tensor_scalar(b3, tb, -ROFF, 127.0,
                                                   ALU.add, ALU.min))
            self.sec("vector", f)
            dvem[fi] = self.c_ve.n

            def f(tensor, fi=fi, wpd=wpd):
                if fi == 0:
                    for b in range(8):
                        self.bank(b, tensor)
                tensor.wait_ge(self.s_sy.h, dldm[fi])
                tensor.wait_ge(self.c_ve.h, dvem[fi])
                b3 = self.AB[:, self.BL_T + (fi % 4) * 256:self.BL_T + (fi % 4) * 256 + w]
                mm = None
                for fo in range(8):
                    ps = self.PS[fo]
                    mm = tensor.matmul(ps[:, 0:w],
                                       self.AB[:, wpd + fo * 128:wpd + (fo + 1) * 128],
                                       b3, start=(fi == 0), stop=(fi == 31))
                self.c_pe.inc(mm)
            self.sec("tensor", f)
            dpem[fi] = self.c_pe.n

        self.wp_pe = dpem[31]

        def f(vector):
            vector.wait_ge(self.c_pe.h, self.c_pe.n)
            dr = w < 128
            last = None
            for fo in range(8):
                ps = self.PS[fo][:, 0:w]
                vector.tensor_tensor(self.T0[:, 0:w], ps,
                                     self.AR[:, BC + 768:BC + 768 + w], ALU.mult)
                if dr:
                    vector.drain()
                ysrc = self.AR[:, ysrc_sl + fo * ystride + hv * w:
                               ysrc_sl + fo * ystride + hv * w + w]
                ydst = self.AR[:, ydst_sl + fo * ystride + hv * w:
                               ydst_sl + fo * ystride + hv * w + w]
                last = vector.tensor_tensor(ydst, self.T0[:, 0:w], ysrc, ALU.add)
            self.c_ve.inc(last)
        self.sec("vector", f)
        for b in range(8):
            self.set_guard(b, self.c_ve)

    def emit_proj(self, hname, lname, rhsh_off, rhsl_off, w, out_cb, waits=()):
        ldm = {}
        pemk = {}
        wp0 = self.wp_pe
        for fo2 in range(4):
            wph = self.BL_WP + (fo2 % 2) * 2048
            wpl = self.BL_WP + 4096 + (fo2 % 2) * 2048

            def f(sync, fo2=fo2, wph=wph, wpl=wpl):
                if fo2 < 2:
                    sync.wait_ge(self.c_pe.h, wp0)
                else:
                    sync.wait_ge(self.c_pe.h, pemk[(fo2 - 2, 1)])
                blk = self.ds[hname][:, fo2 * 2048:(fo2 + 1) * 2048]
                self.s_sy.inc(sync.dma_start(out=self.AB[:, wph:wph + 2048], in_=blk), 16)
                blk = self.ds[lname][:, fo2 * 2048:(fo2 + 1) * 2048]
                self.s_sy.inc(sync.dma_start(out=self.AB[:, wpl:wpl + 2048], in_=blk), 16)
            self.sec("sync", f)
            ldm[fo2] = self.s_sy.n

            for fop in range(2):
                fo = fo2 * 2 + fop
                bk = fo % 2

                def f(tensor, fo2=fo2, fop=fop, bk=bk, wph=wph, wpl=wpl):
                    ps = self.bank(bk, tensor)
                    tensor.wait_ge(self.s_sy.h, ldm[fo2])
                    if fo2 == 0 and fop == 0:
                        for sm, cnt in waits:
                            tensor.wait_ge(sm.h, cnt)
                    mm = None
                    for fi in range(8):
                        xh = self.AB[:, rhsh_off + fi * w:rhsh_off + fi * w + w]
                        xl = self.AB[:, rhsl_off + fi * w:rhsl_off + fi * w + w]
                        wh = self.AB[:, wph + fi * 256 + fop * 128:
                                     wph + fi * 256 + fop * 128 + 128]
                        wl = self.AB[:, wpl + fi * 256 + fop * 128:
                                     wpl + fi * 256 + fop * 128 + 128]
                        tensor.matmul(ps[:, 0:w], wh, xh, start=(fi == 0), stop=False)
                        tensor.matmul(ps[:, 0:w], wl, xh, start=False, stop=False)
                        mm = tensor.matmul(ps[:, 0:w], wh, xl, start=False,
                                           stop=(fi == 7))
                    self.c_pe.inc(mm)
                self.sec("tensor", f)
                pemk[(fo2, fop)] = self.c_pe.n
                out_cb(fo, bk, self.c_pe.n)
        self.wp_pe = pemk[(3, 1)]

    def emit_ln_apply(self, src_sl, dst_sl, w, rw, gcol=None, bcol=None, nch=8,
                      extra_cb=None):
        nblk = (w + 127) // 128
        self.emit_bcasts([(self.tm(rw + "_rs", nblk), 2),
                          (self.tm(rw + "_mrs", nblk), 3)], w,
                         waits=[(self.c_ve, self.c_ve.n)])
        pem = self.c_pe.n

        def f(vector):
            vector.wait_ge(self.c_pe.h, pem)
            dr = w < 128
            last = None
            for c in range(nch):
                s = self.AR[:, src_sl + c * w:src_sl + c * w + w]
                d = self.AR[:, dst_sl + c * w:dst_sl + c * w + w]
                vector.tensor_tensor(self.T0[:, 0:w], s, self.PS[2][:, 0:w], ALU.mult)
                if dr:
                    vector.drain()
                if gcol is None:
                    last = vector.tensor_tensor(d, self.T0[:, 0:w], self.PS[3][:, 0:w],
                                                ALU.subtract)
                else:
                    vector.tensor_tensor(self.T1[:, 0:w], self.T0[:, 0:w],
                                         self.PS[3][:, 0:w], ALU.subtract)
                    if dr:
                        vector.drain()
                    last = vector.tensor_scalar(d, self.T1[:, 0:w],
                                                gcol[:, c:c + 1], bcol[:, c:c + 1],
                                                ALU.mult, ALU.add)
                if extra_cb is not None:
                    extra_cb(vector, c, d)
            self.c_ve.inc(last)
        self.sec("vector", f)
        self.set_guard(2, self.c_ve)
        self.set_guard(3, self.c_ve)
        return self.c_ve.n

    def emit_split(self, src_sl, hi_off, lo_off, w, nch=8):
        vem = []

        def f(vector):
            for c in range(nch):
                s = self.AR[:, src_sl + c * w:src_sl + c * w + w]
                hb = self.AB[:, hi_off + c * w:hi_off + c * w + w]
                self.c_ve.inc(vector.tensor_copy(hb, s))
                vem.append(self.c_ve.n)
        self.sec("vector", f)

        def f(gpsimd):
            last = None
            for c in range(nch):
                gpsimd.wait_ge(self.c_ve.h, vem[c])
                s = self.AR[:, src_sl + c * w:src_sl + c * w + w]
                hb = self.AB[:, hi_off + c * w:hi_off + c * w + w]
                lb = self.AB[:, lo_off + c * w:lo_off + c * w + w]
                last = gpsimd.tensor_sub(lb, s, hb)
            self.c_gc.inc(last)
        self.sec("gpsimd", f)
        return self.c_gc.n

    def emit_store(self, slot, sname, c0, w, waits=()):
        def f(gpsimd):
            for sm, cnt in waits:
                gpsimd.wait_ge(sm.h, cnt)
            src = self.AR[:, slot:slot + 8 * w].rearrange("p (c t) -> p c t", t=w)
            dst = self.fm(self.ds[sname], c0, w)
            self.s_st.inc(gpsimd.dma_start(out=dst, in_=src), 16)
        self.sec("gpsimd", f)
        return self.s_st.n

    def emit_load(self, sname_or_ap, slot, c0, w, eng="gpsimd", is_input=False,
                  waits=()):
        sem = {"gpsimd": self.s_gp, "sync": self.s_sy, "scalar": self.s_sc}[eng]

        def f(e):
            for sm, cnt in waits:
                e.wait_ge(sm.h, cnt)
            dram = self.di[sname_or_ap] if is_input else self.ds[sname_or_ap]
            src = self.fm(dram, c0, w)
            dst = self.AR[:, slot:slot + 8 * w].rearrange("p (c t) -> p c t", t=w)
            sem.inc(e.dma_start(out=dst, in_=src), 16)
        self.sec(eng, f)
        return sem.n

    def emit_sweep1(self):
        tiles = range(NT + 1) if _S1_TILES is None else _S1_TILES
        for t in tiles:
            mini = (t == NT)
            w = HALO if mini else TT
            c0 = NT * TT if mini else t * TT
            self.emit_s1_tile(t, c0, w, mini)

            def f(vector):
                vector.wait_ge(self.s_st.h, self.s_st.n)
                vector.wait_ge(self.s_gp.h, self.s_gp.n)
            self.sec("vector", f)
        self.barrier()

    def emit_s1_tile(self, t, c0, w, mini):
        SX, SY, SV = self.SL_X, self.SL_Y1, self.SL_V
        RAW, NORM, GU = self.SL_RAW, self.SL_NORM, self.SL_GU
        B = self.BIAS

        ldx = self.emit_load("x", SX, c0, w, eng="scalar", is_input=True,
                             waits=[(self.c_ve, self.c_ve.n),
                                    (self.c_pe, self.c_pe.n)])

        self.emit_sums(SX, w, GU, 6, waits=[(self.s_sc, ldx)])
        self.emit_minmax(SX, w, 7, "ln1_mx", "ln1_mn", waits=[(self.s_sc, ldx)])
        self.emit_ln_stats(6, w, "ln1", minmax=True)
        self.emit_quant(SX, w, self.BL_UQ, "ln1", 2, 3)

        if mini:
            self.emit_ffn(self.BL_UQ, w, 0, w, "ln1", SY, SX, w)
        else:
            for hv in range(2):
                self.emit_ffn(self.BL_UQ, w, hv, HT, "ln1", SY, SX, w)
        if not mini:
            st_y1 = self.emit_store(SY, "s_y1", c0, w, waits=[(self.c_ve, self.c_ve.n)])
        if _S1_FFN_ONLY:
            return

        self.emit_sums(SY, w, GU, 6, waits=[(self.c_ve, self.c_ve.n)])
        self.emit_ln_stats(6, w, "lnv", minmax=False)
        self.emit_ln_apply(SY, SV, w, "lnv")
        if not mini:
            st_v = self.emit_store(SV, "s_v", c0, w, waits=[(self.c_ve, self.c_ve.n)])
        self.emit_split(SV, self.BL_VH, self.BL_VL, w)

        def mkevac(dst_sl, bias_c):
            def cb(fo, bk, pem):
                def f(vector):
                    vector.wait_ge(self.c_pe.h, pem)
                    vector.wait_ge(self.c_gc.h, self.c_gc.n)
                    d = self.AR[:, dst_sl + fo * w:dst_sl + fo * w + w]
                    self.c_ve.inc(vector.tensor_scalar(
                        d, self.PS[bk][:, 0:w], B[:, bias_c + fo:bias_c + fo + 1],
                        None, ALU.add))
                self.sec("vector", f)
                self.set_guard(bk, self.c_ve)
            return cb

        self.emit_proj("wsh_q", "wsl_q", self.BL_VH, self.BL_VL, w, mkevac(RAW, 0),
                       waits=[(self.c_ve, self.c_ve.n), (self.c_gc, self.c_gc.n)])
        self.emit_sums(RAW, w, GU, 6, waits=[(self.c_ve, self.c_ve.n)])
        self.emit_ln_stats(6, w, "lnq", minmax=False)
        self.emit_ln_apply(RAW, NORM, w, "lnq", gcol=B[:, 40:48], bcol=B[:, 48:56])
        if not mini:
            def f(vector, t=t):
                srch = 8 * (t % 2)
                dsth = 8 * ((t + 1) % 2)
                for c in range(8):
                    s = self.AR[:, NORM + c * w:NORM + c * w + w]
                    vector.reduce_sum(self.T0[:, 0:1], s, AX.X)
                    vector.drain()
                    vector.tensor_tensor(self.QG[:, dsth + c:dsth + c + 1],
                                         self.QG[:, srch + c:srch + c + 1],
                                         self.T0[:, 0:1], ALU.add)
                vector.drain()
                self.c_ve.inc(self.vfence(vector))
            self.sec("vector", f)

        if not mini:
            self.emit_proj("wsh_k", "wsl_k", self.BL_VH, self.BL_VL, w, mkevac(RAW, 8))
            self.emit_sums(RAW, w, GU, 6, waits=[(self.c_ve, self.c_ve.n)])
            self.emit_ln_stats(6, w, "lnk", minmax=False)
            self.emit_ln_apply(RAW, GU + 4096, w, "lnk", gcol=B[:, 56:64], bcol=B[:, 64:72])
            st_k = self.emit_store(GU + 4096, "s_kk", c0, w, waits=[(self.c_ve, self.c_ve.n)])

            self.emit_proj("wsh_v", "wsl_v", self.BL_VH, self.BL_VL, w, mkevac(RAW, 16))
            self.emit_sums(RAW, w, GU, 6, waits=[(self.c_ve, self.c_ve.n)])
            self.emit_ln_stats(6, w, "lnu", minmax=False)
            self.emit_ln_apply(RAW, RAW, w, "lnu", gcol=B[:, 72:80], bcol=B[:, 80:88])
            st_vv = self.emit_store(RAW, "s_vv", c0, w, waits=[(self.c_ve, self.c_ve.n)])

        self.emit_split(NORM, self.BL_VH, self.BL_VL, w)
        stv = self.s_st.n

        def mkphi(act_sl, dst_sl, bias_c):
            def cb(fo, bk, pem):
                def f(scalar):
                    scalar.wait_ge(self.c_pe.h, pem)
                    scalar.wait_ge(self.s_st.h, stv)
                    a_ = self.AR[:, act_sl + fo * w:act_sl + fo * w + w]
                    self.c_ac.inc(scalar.activation(
                        a_, self.PS[bk][:, 0:w], AF.Tanh,
                        bias=B[:, bias_c + fo:bias_c + fo + 1], scale=1.0))
                self.sec("scalar", f)
                acm = self.c_ac.n

                def f(vector, acm=acm):
                    vector.wait_ge(self.c_ac.h, acm)
                    a_ = self.AR[:, act_sl + fo * w:act_sl + fo * w + w]
                    d = self.AR[:, dst_sl + fo * w:dst_sl + fo * w + w]
                    self.c_ve.inc(vector.tensor_scalar(d, a_, 1.0, None, ALU.add))
                self.sec("vector", f)
                self.set_guard(bk, self.c_ve)
            return cb

        self.emit_proj("wsh_pq", "wsl_pq", self.BL_VH, self.BL_VL, w,
                       mkphi(NORM, RAW, 24),
                       waits=[(self.c_ve, self.c_ve.n), (self.c_gc, self.c_gc.n)])
        self.emit_store(RAW, "s_pq", c0, w, waits=[(self.c_ve, self.c_ve.n)])

        if not mini:
            self.emit_split(GU + 4096, self.BL_VH, self.BL_VL, w)
            self.emit_proj("wsh_pk", "wsl_pk", self.BL_VH, self.BL_VL, w,
                           mkphi(GU + 4096, NORM, 32),
                           waits=[(self.c_ve, self.c_ve.n), (self.c_gc, self.c_gc.n)])
            self.emit_store(NORM, "s_fk", c0, w, waits=[(self.c_ve, self.c_ve.n)])

    def emit_qg_collective(self):
        groups = [[0, 1, 2, 3], [4, 5, 6, 7]]

        def f(gpsimd):
            self.s_st.inc(gpsimd.dma_start(out=self.ds["cb_qg_i"][:],
                                           in_=self.QG[:, 0:8]), 16)
            gpsimd.wait_ge(self.s_st.h, self.s_st.n)
            self.c_cc.inc(gpsimd.collective_compute(
                "AllReduce", ALU.add, replica_groups=groups,
                ins=[self.ds["cb_qg_i"][:]], outs=[self.ds["cb_qg_o"][:]]))
            gpsimd.wait_ge(self.c_cc.h, self.c_cc.n)
            self.s_gp.inc(gpsimd.dma_start(out=self.QG[:, 0:8],
                                           in_=self.ds["cb_qg_o"][:]), 16)
        self.sec("gpsimd", f)
        gpm = self.s_gp.n

        def f(vector):
            vector.wait_ge(self.s_gp.h, gpm)
            vector.tensor_scalar(self.QG[:, 8:16], self.QG[:, 0:8],
                                 1.0 / (LQ * 8.0), None, ALU.mult)
            vector.drain()
            self.c_ve.inc(self.vfence(vector))
        self.sec("vector", f)
        self.barrier()

    def kvcol(self, s):
        return (0 if s < 4 else 260) + 65 * (s % 4)

    def emit_scores_all(self, ldmark, first, dst_off):
        for c in range(8):
            bk = c % 2

            def f(tensor, c=c, bk=bk):
                ps = self.bank(bk, tensor)
                tensor.wait_ge(self.s_sy.h, ldmark)
                if first and c == 0:
                    tensor.wait_ge(self.c_ve.h, self.c_ve.n)
                self.c_pe.inc(tensor.matmul(
                    ps[0:2, :], self.QGE[:, 2 * c:2 * c + 2],
                    self.AR[:, self.SL_X + 512 * c:self.SL_X + 512 * c + 512],
                    start=True, stop=True))
            self.sec("tensor", f)
            pem = self.c_pe.n

            def f(vector, c=c, bk=bk, pem=pem):
                vector.wait_ge(self.c_pe.h, pem)
                self.c_ve.inc(vector.tensor_copy(
                    self.ESB[0:2, dst_off + 512 * c:dst_off + 512 * c + 512],
                    self.PS[bk][0:2, :]))
            self.sec("vector", f)
            self.set_guard(bk, self.c_ve)
        return self.c_ve.n

    def emit_sweep2(self):
        def f(vector):
            vector.memset(self.QGE[:], 0.0)
            vector.drain()
            for c in range(8):
                vector.tensor_copy(self.QGE[0:64, 2 * c:2 * c + 1],
                                   self.QG[0:64, 8 + c:9 + c])
                vector.tensor_copy(self.QGE[64:128, 2 * c + 1:2 * c + 2],
                                   self.QG[64:128, 8 + c:9 + c])
            vector.drain()
            self.c_ve.inc(self.vfence(vector))
        self.sec("vector", f)

        for t in range(NT):
            ld = self.emit_load("s_kk", self.SL_X, t * TT, TT, eng="sync")
            vem = self.emit_scores_all(ld, t == 0, 0)

            def f(vector, t=t, vem=vem):
                vector.wait_ge(self.c_ve.h, vem)
                srch = (0, 32)[t % 2]
                dsth = (0, 32)[(t + 1) % 2]
                for c in range(8):
                    vector.reduce_max(self.MX[:, 24:25],
                                      self.ESB[0:2, 512 * c:512 * c + 512], AX.X)
                    vector.drain()
                    if t == 0:
                        vector.tensor_copy(self.MX[:, dsth + c:dsth + c + 1],
                                           self.MX[:, 24:25])
                    else:
                        vector.tensor_tensor(self.MX[:, dsth + c:dsth + c + 1],
                                             self.MX[:, srch + c:srch + c + 1],
                                             self.MX[:, 24:25], ALU.max)
                vector.drain()
                self.c_ve.inc(self.vfence(vector))
            self.sec("vector", f)
            self.barrier(("sync", "vector", "tensor"))

        def f(vector):
            vector.tensor_scalar(self.MX[:, 8:16], self.MX[:, 0:8], -1.0, None, ALU.mult)
            vector.memset(self.MX[:, 16:24], 0.0)
            vector.memset(self.MX[:, 40:48], 0.0)
            vector.drain()
            self.c_ve.inc(self.vfence(vector))
        self.sec("vector", f)
        self.barrier()

        VTM = self.SL_NORM
        PKT = self.SL_NORM + 1040

        def f(vector):
            last = None
            for h in range(16):
                last = vector.memset(self.AR[:, VTM + 65 * h + 64:VTM + 65 * h + 65], 1.0)
            self.c_ve.inc(last)
        self.sec("vector", f)

        KVA = self.SL_RAW
        KVB = self.SL_RAW + 520

        def f(vector):
            self.c_ve.inc(vector.memset(self.AR[:, KVA:KVA + 520], 0.0))
        self.sec("vector", f)

        for t in range(NT):
            c0 = t * TT
            ldk = self.emit_load("s_kk", self.SL_X, c0, TT, eng="sync")
            ldf = self.emit_load("s_fk", self.SL_Y1, c0, TT, eng="scalar")
            ldv = self.emit_load("s_vv", self.SL_V, c0, TT, eng="gpsimd")
            vem_s = self.emit_scores_all(ldk, False, 0)

            def f(scalar, vem_s=vem_s):
                scalar.wait_ge(self.c_ve.h, vem_s)
                last = None
                for c in range(8):
                    sl = self.ESB[0:2, 512 * c:512 * c + 512]
                    last = scalar.activation(sl, sl, AF.Exp,
                                             bias=self.MX[:, 8 + c:9 + c], scale=1.0)
                self.c_ac.inc(last)
            self.sec("scalar", f)
            acm = self.c_ac.n

            def f(vector, acm=acm, t=t):
                vector.wait_ge(self.c_ac.h, acm)
                srch = (16, 40)[t % 2]
                dsth = (16, 40)[(t + 1) % 2]
                for c in range(8):
                    vector.reduce_sum(self.MX[:, 24:25],
                                      self.ESB[0:2, 512 * c:512 * c + 512], AX.X)
                    vector.drain()
                    vector.tensor_tensor(self.MX[:, dsth + c:dsth + c + 1],
                                         self.MX[:, srch + c:srch + c + 1],
                                         self.MX[:, 24:25], ALU.add)
                vector.drain()
                self.c_ve.inc(self.vfence(vector))
            self.sec("vector", f)
            vem_e = self.c_ve.n

            def f(tensor, vem_e=vem_e):
                ps = self.bank(1, tensor)
                tensor.wait_ge(self.c_ve.h, vem_e)
                mm = None
                for j in range(4):
                    for c in range(8):
                        mm = tensor.matmul(
                            ps[:, j * 16 + 2 * c:j * 16 + 2 * c + 2],
                            self.ESB[0:2, 512 * c + 128 * j:512 * c + 128 * j + 128],
                            self.EYE[0:2, 0:2], is_transpose=True)
                self.c_pe.inc(mm)
            self.sec("tensor", f)
            pem_t = self.c_pe.n

            def f(vector, pem_t=pem_t):
                vector.wait_ge(self.c_pe.h, pem_t)
                vector.tensor_copy(self.ETM[:], self.PS[1][:, 0:64])
                vector.drain()
                self.c_ve.inc(self.vfence(vector))
            self.sec("vector", f)
            self.set_guard(1, self.c_ve)

            for j in range(4):
                def f(tensor, j=j, ldf=ldf):
                    ps2 = self.bank(2, tensor)
                    ps3 = self.bank(3, tensor)
                    tensor.wait_ge(self.s_sc.h, ldf)
                    mm = None
                    for c in range(8):
                        ps = ps2 if c < 4 else ps3
                        mm = tensor.matmul(
                            ps[:, (c % 4) * 128:(c % 4) * 128 + 128],
                            self.AR[:, self.SL_Y1 + 512 * c + 128 * j:
                                    self.SL_Y1 + 512 * c + 128 * j + 128],
                            self.EYE[:], is_transpose=True)
                    self.c_pe.inc(mm)
                self.sec("tensor", f)
                pem_k = self.c_pe.n

                def f(vector, j=j, pem_k=pem_k):
                    vector.wait_ge(self.c_pe.h, pem_k)
                    last = None
                    for h in range(16):
                        c = h // 2
                        ps = self.PS[2 + c // 4]
                        sl = ps[:, 128 * (c % 4) + 64 * (h % 2):
                                128 * (c % 4) + 64 * (h % 2) + 64]
                        last = vector.tensor_scalar(
                            self.AR[:, PKT + 64 * h:PKT + 64 * h + 64], sl,
                            self.ETM[:, 16 * j + h:16 * j + h + 1], None, ALU.mult)
                    self.c_ve.inc(last)
                self.sec("vector", f)
                self.set_guard(2, self.c_ve)
                self.set_guard(3, self.c_ve)
                vem_pk = self.c_ve.n

                def f(tensor, j=j, ldv=ldv, vem_pk=vem_pk):
                    ps2 = self.bank(2, tensor)
                    ps3 = self.bank(3, tensor)
                    tensor.wait_ge(self.s_gp.h, ldv)
                    mm = None
                    for c in range(8):
                        ps = ps2 if c < 4 else ps3
                        mm = tensor.matmul(
                            ps[:, (c % 4) * 128:(c % 4) * 128 + 128],
                            self.AR[:, self.SL_V + 512 * c + 128 * j:
                                    self.SL_V + 512 * c + 128 * j + 128],
                            self.EYE[:], is_transpose=True)
                    self.c_pe.inc(mm)
                self.sec("tensor", f)
                pem_v = self.c_pe.n

                def f(vector, pem_v=pem_v):
                    vector.wait_ge(self.c_pe.h, pem_v)
                    last = None
                    for h in range(16):
                        c = h // 2
                        ps = self.PS[2 + c // 4]
                        sl = ps[:, 128 * (c % 4) + 64 * (h % 2):
                                128 * (c % 4) + 64 * (h % 2) + 64]
                        last = vector.tensor_copy(
                            self.AR[:, VTM + 65 * h:VTM + 65 * h + 64], sl)
                    self.c_ve.inc(last)
                self.sec("vector", f)
                self.set_guard(2, self.c_ve)
                self.set_guard(3, self.c_ve)
                vem_v = self.c_ve.n

                idx = t * 4 + j
                src_kv = (KVA, KVB)[idx % 2]
                dst_kv = (KVA, KVB)[(idx + 1) % 2]
                for rnd in range(2):
                    def f(tensor, j=j, vem_v=vem_v, rnd=rnd):
                        mm = None
                        for hh in range(8):
                            h = rnd * 8 + hh
                            pb = 64 * (h % 2)
                            ps = self.bank(hh, tensor)
                            if hh == 0 and rnd == 0:
                                tensor.wait_ge(self.c_ve.h, vem_v)
                            mm = tensor.matmul(
                                ps[pb:pb + 64, 0:65],
                                self.AR[:, PKT + 64 * h:PKT + 64 * h + 64],
                                self.AR[:, VTM + 65 * h:VTM + 65 * h + 65],
                                start=True, stop=True)
                        self.c_pe.inc(mm)
                    self.sec("tensor", f)
                    pem_kv = self.c_pe.n

                    def f(vector, rnd=rnd, src_kv=src_kv, dst_kv=dst_kv,
                          pem_kv=pem_kv):
                        vector.wait_ge(self.c_pe.h, pem_kv)
                        last = None
                        for hh in range(8):
                            h = rnd * 8 + hh
                            pb = 64 * (h % 2)
                            a_src = self.AR[pb:pb + 64,
                                            src_kv + self.kvcol(h // 2):
                                            src_kv + self.kvcol(h // 2) + 65]
                            a_dst = self.AR[pb:pb + 64,
                                            dst_kv + self.kvcol(h // 2):
                                            dst_kv + self.kvcol(h // 2) + 65]
                            last = vector.tensor_tensor(a_dst,
                                                        self.PS[hh][pb:pb + 64, 0:65],
                                                        a_src, ALU.add)
                        self.c_ve.inc(last)
                    self.sec("vector", f)
                    for hh in range(8):
                        self.set_guard(hh, self.c_ve)
            self.barrier(("sync", "vector", "scalar", "tensor", "gpsimd"))

        fin_kv = (KVA, KVB)[(NT * 4) % 2]

        def f(vector):
            vector.wait_ge(self.c_pe.h, self.c_pe.n)
            vector.tensor_copy(self.KVSB[:, 0:520], self.AR[:, fin_kv:fin_kv + 520])
            vector.tensor_copy(self.KVSB[0:2, 520:528], self.MX[:, 0:8])
            self.c_ve.inc(vector.tensor_copy(self.KVSB[0:2, 528:536], self.MX[:, 16:24]))
        self.sec("vector", f)

        groups = [[0, 1, 2, 3], [4, 5, 6, 7]]

        def f(gpsimd):
            gpsimd.wait_ge(self.c_ve.h, self.c_ve.n)
            self.s_st.inc(gpsimd.dma_start(out=self.ds["cb_ag_i"][:],
                                           in_=self.KVSB[:, 0:536]), 16)
            gpsimd.wait_ge(self.s_st.h, self.s_st.n)
            self.c_cc.inc(gpsimd.collective_compute(
                "AllGather", ALU.bypass, replica_groups=groups,
                ins=[self.ds["cb_ag_i"][:]], outs=[self.ds["cb_ag_o"][:]]))
            gpsimd.wait_ge(self.c_cc.h, self.c_cc.n)
            K4 = self.AR[:, self.SL_RAW:self.SL_RAW + 4 * 536]
            self.s_gp.inc(gpsimd.dma_start(
                out=K4.rearrange("p (a b) -> p a b", b=536),
                in_=self.ds["cb_ag_o"].rearrange("(a p) b -> p a b", p=128)), 16)
        self.sec("gpsimd", f)
        gpm = self.s_gp.n

        RAWs = self.SL_RAW
        C = self.COMB

        def f(vector):
            vector.wait_ge(self.s_gp.h, gpm)
            for i in range(4):
                vector.tensor_copy(C[:, 8 * i:8 * i + 8],
                                   self.AR[0:2, RAWs + 536 * i + 520:RAWs + 536 * i + 528])
            vector.drain()
            vector.tensor_tensor(self.MX[:, 48:56], C[:, 0:8], C[:, 8:16], ALU.max)
            vector.drain()
            vector.tensor_tensor(self.MX[:, 56:64], self.MX[:, 48:56], C[:, 16:24], ALU.max)
            vector.drain()
            vector.tensor_tensor(self.MX[:, 48:56], self.MX[:, 56:64], C[:, 24:32], ALU.max)
            vector.drain()
            vector.tensor_scalar(C[:, 40:48], self.MX[:, 48:56], 0.0, None, ALU.max)
            vector.drain()
            for i in range(4):
                vector.tensor_tensor(self.MX[:, 32:40], C[:, 8 * i:8 * i + 8],
                                     C[:, 40:48], ALU.subtract)
                vector.drain()
                vector.tensor_copy(C[:, 8 * i:8 * i + 8], self.MX[:, 32:40])
                vector.drain()
            self.c_ve.inc(self.vfence(vector))
        self.sec("vector", f)

        def f(scalar):
            scalar.wait_ge(self.c_ve.h, self.c_ve.n)
            scalar.activation(C[:, 0:32], C[:, 0:32], AF.Exp)
            self.c_ac.inc(scalar.activation(C[:, 48:56], C[:, 40:48], AF.Exp, scale=-1.0))
        self.sec("scalar", f)

        def f(vector):
            vector.wait_ge(self.c_ac.h, self.c_ac.n)
            zp = [C[:, 32:40], self.MX[:, 56:64]]
            vector.tensor_copy(zp[0], C[:, 48:56])
            vector.drain()
            for i in range(4):
                vector.tensor_tensor(self.MX[:, 48:56],
                                     self.AR[0:2, RAWs + 536 * i + 528:RAWs + 536 * i + 536],
                                     C[:, 8 * i:8 * i + 8], ALU.mult)
                vector.drain()
                vector.tensor_tensor(zp[(i + 1) % 2], zp[i % 2], self.MX[:, 48:56], ALU.add)
                vector.drain()
            vector.reciprocal(self.MX[:, 48:56], zp[0])
            vector.drain()
            vector.tensor_scalar(C[:, 56:64], self.MX[:, 48:56], float(LQ), None, ALU.mult)
            vector.drain()
            self.c_ve.inc(self.vfence(vector))
        self.sec("vector", f)

        def f(tensor):
            ps = self.bank(1, tensor)
            tensor.wait_ge(self.c_ve.h, self.c_ve.n)
            sel2 = self.SELP[0:2, 128:256]
            tensor.matmul(ps[:, 0:32], sel2, C[:, 0:32], start=True, stop=True)
            tensor.matmul(ps[:, 32:40], sel2, C[:, 48:56], start=True, stop=True)
            self.c_pe.inc(tensor.matmul(ps[:, 40:48], sel2, C[:, 56:64],
                                        start=True, stop=True))
        self.sec("tensor", f)
        pem_f = self.c_pe.n

        NB0 = self.SL_NORM
        NB1 = self.SL_NORM + 1040

        def f(vector):
            vector.wait_ge(self.c_pe.h, pem_f)
            F = self.PS[1]
            last = None
            for c in range(8):
                col = self.kvcol(c)
                a0 = self.AR[:, NB0 + 65 * c:NB0 + 65 * c + 65]
                a1 = self.AR[:, NB1 + 65 * c:NB1 + 65 * c + 65]
                k4 = [self.AR[:, RAWs + 536 * i + col:RAWs + 536 * i + col + 65]
                      for i in range(4)]
                vector.tensor_scalar(a0, k4[0], F[:, c:c + 1], None, ALU.mult)
                vector.drain()
                vector.scalar_tensor_tensor(a1, k4[1], F[:, 8 + c:9 + c], a0,
                                            ALU.mult, ALU.add)
                vector.drain()
                vector.scalar_tensor_tensor(a0, k4[2], F[:, 16 + c:17 + c], a1,
                                            ALU.mult, ALU.add)
                vector.drain()
                vector.scalar_tensor_tensor(a1, k4[3], F[:, 24 + c:25 + c], a0,
                                            ALU.mult, ALU.add)
                vector.drain()
                vector.scalar_tensor_tensor(a0[:, 64:65], self.PK0[:, c:c + 1],
                                            F[:, 32 + c:33 + c], a1[:, 64:65],
                                            ALU.mult, ALU.add)
                vector.drain()
                vector.tensor_scalar(self.KVSB[:, col:col + 64], a1[:, 0:64],
                                     F[:, 40 + c:41 + c], None, ALU.mult)
                last = vector.tensor_scalar(self.KVSB[:, col + 64:col + 65],
                                            a0[:, 64:65],
                                            F[:, 40 + c:41 + c], None, ALU.mult)
            vector.drain()
            self.c_ve.inc(self.vfence(vector))
        self.sec("vector", f)
        self.set_guard(1, self.c_ve)

        def f(vector):
            vector.memset(self.KSE[:], 0.0)
            vector.drain()
            for c in range(8):
                col = self.kvcol(c) + 64
                vector.tensor_copy(self.KSE[0:64, 2 * c:2 * c + 1],
                                   self.KVSB[0:64, col:col + 1])
                vector.tensor_copy(self.KSE[64:128, 2 * c + 1:2 * c + 2],
                                   self.KVSB[64:128, col:col + 1])
            vector.drain()
            self.c_ve.inc(self.vfence(vector))
        self.sec("vector", f)
        self.barrier()

    def emit_attn(self, pq_sl, pq_stride, pq_off, w, dst_fn, use_mask=False,
                  waits=()):
        for c in range(8):
            bn = c % 2
            bd = 2 + c % 2

            def f(tensor, c=c, bn=bn, bd=bd):
                ps5 = self.bank(5, tensor)
                if c == 0:
                    for sm, cnt in waits:
                        tensor.wait_ge(sm.h, cnt)
                self.c_pe.inc(tensor.matmul(
                    ps5[0:2, 0:w], self.KSE[:, 2 * c:2 * c + 2],
                    self.AR[:, pq_sl + c * pq_stride + pq_off:
                            pq_sl + c * pq_stride + pq_off + w],
                    start=True, stop=True))
            self.sec("tensor", f)
            pem_d = self.c_pe.n

            def f(vector, c=c, pem_d=pem_d):
                vector.wait_ge(self.c_pe.h, pem_d)
                vector.tensor_scalar(self.DEN[:, 0:w], self.PS[5][0:2, 0:w], EPS_ATT,
                                     None, ALU.add)
                if w < 128:
                    vector.drain()
                vector.reciprocal(self.DEN[:, 512:512 + w], self.DEN[:, 0:w])
                vector.drain()
                self.c_ve.inc(self.vfence(vector))
            self.sec("vector", f)
            self.set_guard(5, self.c_ve)
            vem_d = self.c_ve.n

            def f(tensor, c=c, bn=bn, bd=bd, vem_d=vem_d):
                psn = self.bank(bn, tensor)
                psd = self.bank(bd, tensor)
                tensor.wait_ge(self.c_ve.h, vem_d)
                for h in (2 * c, 2 * c + 1):
                    pb = 64 * (h % 2)
                    col = self.kvcol(c)
                    lhs = self.KVSB[pb:pb + 64, col:col + 64]
                    rhs = self.AR[pb:pb + 64, pq_sl + c * pq_stride + pq_off:
                                  pq_sl + c * pq_stride + pq_off + w]
                    tensor.matmul(psn[pb:pb + 64, 0:w], lhs, rhs, start=True, stop=True)
                self.c_pe.inc(tensor.matmul(psd[:, 0:w], self.SELP[0:2, 128:256],
                                            self.DEN[:, 512:512 + w],
                                            start=True, stop=True))
            self.sec("tensor", f)
            pem_n = self.c_pe.n

            def f(vector, c=c, bn=bn, bd=bd, pem_n=pem_n):
                vector.wait_ge(self.c_pe.h, pem_n)
                d = dst_fn(c)
                vector.tensor_copy(self.T1[:, 0:w], self.PS[bd][:, 0:w])
                if w < 128:
                    vector.drain()
                if use_mask:
                    vector.tensor_tensor(self.T0[:, 0:w], self.PS[bn][:, 0:w],
                                         self.T1[:, 0:w], ALU.mult)
                    vector.drain()
                    last = vector.tensor_tensor(d, self.T0[:, 0:w],
                                                self.MASKB[:, 0:w], ALU.mult)
                else:
                    last = vector.tensor_tensor(d, self.PS[bn][:, 0:w],
                                                self.T1[:, 0:w], ALU.mult)
                self.c_ve.inc(last)
            self.sec("vector", f)
            self.set_guard(bn, self.c_ve)
            self.set_guard(bd, self.c_ve)

    def emit_w1(self, w):
        ldm = {}
        pemk = {}
        self.acm_w1 = {}
        wp0 = self.wp_pe
        ve_rdy = self.c_ve.n
        gc_rdy = self.c_gc.n
        for fo2 in range(16):
            wph = self.BL_WP + (fo2 % 2) * 2048
            wpl = self.BL_WP + 4096 + (fo2 % 2) * 2048

            def f(sync, fo2=fo2, wph=wph, wpl=wpl):
                if fo2 < 2:
                    sync.wait_ge(self.c_pe.h, wp0)
                else:
                    sync.wait_ge(self.c_pe.h, pemk[(fo2 - 2, 1)])
                blk = self.ds["ws_w1h"][:, fo2 * 2048:(fo2 + 1) * 2048]
                self.s_sy.inc(sync.dma_start(out=self.AB[:, wph:wph + 2048], in_=blk), 16)
                blk = self.ds["ws_w1l"][:, fo2 * 2048:(fo2 + 1) * 2048]
                self.s_sy.inc(sync.dma_start(out=self.AB[:, wpl:wpl + 2048], in_=blk), 16)
            self.sec("sync", f)
            ldm[fo2] = self.s_sy.n

            for fop in range(2):
                fo = fo2 * 2 + fop
                bk = fo % 4

                def f(tensor, fo2=fo2, fop=fop, bk=bk, wph=wph, wpl=wpl):
                    ps = self.bank(bk, tensor)
                    tensor.wait_ge(self.s_sy.h, ldm[fo2])
                    if fo2 == 0 and fop == 0:
                        tensor.wait_ge(self.c_ve.h, ve_rdy)
                        tensor.wait_ge(self.c_gc.h, gc_rdy)
                    mm = None
                    for fi in range(8):
                        xh = self.AB[:, self.BL_VH + fi * w:self.BL_VH + fi * w + w]
                        xl = self.AB[:, self.BL_VL + fi * w:self.BL_VL + fi * w + w]
                        wh = self.AB[:, wph + fi * 256 + fop * 128:
                                     wph + fi * 256 + fop * 128 + 128]
                        wl = self.AB[:, wpl + fi * 256 + fop * 128:
                                     wpl + fi * 256 + fop * 128 + 128]
                        tensor.matmul(ps[:, 0:w], wh, xh, start=(fi == 0), stop=False)
                        tensor.matmul(ps[:, 0:w], wl, xh, start=False, stop=False)
                        mm = tensor.matmul(ps[:, 0:w], wh, xl, start=False, stop=(fi == 7))
                    self.c_pe.inc(mm)
                self.sec("tensor", f)
                pemk[(fo2, fop)] = self.c_pe.n

            if fo2 % 2 == 1:
                def f(scalar, fo2=fo2):
                    scalar.wait_ge(self.c_pe.h, pemk[(fo2, 1)])
                    last = None
                    for fo in range(fo2 * 2 - 2, fo2 * 2 + 2):
                        d = self.AR[:, self.SL_GU + fo * w:self.SL_GU + fo * w + w]
                        last = scalar.activation(d, self.PS[fo % 4][:, 0:w], AF.Gelu,
                                                 bias=self.B1P[:, fo:fo + 1], scale=1.0)
                    self.c_ac.inc(last)
                self.sec("scalar", f)
                for b in range(4):
                    self.set_guard(b, self.c_ac)
                self.acm_w1[fo2 * 2 - 2] = self.acm_w1[fo2 * 2 - 1] = \
                    self.acm_w1[fo2 * 2] = self.acm_w1[fo2 * 2 + 1] = self.c_ac.n
        self.wp_pe = pemk[(15, 1)]

    def emit_w2(self, w):
        dldm = {}
        dvem = {}
        dpem = {}
        wp0 = self.wp_pe
        for fi in range(32):
            wph = self.BL_WP + (fi % 2) * 1024
            wpl = self.BL_WP + 4096 + (fi % 2) * 1024

            def f(sync, fi=fi, wph=wph, wpl=wpl):
                if fi < 2:
                    sync.wait_ge(self.c_pe.h, wp0)
                else:
                    sync.wait_ge(self.c_pe.h, dpem[fi - 2])
                blk = self.ds["ws_w2h"][:, fi * 1024:(fi + 1) * 1024]
                self.s_sy.inc(sync.dma_start(out=self.AB[:, wph:wph + 1024], in_=blk), 16)
                blk = self.ds["ws_w2l"][:, fi * 1024:(fi + 1) * 1024]
                self.s_sy.inc(sync.dma_start(out=self.AB[:, wpl:wpl + 1024], in_=blk), 16)
            self.sec("sync", f)
            dldm[fi] = self.s_sy.n

            def f(vector, fi=fi):
                if fi >= 2:
                    vector.wait_ge(self.c_pe.h, dpem[fi - 2])
                vector.wait_ge(self.c_ac.h, self.acm_w1[fi])
                hc = self.AR[:, self.SL_GU + fi * w:self.SL_GU + fi * w + w]
                hh = self.AB[:, self.BL_T + (fi % 2) * 512:self.BL_T + (fi % 2) * 512 + w]
                hl = self.AB[:, self.BL_T + 1024 + (fi % 2) * 512:
                             self.BL_T + 1024 + (fi % 2) * 512 + w]
                vector.tensor_copy(hh, hc)
                self.c_ve.inc(vector.tensor_sub(hl, hc, hh))
            self.sec("vector", f)
            dvem[fi] = self.c_ve.n

            def f(tensor, fi=fi, wph=wph, wpl=wpl):
                if fi == 0:
                    for b in range(8):
                        self.bank(b, tensor)
                tensor.wait_ge(self.s_sy.h, dldm[fi])
                tensor.wait_ge(self.c_ve.h, dvem[fi])
                hh = self.AB[:, self.BL_T + (fi % 2) * 512:self.BL_T + (fi % 2) * 512 + w]
                hl = self.AB[:, self.BL_T + 1024 + (fi % 2) * 512:
                             self.BL_T + 1024 + (fi % 2) * 512 + w]
                mm = None
                for fo in range(8):
                    ps = self.PS[fo][:, 0:w]
                    wh = self.AB[:, wph + fo * 128:wph + (fo + 1) * 128]
                    wl = self.AB[:, wpl + fo * 128:wpl + (fo + 1) * 128]
                    tensor.matmul(ps, wh, hh, start=(fi == 0), stop=False)
                    tensor.matmul(ps, wl, hh, start=False, stop=False)
                    mm = tensor.matmul(ps, wh, hl, start=False, stop=(fi == 31))
                self.c_pe.inc(mm)
            self.sec("tensor", f)
            dpem[fi] = self.c_pe.n
        self.wp_pe = dpem[31]

    def emit_finish(self, tau):
        CIN = self.SL_Y1
        w = HT
        B = self.BIAS
        for hv in range(2):
            def f(sync, hv=hv):
                sync.wait_ge(self.c_ve.h, self.c_ve.n)
                sync.wait_ge(self.c_ac.h, self.c_ac.n)
                self.s_sy.inc(sync.dma_start(out=self.AR[:, self.SL_GU:self.SL_GU + 5120],
                                             in_=self.di["convbd"][:]), 16)
            self.sec("sync", f)
            ldcv = self.s_sy.n

            pemc = {}
            for p in range(8):
                bk = 6 + p % 2

                def f(tensor, p=p, bk=bk, hv=hv):
                    ps = self.bank(bk, tensor)
                    tensor.wait_ge(self.s_sy.h, ldcv)
                    if p < 2:
                        tensor.wait_ge(self.c_ve.h, self.c_ve.n)
                    mm = None
                    for k in range(5):
                        lhs = self.AR[:, self.SL_GU + (p * 5 + k) * 128:
                                      self.SL_GU + (p * 5 + k) * 128 + 128]
                        rhs = self.AR[:, CIN + 516 * p + 256 * hv + k:
                                      CIN + 516 * p + 256 * hv + k + w]
                        mm = tensor.matmul(ps[:, 0:w], lhs, rhs,
                                           start=(k == 0), stop=(k == 4))
                    self.c_pe.inc(mm)
                self.sec("tensor", f)
                pemc[p] = self.c_pe.n

                def f(vector, p=p, bk=bk, hv=hv):
                    vector.wait_ge(self.c_pe.h, pemc[p])
                    vector.tensor_scalar(self.T0[:, 0:w], self.PS[bk][:, 0:w],
                                         B[:, 120:121], self.GCB[:, p:p + 1],
                                         ALU.mult, ALU.add)
                    d = self.AR[:, self.SL_V + p * w:self.SL_V + p * w + w]
                    self.c_ve.inc(vector.tensor_tensor(
                        d, self.T0[:, 0:w],
                        self.AR[:, CIN + 516 * p + 2 + 256 * hv:
                                CIN + 516 * p + 2 + 256 * hv + w], ALU.add))
                self.sec("vector", f)
                self.set_guard(bk, self.c_ve)

            self.emit_sums(self.SL_V, w, self.SL_RAW + 2048, 6,
                           waits=[(self.c_ve, self.c_ve.n)])
            self.emit_ln_stats(6, w, "lna", minmax=False)
            self.emit_ln_apply(self.SL_V, self.SL_V, w, "lna",
                               gcol=B[:, 88:96], bcol=B[:, 96:104])
            self.emit_split(self.SL_V, self.BL_VH, self.BL_VL, w)

            ldy1 = self.emit_load("s_y1", self.SL_V + 2048, tau * TT + hv * w, w,
                                  eng="gpsimd", waits=[(self.c_ve, self.c_ve.n)])
            ldv2 = self.emit_load("s_v", self.SL_RAW, tau * TT + hv * w, w,
                                  eng="gpsimd")

            self.emit_w1(w)
            self.emit_w2(w)

            def f(vector, hv=hv, ldy1=ldy1, ldv2=ldv2):
                vector.wait_ge(self.c_pe.h, self.c_pe.n)
                vector.wait_ge(self.s_gp.h, max(ldy1, ldv2))
                last = None
                for fo in range(8):
                    ps = self.PS[fo][:, 0:w]
                    vector.tensor_scalar(self.T0[:, 0:w], ps, B[:, 104 + fo:105 + fo],
                                         None, ALU.add)
                    vector.tensor_tensor(self.T1[:, 0:w], self.T0[:, 0:w],
                                         self.AR[:, self.SL_V + fo * w:
                                                 self.SL_V + fo * w + w], ALU.add)
                    vector.tensor_tensor(self.T0[:, 0:w], self.T1[:, 0:w],
                                         self.AR[:, self.SL_RAW + fo * w:
                                                 self.SL_RAW + fo * w + w], ALU.add)
                    last = vector.tensor_tensor(
                        self.AR[:, self.SL_V + fo * w:self.SL_V + fo * w + w],
                        self.T0[:, 0:w],
                        self.AR[:, self.SL_V + 2048 + fo * w:
                                self.SL_V + 2048 + fo * w + w], ALU.add)
                self.c_ve.inc(last)
            self.sec("vector", f)
            for b in range(8):
                self.set_guard(b, self.c_ve)

            self.emit_sums(self.SL_V, w, self.SL_RAW + 2048, 6,
                           waits=[(self.c_ve, self.c_ve.n)])
            self.emit_minmax(self.SL_V, w, 7, "ln2_mx", "ln2_mn",
                             waits=[(self.c_ve, self.c_ve.n)])
            self.emit_ln_stats(6, w, "ln2", minmax=True)
            self.emit_quant(self.SL_V, w, self.BL_UQ, "ln2", 2, 3)
            self.emit_ffn(self.BL_UQ, w, 0, w, "ln2", self.SL_RAW, self.SL_V, w)

            def f(gpsimd, hv=hv):
                gpsimd.wait_ge(self.c_ve.h, self.c_ve.n)
                src = self.AR[:, self.SL_RAW:self.SL_RAW + 8 * w].rearrange(
                    "p (c t) -> p c t", t=w)
                dst = self.fm(self.out, tau * TT + hv * w, w)
                self.s_st.inc(gpsimd.dma_start(out=dst, in_=src), 16)
            self.sec("gpsimd", f)

    def emit_sweep3(self):
        CIN = self.SL_Y1

        ldm = self.emit_load("s_pq", self.SL_X, TOK, HALO, eng="gpsimd")
        self.emit_attn(self.SL_X, HALO, 0, HALO,
                       lambda s: self.MINIA[:, 4 * s:4 * s + 4], use_mask=True,
                       waits=[(self.s_gp, ldm)])

        def f(vector):
            vector.wait_ge(self.c_ve.h, self.c_ve.n)
            last = None
            for p in range(8):
                last = vector.tensor_copy(self.AR[:, CIN + 516 * p:CIN + 516 * p + 2],
                                          self.MINIA[:, 4 * p:4 * p + 2])
            self.c_ve.inc(last)
        self.sec("vector", f)
        self.barrier()

        for t in range(NT):
            ldp = self.emit_load("s_pq", self.SL_X, t * TT, TT, eng="gpsimd",
                                 waits=[(self.c_pe, self.c_pe.n)])
            if t >= 1:
                self.emit_attn(self.SL_X, TT, 0, 2,
                               lambda s: self.AR[:, CIN + 516 * s + 514:CIN + 516 * s + 516],
                               waits=[(self.s_gp, ldp)])
                self.emit_finish(t - 1)

                def f(vector):
                    vector.wait_ge(self.c_ve.h, self.c_ve.n)
                    vector.wait_ge(self.c_ac.h, self.c_ac.n)
                    last = None
                    for p in range(8):
                        last = vector.tensor_copy(
                            self.AR[:, CIN + 516 * p:CIN + 516 * p + 2],
                            self.AR[:, CIN + 516 * p + 512:CIN + 516 * p + 514])
                    self.c_ve.inc(last)
                self.sec("vector", f)
            self.emit_attn(self.SL_X, TT, 0, TT,
                           lambda s: self.AR[:, CIN + 516 * s + 2:CIN + 516 * s + 514],
                           waits=[(self.s_gp, ldp), (self.c_ve, self.c_ve.n)])

            def f(vector):
                vector.wait_ge(self.s_st.h, self.s_st.n)
                vector.wait_ge(self.s_gp.h, self.s_gp.n)
            self.sec("vector", f)
        self.barrier()

        def f(vector):
            last = None
            for p in range(8):
                last = vector.tensor_copy(
                    self.AR[:, CIN + 516 * p + 514:CIN + 516 * p + 516],
                    self.MINIA[:, 4 * p + 2:4 * p + 4])
            self.c_ve.inc(last)
        self.sec("vector", f)
        self.emit_finish(NT - 1)
        self.barrier()

    def emit_taps_and_close(self):
        if self.taps:
            def f(gpsimd):
                gpsimd.wait_ge(self.c_ve.h, self.c_ve.n)
                gpsimd.wait_ge(self.c_pe.h, self.c_pe.n)
                gpsimd.wait_ge(self.c_ac.h, self.c_ac.n)
                gpsimd.wait_ge(self.c_gc.h, self.c_gc.n)
                gpsimd.wait_ge(self.s_st.h, self.s_st.n)
                gpsimd.wait_ge(self.s_sy.h, self.s_sy.n)
                for nm in ("s_y1", "s_v", "s_pq", "s_kk", "s_fk", "s_vv",
                           "ws_wgq", "ws_wdq", "wsh_q", "wsl_q", "ws_w2h"):
                    self.s_st.inc(gpsimd.dma_start(out=self.tap_t[nm][:],
                                                   in_=self.ds[nm][:]), 16)
                self.s_st.inc(gpsimd.dma_start(out=self.tap_t["qg"][:],
                                               in_=self.QG[:]), 16)
                self.s_st.inc(gpsimd.dma_start(out=self.tap_t["wsr"][:],
                                               in_=self.WS[:]), 16)
                self.s_st.inc(gpsimd.dma_start(out=self.tap_t["const"][:],
                                               in_=self.CONST[:]), 16)
                self.s_st.inc(gpsimd.dma_start(out=self.tap_t["uq"][:],
                                               in_=self.AB[:, self.BL_UQ:self.BL_UQ + 4096]), 16)
                self.s_st.inc(gpsimd.dma_start(out=self.tap_t["gu"][:],
                                               in_=self.AR[:, self.SL_GU:self.SL_GU + 8192]), 16)
                self.s_st.inc(gpsimd.dma_start(out=self.tap_t["bc"][:],
                                               in_=self.AR[:, self.SL_BC:self.SL_BC + 3072]), 16)
                self.s_st.inc(gpsimd.dma_start(out=self.tap_t["st"][:],
                                               in_=self.ST[:]), 16)
                self.s_st.inc(gpsimd.dma_start(out=self.tap_t["kv"][:],
                                               in_=self.KVSB[:]), 16)
            self.sec("gpsimd", f)

        def f(gpsimd):
            gpsimd.wait_ge(self.s_st.h, self.s_st.n)
        self.sec("gpsimd", f)


def _pack_col(v):
    return np.ascontiguousarray(np.asarray(v, dtype=np.float32).reshape(-1, 128).T)


def build_nc(taps=False):
    return Builder(taps=taps, stages=_STAGES).build()


def kernel(L0_attn_x, L0_attn_Wq, L0_attn_bq, L0_attn_lnq_g, L0_attn_lnq_b,
           L0_attn_Wk, L0_attn_bk, L0_attn_lnk_g, L0_attn_lnk_b,
           L0_attn_Wv, L0_attn_bv, L0_attn_lnv_g, L0_attn_lnv_b,
           L0_attn_Wphi_q, L0_attn_bphi_q, L0_attn_Wphi_k, L0_attn_bphi_k,
           L0_attn_lna_g, L0_attn_lna_b, L0_attn_W1, L0_attn_b1,
           L0_attn_W2, L0_attn_b2, L0_attn_gamma, L0_attn_conv_w, L0_attn_conv_b,
           L0_ffn_w_gate, L0_ffn_w_up, L0_ffn_w_down):
    x = np.asarray(L0_attn_x, dtype=np.float32)
    f32 = lambda a: np.ascontiguousarray(np.asarray(a, dtype=np.float32))

    bias = np.zeros((128, 160), dtype=np.float32)
    cols = [L0_attn_bq, L0_attn_bk, L0_attn_bv, L0_attn_bphi_q, L0_attn_bphi_k,
            L0_attn_lnq_g, L0_attn_lnq_b, L0_attn_lnk_g, L0_attn_lnk_b,
            L0_attn_lnv_g, L0_attn_lnv_b, L0_attn_lna_g, L0_attn_lna_b,
            L0_attn_b2, L0_attn_conv_b]
    for i, v in enumerate(cols):
        bias[:, 8 * i:8 * (i + 1)] = _pack_col(v)
    bias[:, 120] = np.float32(L0_attn_gamma)
    b1p = _pack_col(L0_attn_b1)

    convw = np.asarray(L0_attn_conv_w, dtype=np.float32)
    convbd = np.zeros((128, 5120), dtype=np.float32)
    for p in range(8):
        for k in range(5):
            c0 = (p * 5 + k) * 128
            convbd[0:64, c0:c0 + 64] = convw[128 * p:128 * p + 64, :, k].T
            convbd[64:128, c0 + 64:c0 + 128] = convw[128 * p + 64:128 * p + 128, :, k].T

    eye = np.eye(128, dtype=np.float32)
    selp = np.zeros((16, 264), dtype=np.float32)
    for h in range(16):
        for p in range(128):
            selp[h, p] = 1.0 if (p // 64) == (h % 2) else 0.0
    for r in range(2):
        for p in range(128):
            selp[r, 128 + p] = 1.0 if (p // 64) == r else 0.0
    for h in range(16):
        for s in range(8):
            selp[h, 256 + s] = 1.0 if (h // 2) == s else 0.0

    shared = {
        "wq": f32(L0_attn_Wq), "wk": f32(L0_attn_Wk), "wv": f32(L0_attn_Wv),
        "wpq": f32(L0_attn_Wphi_q), "wpk": f32(L0_attn_Wphi_k),
        "w1": f32(L0_attn_W1), "w2": f32(L0_attn_W2),
        "wg": f32(np.asarray(L0_ffn_w_gate).T), "wu": f32(np.asarray(L0_ffn_w_up).T),
        "wd": f32(np.asarray(L0_ffn_w_down).T),
        "bias": bias, "b1p": b1p, "convbd": convbd, "eye": eye, "selp": selp,
    }

    in_maps = []
    for c in range(NCORES):
        b, q = c // 4, c % 4
        s0 = q * TOK
        xcore = np.zeros((E, TOKH), dtype=np.float32)
        xcore[:, 0:TOK] = x[b, s0:s0 + TOK, :].T
        for j, idx in enumerate((s0 - 2, s0 - 1, s0 + TOK, s0 + TOK + 1)):
            if 0 <= idx < LQ:
                xcore[:, TOK + j] = x[b, idx, :]
        mask = np.array([[1.0 if q > 0 else 0.0, 1.0 if q > 0 else 0.0,
                          1.0 if q < 3 else 0.0, 1.0 if q < 3 else 0.0]],
                        dtype=np.float32)
        m = dict(shared)
        m["x"] = xcore
        m["mask"] = mask
        in_maps.append(m)

    nc = build_nc(taps=_TAPS)
    res = run_bass_kernel_spmd(nc, in_maps, core_ids=list(range(NCORES)))

    out = np.empty((2, LQ, E), dtype=np.float32)
    for c in range(NCORES):
        b, q = c // 4, c % 4
        out[b, q * TOK:(q + 1) * TOK, :] = res.results[c]["out"].T
    if _TAPS:
        kernel._last_results = res
    return out


# version 10 (final)
# speedup vs baseline: 1.0448x; 1.0072x over previous
import numpy as np
import concourse.bass as bass
import concourse.mybir as mybir
from concourse.bass_utils import run_bass_kernel_spmd

F32 = mybir.dt.float32
BF16 = mybir.dt.bfloat16
F16 = mybir.dt.float16
I32 = mybir.dt.int32
AF = mybir.ActivationFunctionType
ALU = mybir.AluOpType
AX = mybir.AxisListType

E = 1024
H = 16
D = 64
MLPD = 4096
LQ = 16384
NCORES = 8
TOK = 4096
TT = 512
HT = 256
NT = 8
HALO = 4
TOKH = TOK + HALO
EC = E // 128
MC = MLPD // 128
ROFF = 1536.0
EPS_LN = 1e-5
EPS_ATT = 1e-6

_TAPS = False
_STAGES = 5
_S1_TILES = None
_S1_FFN_ONLY = False


class Sem:
    def __init__(self, nc, name):
        self.h = nc.alloc_semaphore(name)
        self.n = 0

    def inc(self, instr, k=1):
        instr.then_inc(self.h, k)
        self.n += k
        return instr


class Builder:
    def __init__(self, taps=False, stages=None):
        self.taps = taps
        self.stages = _STAGES if stages is None else stages
        nc = bass.Bass("TRN2", target_bir_lowering=False)
        self.nc = nc

        di = {}
        di["x"] = nc.dram_tensor("x", [E, TOKH], F32, kind="ExternalInput").ap()
        for nm in ("wq", "wk", "wv", "wpq", "wpk"):
            di[nm] = nc.dram_tensor(nm, [E, E], F32, kind="ExternalInput").ap()
        di["w1"] = nc.dram_tensor("w1", [E, MLPD], F32, kind="ExternalInput").ap()
        di["w2"] = nc.dram_tensor("w2", [MLPD, E], F32, kind="ExternalInput").ap()
        di["wg"] = nc.dram_tensor("wg", [E, MLPD], F32, kind="ExternalInput").ap()
        di["wu"] = nc.dram_tensor("wu", [E, MLPD], F32, kind="ExternalInput").ap()
        di["wd"] = nc.dram_tensor("wd", [MLPD, E], F32, kind="ExternalInput").ap()
        di["bias"] = nc.dram_tensor("bias", [128, 160], F32, kind="ExternalInput").ap()
        di["b1p"] = nc.dram_tensor("b1p", [128, 32], F32, kind="ExternalInput").ap()
        di["convbd"] = nc.dram_tensor("convbd", [128, 5120], F32, kind="ExternalInput").ap()
        di["eye"] = nc.dram_tensor("eye", [128, 128], F32, kind="ExternalInput").ap()
        di["selp"] = nc.dram_tensor("selp", [16, 264], F32, kind="ExternalInput").ap()
        di["mask"] = nc.dram_tensor("mask", [1, 4], F32, kind="ExternalInput").ap()
        self.di = di
        self.out = nc.dram_tensor("out", [E, TOK], F32, kind="ExternalOutput").ap()

        ds = {}
        for nm in ("s_y1", "s_v", "s_pq", "s_kk", "s_fk", "s_vv"):
            ds[nm] = nc.dram_tensor(nm, [E, TOKH], F32).ap()
        for nm in ("q", "k", "v", "pq", "pk"):
            ds["wsh_" + nm] = nc.dram_tensor("wsh_" + nm, [128, 8192], BF16).ap()
            ds["wsl_" + nm] = nc.dram_tensor("wsl_" + nm, [128, 8192], BF16).ap()
        for nm in ("w1h", "w1l", "wgq", "wuq"):
            ds["ws_" + nm] = nc.dram_tensor("ws_" + nm, [128, 32768], BF16).ap()
        for nm in ("w2h", "w2l", "wdq"):
            ds["ws_" + nm] = nc.dram_tensor("ws_" + nm, [128, 32768], BF16).ap()
        ds["cb_qg_i"] = nc.dram_tensor("cb_qg_i", [128, 8], F32).ap()
        ds["cb_qg_o"] = nc.dram_tensor("cb_qg_o", [128, 8], F32).ap()
        ds["cb_ag_i"] = nc.dram_tensor("cb_ag_i", [128, 536], F32).ap()
        ds["cb_ag_o"] = nc.dram_tensor("cb_ag_o", [512, 536], F32).ap()
        self.ds = ds

        self.AR = nc.alloc_sbuf_tensor("AR", [128, 31808], F32).ap()
        self.SL_X = 0
        self.SL_Y1 = 4096
        self.SL_V = 8256
        self.SL_RAW = 12352
        self.SL_NORM = 16448
        self.SL_GU = 20544
        self.SL_BC = 28736
        self.SL_MISC = 30784
        self.AB = nc.alloc_sbuf_tensor("AB", [128, 22528], BF16).ap()
        self.BL_UQ = 0
        self.BL_VH = 4096
        self.BL_VL = 8192
        self.BL_WP = 12288
        self.BL_T = 20480
        self.T0 = nc.alloc_sbuf_tensor("T0", [128, 512], F32).ap()
        self.T1 = nc.alloc_sbuf_tensor("T1", [128, 512], F32).ap()
        self.TH = nc.alloc_sbuf_tensor("TH", [128, 512], F16).ap()
        self.TH2 = nc.alloc_sbuf_tensor("TH2", [128, 512], F16).ap()
        self.FEN = nc.alloc_sbuf_tensor("FEN", [128, 8], F32).ap()
        self.TI = nc.alloc_sbuf_tensor("TI", [128, 512], I32).ap()
        self.ST = nc.alloc_sbuf_tensor("ST", [128, 320], F32).ap()
        self.WS = nc.alloc_sbuf_tensor("WS", [1, 16], F32).ap()
        self.SR0 = nc.alloc_sbuf_tensor("SR0", [1, 512], F32).ap()
        self.SR1 = nc.alloc_sbuf_tensor("SR1", [1, 512], F32).ap()
        self.REPB = nc.alloc_sbuf_tensor("REPB", [128, 1024], F32).ap()
        self.ONES128 = nc.alloc_sbuf_tensor("ONES128", [128, 128], F32).ap()
        self.MROW = nc.alloc_sbuf_tensor("MROW", [1, 4], F32).ap()
        self.KVSB = nc.alloc_sbuf_tensor("KVSB", [128, 536], F32).ap()
        self.QG = nc.alloc_sbuf_tensor("QG", [128, 16], F32).ap()
        self.MX = nc.alloc_sbuf_tensor("MX", [2, 64], F32).ap()
        self.ETM = nc.alloc_sbuf_tensor("ETM", [128, 64], F32).ap()
        self.QGE = nc.alloc_sbuf_tensor("QGE", [128, 16], F32).ap()
        self.KSE = nc.alloc_sbuf_tensor("KSE", [128, 16], F32).ap()
        self.EYE = nc.alloc_sbuf_tensor("EYE", [128, 128], F32).ap()
        self.CONST = nc.alloc_sbuf_tensor("CONST", [128, 12], F32).ap()
        self.ONESR = nc.alloc_sbuf_tensor("ONESR", [1, 128], F32).ap()
        self.SELP = nc.alloc_sbuf_tensor("SELP", [16, 264], F32).ap()
        self.BIAS = nc.alloc_sbuf_tensor("BIAS", [128, 160], F32).ap()
        self.B1P = nc.alloc_sbuf_tensor("B1P", [128, 32], F32).ap()
        self.MASKB = nc.alloc_sbuf_tensor("MASKB", [128, 4], F32).ap()
        self.MINIA = nc.alloc_sbuf_tensor("MINIA", [128, 32], F32).ap()
        self.PK0 = nc.alloc_sbuf_tensor("PK0", [128, 16], F32).ap()
        self.COMB = nc.alloc_sbuf_tensor("COMB", [2, 64], F32).ap()
        self.ESB = self.AR[0:2, self.SL_GU:self.SL_GU + 4096]
        self.DEN = self.AR[0:2, self.SL_GU + 4096:self.SL_GU + 5120 + 1024]
        self.GCB = nc.alloc_sbuf_tensor("GCB", [128, 8], F32).ap()
        self.VTMO = None

        self.PS = [nc.alloc_psum_tensor(f"ps{i}", [128, 512], F32).ap() for i in range(8)]
        self.guard = [None] * 8
        self.wp_pe = 0

        nc_ = nc
        self.s_sy = Sem(nc_, "s_sy")
        self.s_sc = Sem(nc_, "s_sc")
        self.s_gp = Sem(nc_, "s_gp")
        self.s_st = Sem(nc_, "s_st")
        self.c_pe = Sem(nc_, "c_pe")
        self.c_ve = Sem(nc_, "c_ve")
        self.c_ac = Sem(nc_, "c_ac")
        self.c_gc = Sem(nc_, "c_gc")
        self.c_cc = Sem(nc_, "c_cc")
        self.esem = {"sync": Sem(nc_, "b_sy"), "vector": Sem(nc_, "b_ve"),
                     "scalar": Sem(nc_, "b_sc"), "tensor": Sem(nc_, "b_pe"),
                     "gpsimd": Sem(nc_, "b_gp")}

        self._tmv = {}
        self._tmn = 0

        self.tap_t = {}
        if taps:
            for nm in ("s_y1", "s_v", "s_pq", "s_kk", "s_fk", "s_vv"):
                self.tap_t[nm] = nc.dram_tensor("tap_" + nm, [E, TOKH], F32,
                                                kind="ExternalOutput").ap()
            self.tap_t["qg"] = nc.dram_tensor("tap_qg", [128, 16], F32,
                                              kind="ExternalOutput").ap()
            self.tap_t["kv"] = nc.dram_tensor("tap_kv", [128, 536], F32,
                                              kind="ExternalOutput").ap()
            self.tap_t["ws_wgq"] = nc.dram_tensor("tap_ws_wgq", [128, 32768], BF16,
                                                  kind="ExternalOutput").ap()
            self.tap_t["ws_wdq"] = nc.dram_tensor("tap_ws_wdq", [128, 32768], BF16,
                                                  kind="ExternalOutput").ap()
            self.tap_t["wsh_q"] = nc.dram_tensor("tap_wsh_q", [128, 8192], BF16,
                                                 kind="ExternalOutput").ap()
            self.tap_t["wsl_q"] = nc.dram_tensor("tap_wsl_q", [128, 8192], BF16,
                                                 kind="ExternalOutput").ap()
            self.tap_t["ws_w2h"] = nc.dram_tensor("tap_ws_w2h", [128, 32768], BF16,
                                                  kind="ExternalOutput").ap()
            self.tap_t["wsr"] = nc.dram_tensor("tap_wsr", [1, 16], F32,
                                               kind="ExternalOutput").ap()
            self.tap_t["uq"] = nc.dram_tensor("tap_uq", [128, 4096], BF16,
                                              kind="ExternalOutput").ap()
            self.tap_t["gu"] = nc.dram_tensor("tap_gu", [128, 8192], F32,
                                              kind="ExternalOutput").ap()
            self.tap_t["bc"] = nc.dram_tensor("tap_bc", [128, 3072], F32,
                                              kind="ExternalOutput").ap()
            self.tap_t["st"] = nc.dram_tensor("tap_st", [128, 320], F32,
                                              kind="ExternalOutput").ap()
            self.tap_t["const"] = nc.dram_tensor("tap_const", [128, 12], F32,
                                                 kind="ExternalOutput").ap()

    def sec(self, eng, fn):
        getattr(self.block, eng)(fn)

    def vfence(self, vector):
        vector.drain()
        return vector.memset(self.FEN[:, 0:1], 0.0)

    def tm(self, name, nblk=4):
        if name not in self._tmv:
            self._tmv[name] = self._tmn
            self._tmn += 4
            assert self._tmn <= 320
        o = self._tmv[name]
        return self.ST[:, o:o + nblk]

    def bank(self, i, eng):
        g = self.guard[i]
        if g is not None:
            eng.wait_ge(g[0].h, g[1])
            self.guard[i] = None
        return self.PS[i]

    def set_guard(self, i, sem):
        self.guard[i] = (sem, sem.n)

    def barrier(self, engs=("sync", "vector", "scalar", "tensor", "gpsimd")):
        counts = {}
        dma_waits = {"sync": [self.s_sy], "scalar": [self.s_sc],
                     "gpsimd": [self.s_gp, self.s_st]}
        for e in engs:
            s = self.esem[e]

            def f(eng, s=s, e=e):
                for dsem in dma_waits.get(e, ()):
                    eng.wait_ge(dsem.h, dsem.n)
                if e == "tensor":
                    eng.drain()
                    eng.sem_inc(s.h, 1)
                elif e == "vector":
                    eng.drain()
                    eng.memset(self.FEN[:, 1:2], 0.0).then_inc(s.h, 1)
                elif e == "gpsimd":
                    eng.memset(self.FEN[:, 2:3], 0.0).then_inc(s.h, 1)
                elif e == "scalar":
                    eng.activation(self.FEN[:, 3:4], self.FEN[:, 3:4],
                                   AF.Identity).then_inc(s.h, 1)
                else:
                    eng.sem_inc(s.h, 1)
            self.sec(e, f)
            s.n += 1
            counts[e] = s.n
        for e in engs:
            def f(eng, e=e):
                for o, c in counts.items():
                    if o != e:
                        eng.wait_ge(self.esem[o].h, c)
            self.sec(e, f)

    def fm(self, dram, c0, w):
        return dram.rearrange("(c p) t -> p c t", p=128)[:, :, c0:c0 + w]

    def arv(self, sl, w8):
        return self.AR[:, sl:sl + 8 * w8].rearrange("p (c t) -> p c t", t=w8)

    def chunk(self, sl, c, w, off=0):
        base = sl + c * w if off == 0 else sl + off + c * w
        return self.AR[:, base:base + w]

    def mm3(self, te, out, wh, wl, xh, xl, start, stop):
        te.matmul(out, wh, xh, start=start, stop=False)
        te.matmul(out, wl, xh, start=False, stop=False)
        te.matmul(out, wh, xl, start=False, stop=stop)

    def build(self):
        nc = self.nc
        with nc.Block() as block:
            self.block = block
            self.emit_preamble()
            if self.stages >= 1:
                self.emit_weight_prep()
            if self.stages >= 2:
                self.emit_sweep1()
            if self.stages >= 3:
                self.emit_qg_collective()
            if self.stages >= 4:
                self.emit_sweep2()
            if self.stages >= 5:
                self.emit_sweep3()
            self.emit_taps_and_close()
        return nc

    def emit_preamble(self):
        di = self.di

        def f(sync):
            self.s_sy.inc(sync.dma_start(out=self.EYE[:], in_=di["eye"][:]), 16)
            self.s_sy.inc(sync.dma_start(out=self.SELP[:], in_=di["selp"][:]), 16)
            self.s_sy.inc(sync.dma_start(out=self.BIAS[:], in_=di["bias"][:]), 16)
            self.s_sy.inc(sync.dma_start(out=self.B1P[:], in_=di["b1p"][:]), 16)
            self.s_sy.inc(sync.dma_start(out=self.MROW[:], in_=di["mask"][:]), 16)
        self.sec("sync", f)
        ld = self.s_sy.n

        def f(vector):
            vector.memset(self.CONST[:, 0:1], 1.0)
            vector.memset(self.ONESR[:], 1.0)
            vector.memset(self.ONES128[:], 1.0)
            vector.memset(self.QG[:], 0.0)
            vector.wait_ge(self.s_sy.h, self.s_sy.n)
            vector.tensor_scalar(self.GCB[:], self.BIAS[:, 112:120],
                                 self.BIAS[:, 120:121], None, ALU.mult)
            self.c_ve.inc(vector.memset(self.MX[:, 2:3], 0.0))
        self.sec("vector", f)

        def f(tensor):
            tensor.wait_ge(self.s_sy.h, ld)
            tensor.wait_ge(self.c_ve.h, self.c_ve.n)
            self.c_pe.inc(tensor.matmul(self.PS[0][:, 0:4], self.ONESR[:],
                                        self.MROW[:], start=True, stop=True))
        self.sec("tensor", f)

        def f(scalar):
            scalar.wait_ge(self.s_sy.h, ld)
            self.c_ac.inc(scalar.activation(self.PK0[:, 8:16], self.BIAS[:, 32:40], AF.Tanh))
        self.sec("scalar", f)

        def f(vector):
            vector.wait_ge(self.c_pe.h, self.c_pe.n)
            vector.tensor_copy(self.MASKB[:], self.PS[0][:, 0:4])
            vector.wait_ge(self.c_ac.h, self.c_ac.n)
            self.c_ve.inc(vector.tensor_scalar(self.PK0[:, 0:8], self.PK0[:, 8:16],
                                               1.0, None, ALU.add))
        self.sec("vector", f)

    def emit_weight_prep(self):
        di, ds = self.di, self.ds
        GU0 = self.SL_GU
        GU1 = self.SL_GU + 4096
        NORM = self.SL_NORM

        specs = [("wg", 8, 4096, 0), ("wu", 8, 4096, 1), ("wd", 32, 1024, 2)]
        for wname, nchunk, outw, rowname in specs:
            ldm, acm, vem = [], [], []
            ac0, ve0 = self.c_ac.n, self.c_ve.n
            for c in range(nchunk):
                half = GU0 if c % 2 == 0 else GU1

                def f(sync, c=c, half=half):
                    if c < 2:
                        sync.wait_ge(self.c_ac.h, ac0)
                        sync.wait_ge(self.c_ve.h, ve0)
                    if c >= 2:
                        sync.wait_ge(self.c_ac.h, acm[c - 2])
                    self.s_sy.inc(sync.dma_start(
                        out=self.AR[:, half:half + outw],
                        in_=di[wname][c * 128:(c + 1) * 128, :]), 16)
                self.sec("sync", f)
                ldm.append(self.s_sy.n)

                def f(scalar, c=c, half=half):
                    scalar.wait_ge(self.s_sy.h, ldm[c])
                    if c == 0:
                        scalar.wait_ge(self.c_ve.h, ve0)
                    if c >= 1:
                        scalar.wait_ge(self.c_ve.h, vem[c - 1])
                    self.c_ac.inc(scalar.activation(
                        self.AR[:, NORM:NORM + outw], self.AR[:, half:half + outw], AF.Abs))
                self.sec("scalar", f)
                acm.append(self.c_ac.n)

                def f(vector, c=c):
                    nseg = outw // 512
                    vector.wait_ge(self.c_ac.h, acm[c])
                    for j in range(nseg):
                        vector.reduce_sum(self.T0[:, j:j + 1],
                                          self.AR[:, NORM + j * 512:NORM + (j + 1) * 512], AX.X)
                    vector.drain()
                    self.c_ve.inc(vector.reduce_sum(
                        self.T0[:, 16 + (c % 32):17 + (c % 32)], self.T0[:, 0:nseg], AX.X))
                self.sec("vector", f)
                vem.append(self.c_ve.n)

            def f(vector, nchunk=nchunk, vem=vem):
                vector.wait_ge(self.c_ve.h, vem[-1])
                vector.wait_ge(self.c_pe.h, self.c_pe.n)
                vector.drain()
                self.c_ve.inc(vector.reduce_sum(self.T0[:, 60:61],
                                                self.T0[:, 16:16 + nchunk], AX.X))
            self.sec("vector", f)

            def f(tensor):
                ps = self.bank(1, tensor)
                tensor.wait_ge(self.c_ve.h, self.c_ve.n)
                self.c_pe.inc(tensor.matmul(ps[0:1, 0:1], self.CONST[:, 0:1],
                                            self.T0[:, 60:61], start=True, stop=True))
            self.sec("tensor", f)

            def f(vector):
                vector.wait_ge(self.c_pe.h, self.c_pe.n)
                self.c_ve.inc(vector.tensor_copy(self.WS[0:1, rowname:rowname + 1],
                                                 self.PS[1][0:1, 0:1]))
            self.sec("vector", f)
            self.set_guard(1, self.c_ve)

        def f(tensor):
            ps = self.bank(1, tensor)
            tensor.wait_ge(self.c_ve.h, self.c_ve.n)
            tensor.matmul(ps[:, 0:1], self.ONESR[:], self.WS[0:1, 0:1], start=True, stop=True)
            tensor.matmul(ps[:, 1:2], self.ONESR[:], self.WS[0:1, 1:2], start=True, stop=True)
            self.c_pe.inc(tensor.matmul(ps[:, 2:3], self.ONESR[:], self.WS[0:1, 2:3],
                                        start=True, stop=True))
        self.sec("tensor", f)

        def f(vector):
            vector.wait_ge(self.c_pe.h, self.c_pe.n)
            C = self.CONST
            specs2 = [(0, 8 * 128 * 4096, 2, 5), (1, 8 * 128 * 4096, 3, 6),
                      (2, 32 * 128 * 1024, 4, 7)]
            last = None
            for i, nelem, c127, csw in specs2:
                vector.tensor_scalar(C[:, 8:9], self.PS[1][:, i:i + 1],
                                     1.0 / nelem, None, ALU.mult)
                vector.drain()
                vector.tensor_scalar(C[:, 9:10], C[:, 8:9], 1e-5, None, ALU.max)
                vector.drain()
                fac = (0.5 / 127.0) if i == 2 else (1.0 / 127.0)
                vector.tensor_scalar(C[:, c127:c127 + 1], C[:, 9:10], fac, None, ALU.mult)
                last = vector.reciprocal(C[:, csw:csw + 1], C[:, 9:10])
                vector.drain()
            self.c_ve.inc(last)
        self.sec("vector", f)
        self.set_guard(1, self.c_ve)

        qspecs = [("wg", "ws_wgq", 8, 4096, 5, True),
                  ("wu", "ws_wuq", 8, 4096, 6, True),
                  ("wd", "ws_wdq", 32, 1024, 7, False)]
        qstm = []
        for wname, sname, nchunk, outw, swcol, fo_outer in qspecs:
            swmark = self.c_ve.n

            ldm, vem, stm = [], [], []
            ve0 = self.c_ve.n
            ac0 = self.c_ac.n
            for c in range(nchunk):
                half = GU0 if c % 2 == 0 else GU1
                g = len(qstm)

                def f(sync, c=c, half=half):
                    if c < 2:
                        sync.wait_ge(self.c_ve.h, max(swmark, ve0))
                        sync.wait_ge(self.c_ac.h, ac0)
                    if c >= 2:
                        sync.wait_ge(self.c_ve.h, vem[c - 2])
                    self.s_sy.inc(sync.dma_start(
                        out=self.AR[:, half:half + outw],
                        in_=di[wname][c * 128:(c + 1) * 128, :]), 16)
                self.sec("sync", f)
                ldm.append(self.s_sy.n)

                def f(vector, c=c, half=half, g=g):
                    nseg = outw // 512
                    vector.wait_ge(self.s_sy.h, ldm[c])
                    if g >= 2:
                        vector.wait_ge(self.s_st.h, qstm[g - 2])
                    wp = self.BL_WP + ((c % 2) * 4096 if outw == 4096
                                       else (c % 4) * 2048)
                    tba = self.TH
                    tbb = self.TH2
                    for j in range(nseg):
                        src = self.AR[:, half + j * 512:half + (j + 1) * 512]
                        dst = self.AB[:, wp + j * 512:wp + (j + 1) * 512]
                        vector.tensor_scalar(self.T0[:], src,
                                             self.CONST[:, swcol:swcol + 1], None, ALU.mult)
                        vector.tensor_scalar(tba, self.T0[:], ROFF, None, ALU.add)
                        vector.tensor_scalar(tbb, tba, -ROFF, 1.0, ALU.add, ALU.min)
                        vector.tensor_scalar(dst, tbb, -1.0, None, ALU.max)
                    self.c_ve.inc(self.vfence(vector))
                self.sec("vector", f)
                vem.append(self.c_ve.n)

                def f(gpsimd, c=c):
                    gpsimd.wait_ge(self.c_ve.h, vem[c])
                    wp = self.BL_WP + ((c % 2) * 4096 if outw == 4096
                                       else (c % 4) * 2048)
                    buf = self.AB[:, wp:wp + outw]
                    if fo_outer:
                        dst = self.ds[sname].rearrange("p (a b) -> p a b", b=2048)[
                            :, :, c * 256:(c + 1) * 256]
                        src = buf.rearrange("p (a b) -> p a b", b=256)
                    else:
                        dst = self.ds[sname][:, c * 1024:(c + 1) * 1024]
                        src = buf
                    self.s_st.inc(gpsimd.dma_start(out=dst, in_=src), 16)
                self.sec("gpsimd", f)
                stm.append(self.s_st.n)
                qstm.append(self.s_st.n)

        sspecs = [("wq", "wsh_q", "wsl_q", 8, 1024, True),
                  ("wk", "wsh_k", "wsl_k", 8, 1024, True),
                  ("wv", "wsh_v", "wsl_v", 8, 1024, True),
                  ("wpq", "wsh_pq", "wsl_pq", 8, 1024, True),
                  ("wpk", "wsh_pk", "wsl_pk", 8, 1024, True),
                  ("w1", "ws_w1h", "ws_w1l", 8, 4096, True),
                  ("w2", "ws_w2h", "ws_w2l", 32, 1024, False)]
        for wname, hname, lname, nchunk, outw, fo_outer in sspecs:
            ldm, vem, stm = [], [], []
            ve0 = self.c_ve.n
            st0 = self.s_st.n
            for c in range(nchunk):
                half = GU0 if c % 2 == 0 else GU1
                hb = self.AB[:, (c % 2) * 4096:(c % 2) * 4096 + outw]
                lb = self.AB[:, 8192 + (c % 2) * 4096:8192 + (c % 2) * 4096 + outw]

                def f(sync, c=c, half=half):
                    if c < 2:
                        sync.wait_ge(self.c_ve.h, ve0)
                    if c >= 2:
                        sync.wait_ge(self.c_ve.h, vem[c - 2])
                    self.s_sy.inc(sync.dma_start(
                        out=self.AR[:, half:half + outw],
                        in_=di[wname][c * 128:(c + 1) * 128, :]), 16)
                self.sec("sync", f)
                ldm.append(self.s_sy.n)

                def f(vector, c=c, half=half, hb=hb, lb=lb):
                    vector.wait_ge(self.s_sy.h, ldm[c])
                    if c < 2:
                        vector.wait_ge(self.s_st.h, st0)
                    if c >= 2:
                        vector.wait_ge(self.s_st.h, stm[2 * (c - 2) + 1])
                    src = self.AR[:, half:half + outw]
                    vector.tensor_copy(hb, src)
                    self.c_ve.inc(vector.tensor_sub(lb, src, hb))
                self.sec("vector", f)
                vem.append(self.c_ve.n)

                def f(gpsimd, c=c, hb=hb, lb=lb):
                    gpsimd.wait_ge(self.c_ve.h, vem[c])
                    for name, buf in ((hname, hb), (lname, lb)):
                        if fo_outer:
                            nfo2 = outw * 8 // 2048
                            dst = self.ds[name].rearrange("p (a b) -> p a b", b=2048)[
                                :, :, c * 256:(c + 1) * 256]
                            src = buf.rearrange("p (a b) -> p a b", b=256)
                        else:
                            dst = self.ds[name][:, c * 1024:(c + 1) * 1024]
                            src = buf
                        self.s_st.inc(gpsimd.dma_start(out=dst, in_=src), 16)
                        stm.append(self.s_st.n)
                self.sec("gpsimd", f)
        self.barrier()

    def emit_sums(self, src_sl, w, tmpsq_sl, bank, bank2=5, waits=(), nch=8):
        acm = []

        def fsq(scalar):
            for sm, cnt in waits:
                scalar.wait_ge(sm.h, cnt)
            for c in range(nch):
                s = self.AR[:, src_sl + c * w:src_sl + c * w + w]
                d = self.AR[:, tmpsq_sl + c * w:tmpsq_sl + c * w + w]
                self.c_ac.inc(scalar.activation(d, s, AF.Square))
                acm.append(self.c_ac.n)
        self.sec("scalar", fsq)

        def fmm(tensor):
            ps = self.bank(bank, tensor)
            ps2 = self.bank(bank2, tensor)
            for sm, cnt in waits:
                tensor.wait_ge(sm.h, cnt)
            for c in range(nch):
                s = self.AR[:, src_sl + c * w:src_sl + c * w + w]
                tensor.matmul(ps[0:1, 0:w], self.CONST[:, 0:1], s,
                              start=(c == 0), stop=(c == nch - 1))
            mm = None
            for c in range(nch):
                tensor.wait_ge(self.c_ac.h, acm[c])
                d = self.AR[:, tmpsq_sl + c * w:tmpsq_sl + c * w + w]
                mm = tensor.matmul(ps2[0:1, 0:w], self.CONST[:, 0:1], d,
                                   start=(c == 0), stop=(c == nch - 1))
            self.c_pe.inc(mm)
        self.sec("tensor", fmm)
        return self.c_pe.n

    def emit_minmax(self, src_sl, w, bank, mxname, mnname, waits=(), nch=8):
        BCT = self.SL_BC

        def f(vector):
            for sm, cnt in waits:
                vector.wait_ge(sm.h, cnt)
            last = None
            for which in (0, 1):
                op = ALU.max if which == 0 else ALU.min
                b0 = BCT + (0 + 2 * which) * 512
                b1 = BCT + (1 + 2 * which) * 512
                cur = src_sl
                dsts = [b0, b1]
                for c in range(1, nch):
                    sc = self.AR[:, src_sl + c * w:src_sl + c * w + w]
                    d = dsts[(c - 1) % 2]
                    last = vector.tensor_tensor(self.AR[:, d:d + w],
                                                self.AR[:, cur:cur + w], sc, op)
                    cur = d
            self.c_ve.inc(last)
        self.sec("vector", f)
        fin = BCT + ((nch - 2) % 2) * 512
        self.emit_minmax_rows(fin, fin + 1024, w, bank, mxname, mnname)

    def emit_bcast(self, row, w, bank, waits=()):
        def f(tensor):
            ps = self.bank(bank, tensor)
            for sm, cnt in waits:
                tensor.wait_ge(sm.h, cnt)
            self.c_pe.inc(tensor.matmul(ps[:, 0:w], self.ONESR[:], row,
                                        start=True, stop=True))
        self.sec("tensor", f)
        return self.c_pe.n

    def emit_bcasts(self, specs, w, waits=()):
        nblk = (w + 127) // 128
        assert len(specs) * nblk <= 8

        def fv(vector):
            for sm, cnt in waits:
                vector.wait_ge(sm.h, cnt)
            vector.wait_ge(self.c_pe.h, self.c_pe.n)
            last = None
            for i, (tmt, bank) in enumerate(specs):
                for b in range(nblk):
                    last = vector.tensor_scalar(
                        self.REPB[:, (i * nblk + b) * 128:(i * nblk + b) * 128 + 128],
                        self.ONES128[:], tmt[:, b:b + 1], None, ALU.mult)
            self.c_ve.inc(last)
        self.sec("vector", fv)
        vem = self.c_ve.n

        def ft(tensor):
            mm = None
            for i, (tmt, bank) in enumerate(specs):
                ps = self.bank(bank, tensor)
                if i == 0:
                    tensor.wait_ge(self.c_ve.h, vem)
                for b in range(nblk):
                    mm = tensor.matmul(
                        ps[:, b * 128:b * 128 + 128],
                        self.REPB[:, (i * nblk + b) * 128:(i * nblk + b) * 128 + 128],
                        self.EYE[:], is_transpose=True)
            self.c_pe.inc(mm)
        self.sec("tensor", ft)
        return self.c_pe.n

    def emit_ln_stats(self, bank, w, pfx, minmax=False, bank2=5):
        nblk = (w + 127) // 128
        mu = self.tm(pfx + "_mu", nblk)
        rs = self.tm(pfx + "_rs", nblk)
        mrs = self.tm(pfx + "_mrs", nblk)
        t1 = self.tm(pfx + "_t1", nblk)
        t2 = self.tm(pfx + "_t2", nblk)

        def f(vector):
            vector.wait_ge(self.c_pe.h, self.c_pe.n)
            vector.tensor_copy(self.SR0[0:1, 0:w], self.PS[bank][0:1, 0:w])
            self.c_ve.inc(vector.tensor_copy(self.SR1[0:1, 0:w],
                                             self.PS[bank2][0:1, 0:w]))
        self.sec("vector", f)
        vem = self.c_ve.n

        def f(tensor):
            ps = self.PS[bank]
            tensor.wait_ge(self.c_ve.h, vem)
            mm = None
            for b in range(nblk):
                bw = min(128, w - b * 128)
                tensor.matmul(ps[0:bw, b:b + 1],
                              self.SR0[0:1, b * 128:b * 128 + bw],
                              self.EYE[0:1, 0:1], is_transpose=True)
                mm = tensor.matmul(ps[0:bw, 8 + b:9 + b],
                                   self.SR1[0:1, b * 128:b * 128 + bw],
                                   self.EYE[0:1, 0:1], is_transpose=True)
            self.c_pe.inc(mm)
        self.sec("tensor", f)
        pem = self.c_pe.n

        t3 = self.tm(pfx + "_t3", nblk)
        t4 = self.tm(pfx + "_t4", nblk)

        def f(vector):
            vector.wait_ge(self.c_pe.h, pem)
            vector.tensor_scalar(mu, self.PS[bank][:, 0:nblk], 1.0 / E, None, ALU.mult)
            vector.tensor_scalar(t3, self.PS[bank][:, 8:8 + nblk], 1.0 / E, None, ALU.mult)
            vector.drain()
            vector.tensor_tensor(t4, mu, mu, ALU.mult)
            vector.drain()
            vector.tensor_tensor(t1, t3, t4, ALU.subtract)
            vector.drain()
            vector.tensor_scalar(t2, t1, EPS_LN, None, ALU.add)
            vector.drain()
            self.c_ve.inc(vector.reciprocal(rs, t2))
        self.sec("vector", f)
        self.set_guard(bank, self.c_ve)
        self.set_guard(bank2, self.c_ve)

        def f(scalar):
            scalar.wait_ge(self.c_ve.h, self.c_ve.n)
            self.c_ac.inc(scalar.activation(rs, rs, AF.Sqrt))
        self.sec("scalar", f)

        def f(vector):
            vector.wait_ge(self.c_ac.h, self.c_ac.n)
            last = vector.tensor_tensor(mrs, mu, rs, ALU.mult)
            if minmax:
                mx = self.tm(pfx + "_mx", nblk)
                mn = self.tm(pfx + "_mn", nblk)
                cl = self.tm(pfx + "_cl", nblk)
                cc = self.tm(pfx + "_c", nblk)
                mc = self.tm(pfx + "_mc", nblk)
                vector.tensor_tensor(t1, mx, mu, ALU.subtract)
                vector.tensor_tensor(t2, mu, mn, ALU.subtract)
                vector.drain()
                vector.tensor_tensor(t3, t1, t2, ALU.max)
                vector.drain()
                vector.tensor_tensor(t4, t3, rs, ALU.mult)
                vector.drain()
                vector.tensor_scalar(cl, t4, 1e-5, None, ALU.max)
                vector.drain()
                vector.reciprocal(t1, cl)
                vector.drain()
                vector.tensor_tensor(t2, rs, t1, ALU.mult)
                vector.drain()
                vector.tensor_scalar(cc, t2, 127.0, None, ALU.mult)
                vector.drain()
                last = vector.tensor_tensor(mc, mu, cc, ALU.mult)
            self.c_ve.inc(last)
        self.sec("vector", f)

    def emit_quant(self, src_sl, w, uq_off, pfx, bankc, bankmc, nch=8):
        nblk = (w + 127) // 128
        self.emit_bcasts([(self.tm(pfx + "_c", nblk), bankc),
                          (self.tm(pfx + "_mc", nblk), bankmc)], w,
                         waits=[(self.c_ve, self.c_ve.n)])
        pem = self.c_pe.n

        def f(vector):
            vector.wait_ge(self.c_pe.h, pem)
            dr = w < 128
            tb = self.TH
            last = None
            for c in range(nch):
                sc = self.AR[:, src_sl + c * w:src_sl + c * w + w]
                uqc = self.AB[:, uq_off + c * w:uq_off + c * w + w]
                vector.tensor_tensor(self.T0[:, 0:w], sc, self.PS[bankc][:, 0:w], ALU.mult)
                if dr:
                    vector.drain()
                vector.scalar_tensor_tensor(tb[:, 0:w], self.T0[:, 0:w], ROFF,
                                            self.PS[bankmc][:, 0:w], ALU.add, ALU.subtract)
                if dr:
                    vector.drain()
                last = vector.tensor_scalar(uqc, tb[:, 0:w], -ROFF, 127.0,
                                            ALU.add, ALU.min)
            self.c_ve.inc(last)
        self.sec("vector", f)
        self.set_guard(bankc, self.c_ve)
        self.set_guard(bankmc, self.c_ve)
        return self.c_ve.n

    def emit_minmax_rows(self, trx_sl, trn_sl, w, bank, mxname, mnname, waits=()):
        nblk = (w + 127) // 128
        for sl, name, op in ((trx_sl, mxname, ALU.max), (trn_sl, mnname, ALU.min)):
            def ftr(tensor, sl=sl):
                ps = self.bank(bank, tensor)
                for sm, cnt in waits:
                    tensor.wait_ge(sm.h, cnt)
                tensor.wait_ge(self.c_ve.h, self.c_ve.n)
                mm = None
                for b in range(nblk):
                    bw = min(128, w - b * 128)
                    mm = tensor.matmul(ps[0:bw, b * 128:b * 128 + 128],
                                       self.AR[:, sl + b * 128:sl + b * 128 + bw],
                                       self.EYE[:], is_transpose=True)
                self.c_pe.inc(mm)
            self.sec("tensor", ftr)
            pem = self.c_pe.n

            def frd(vector, name=name, op=op):
                vector.wait_ge(self.c_pe.h, pem)
                dst = self.tm(name, nblk)
                for b in range(nblk):
                    bw = min(128, w - b * 128)
                    vector.tensor_reduce(dst[0:bw, b:b + 1],
                                         self.PS[bank][0:bw, b * 128:b * 128 + 128],
                                         AX.X, op)
                vector.drain()
                self.c_ve.inc(self.vfence(vector))
            self.sec("vector", frd)
            self.set_guard(bank, self.c_ve)

    def emit_ffn(self, uq_off, uq_stride, hv, w, cl_name, ydst_sl, ysrc_sl, ystride):
        BC = self.SL_BC
        TRXa, TRXb = BC + 1024, BC + 1280
        TRNa, TRNb = BC + 1536, BC + 1792
        GU = self.SL_GU
        nblk = (w + 127) // 128
        cl = self.tm(cl_name + "_cl")[:, 2 * hv:2 * hv + nblk]

        rg = self.tm("ffn_rg", nblk)
        ru = self.tm("ffn_ru", nblk)

        def f(vector):
            vector.wait_ge(self.c_ac.h, self.c_ac.n)
            vector.tensor_scalar(rg, cl, self.CONST[:, 2:3], None, ALU.mult)
            vector.tensor_scalar(ru, cl, self.CONST[:, 3:4], None, ALU.mult)
            vector.drain()
            self.c_ve.inc(self.vfence(vector))
        self.sec("vector", f)
        self.emit_bcasts([(rg, 4), (ru, 5)], w, waits=[(self.c_ve, self.c_ve.n)])
        pem = self.c_pe.n

        def f(vector):
            vector.wait_ge(self.c_pe.h, pem)
            vector.tensor_copy(self.AR[:, BC:BC + w], self.PS[4][:, 0:w])
            self.c_ve.inc(vector.tensor_copy(self.AR[:, BC + 256:BC + 256 + w],
                                             self.PS[5][:, 0:w]))
        self.sec("vector", f)
        self.set_guard(4, self.c_ve)
        self.set_guard(5, self.c_ve)

        ldm = {}
        pemk = {}
        vemk = {}
        gcmk = {}
        acmk = {}
        wp0 = self.wp_pe
        ve_uq = self.c_ve.n
        for fo2 in range(16):
            wpg = self.BL_WP + (fo2 % 2) * 2048
            wpu = self.BL_WP + 4096 + (fo2 % 2) * 2048

            def f(sync, fo2=fo2, wpg=wpg, wpu=wpu):
                if fo2 < 2:
                    sync.wait_ge(self.c_pe.h, wp0)
                if fo2 >= 2:
                    sync.wait_ge(self.c_pe.h, pemk[(fo2 - 2, 1)])
                blk = self.ds["ws_wgq"][:, fo2 * 2048:(fo2 + 1) * 2048]
                self.s_sy.inc(sync.dma_start(out=self.AB[:, wpg:wpg + 2048], in_=blk), 16)
                blk = self.ds["ws_wuq"][:, fo2 * 2048:(fo2 + 1) * 2048]
                self.s_sy.inc(sync.dma_start(out=self.AB[:, wpu:wpu + 2048], in_=blk), 16)
            self.sec("sync", f)
            ldm[fo2] = self.s_sy.n

            MI = self.SL_MISC

            def mmx(vector, fo):
                guc = self.AR[:, GU + fo * w:GU + fo * w + w]
                xd = (TRXa, TRXb)[fo % 2]
                xs = (TRXa, TRXb)[(fo - 1) % 2]
                nd = (TRNa, TRNb)[fo % 2]
                ns = (TRNa, TRNb)[(fo - 1) % 2]
                if fo == 0:
                    vector.tensor_copy(self.AR[:, xd:xd + w], guc)
                    return vector.tensor_copy(self.AR[:, nd:nd + w], guc)
                vector.tensor_tensor(self.AR[:, xd:xd + w],
                                     self.AR[:, xs:xs + w], guc, ALU.max)
                return vector.tensor_tensor(self.AR[:, nd:nd + w],
                                            self.AR[:, ns:ns + w], guc, ALU.min)

            for fop in range(2):
                fo = fo2 * 2 + fop
                bg, bu = fo % 4, 4 + fo % 4
                j = fo % 2

                def f(tensor, fo2=fo2, fop=fop, fo=fo, bg=bg, bu=bu, wpg=wpg, wpu=wpu):
                    psg = self.bank(bg, tensor)
                    psu = self.bank(bu, tensor)
                    tensor.wait_ge(self.s_sy.h, ldm[fo2])
                    if fo == 0:
                        tensor.wait_ge(self.c_ve.h, ve_uq)
                    for fi in range(8):
                        rhs = self.AB[:, uq_off + fi * uq_stride + hv * w:
                                      uq_off + fi * uq_stride + hv * w + w]
                        tensor.matmul(psg[:, 0:w],
                                      self.AB[:, wpg + fi * 256 + fop * 128:
                                              wpg + fi * 256 + fop * 128 + 128],
                                      rhs, start=(fi == 0), stop=(fi == 7))
                    mm = None
                    for fi in range(8):
                        rhs = self.AB[:, uq_off + fi * uq_stride + hv * w:
                                      uq_off + fi * uq_stride + hv * w + w]
                        mm = tensor.matmul(psu[:, 0:w],
                                           self.AB[:, wpu + fi * 256 + fop * 128:
                                                   wpu + fi * 256 + fop * 128 + 128],
                                           rhs, start=(fi == 0), stop=(fi == 7))
                    self.c_pe.inc(mm)
                self.sec("tensor", f)
                pemk[(fo2, fop)] = self.c_pe.n

                def f(vector, fo=fo, bg=bg, bu=bu):
                    jq = fo % 4
                    vector.wait_ge(self.c_pe.h, pemk[(fo // 2, fo % 2)])
                    if fo >= 4:
                        vector.wait_ge(self.c_gc.h, gcmk[fo - 4])
                    if jq < 2:
                        usc = self.T0[:, jq * w:jq * w + w]
                    else:
                        usc = self.AR[:, BC + 512 + (jq - 2) * w:
                                      BC + 512 + (jq - 2) * w + w]
                    vector.tensor_tensor(self.AR[:, MI + jq * w:MI + jq * w + w],
                                         self.PS[bg][:, 0:w],
                                         self.AR[:, BC:BC + w], ALU.mult)
                    last = vector.tensor_tensor(usc, self.PS[bu][:, 0:w],
                                                self.AR[:, BC + 256:BC + 256 + w],
                                                ALU.mult)
                    if fo >= 4:
                        last = mmx(vector, fo - 4)
                    self.c_ve.inc(last)
                self.sec("vector", f)
                vemk[fo] = self.c_ve.n
                self.set_guard(bg, self.c_ve)
                self.set_guard(bu, self.c_ve)

                if fo % 4 == 3:
                    def f(scalar, fo=fo):
                        scalar.wait_ge(self.c_ve.h, vemk[fo])
                        if fo >= 7:
                            scalar.wait_ge(self.c_gc.h, gcmk[fo - 4])
                        self.c_ac.inc(scalar.activation(
                            self.AR[:, self.SL_NORM:self.SL_NORM + 4 * w],
                            self.AR[:, MI:MI + 4 * w], AF.Silu))
                    self.sec("scalar", f)
                    for k in range(4):
                        acmk[fo - 3 + k] = self.c_ac.n

                    def f(gpsimd, fo=fo, acm=self.c_ac.n):
                        gpsimd.wait_ge(self.c_ac.h, acm)
                        last = None
                        for k in range(4):
                            fk = fo - 3 + k
                            jk = fk % 4
                            if jk < 2:
                                usc = self.T0[:, jk * w:jk * w + w]
                            else:
                                usc = self.AR[:, BC + 512 + (jk - 2) * w:
                                              BC + 512 + (jk - 2) * w + w]
                            last = gpsimd.tensor_tensor(
                                self.AR[:, GU + fk * w:GU + fk * w + w],
                                self.AR[:, self.SL_NORM + jk * w:
                                        self.SL_NORM + jk * w + w],
                                usc, ALU.mult)
                        self.c_gc.inc(last)
                    self.sec("gpsimd", f)
                    for k in range(4):
                        gcmk[fo - 3 + k] = self.c_gc.n

        self.wp_pe = pemk[(15, 1)]

        def f(vector):
            vector.wait_ge(self.c_gc.h, self.c_gc.n)
            for k in (28, 29, 30, 31):
                mmx(vector, k)
            self.c_ve.inc(self.vfence(vector))
        self.sec("vector", f)
        self.emit_minmax_rows((TRXa, TRXb)[31 % 2], (TRNa, TRNb)[31 % 2], w, 6,
                              "ffn_gmx", "ffn_gmn")
        gmx = self.tm("ffn_gmx", nblk)
        gmn = self.tm("ffn_gmn", nblk)
        t1 = self.tm("ffn_t1", nblk)
        cl2 = self.tm("ffn_cl2", nblk)
        s2 = self.tm("ffn_s2", nblk)
        rd = self.tm("ffn_rd", nblk)

        t2f = self.tm("ffn_t2", nblk)

        def f(vector):
            vector.wait_ge(self.c_ve.h, self.c_ve.n)
            vector.tensor_scalar(t1, gmn, -1.0, None, ALU.mult)
            vector.drain()
            vector.tensor_tensor(t2f, gmx, t1, ALU.max)
            vector.drain()
            vector.tensor_scalar(cl2, t2f, 1e-5, None, ALU.max)
            vector.drain()
            vector.reciprocal(t1, cl2)
            vector.drain()
            vector.tensor_scalar(s2, t1, 127.0, None, ALU.mult)
            vector.tensor_scalar(rd, cl2, self.CONST[:, 4:5], None, ALU.mult)
            vector.drain()
            self.c_ve.inc(self.vfence(vector))
        self.sec("vector", f)
        self.emit_bcasts([(s2, 4), (rd, 5)], w, waits=[(self.c_ve, self.c_ve.n)])
        pem = self.c_pe.n

        def f(vector):
            vector.wait_ge(self.c_pe.h, pem)
            vector.tensor_copy(self.AR[:, BC + 512:BC + 512 + w], self.PS[4][:, 0:w])
            self.c_ve.inc(vector.tensor_copy(self.AR[:, BC + 768:BC + 768 + w],
                                             self.PS[5][:, 0:w]))
        self.sec("vector", f)
        self.set_guard(4, self.c_ve)
        self.set_guard(5, self.c_ve)

        dldm = {}
        dvem = {}
        dpem = {}
        wp0d = self.wp_pe
        for fi in range(32):
            wpd = self.BL_WP + (fi % 2) * 2048

            def f(sync, fi=fi, wpd=wpd):
                if fi < 2:
                    sync.wait_ge(self.c_pe.h, wp0d)
                if fi >= 2:
                    sync.wait_ge(self.c_pe.h, dpem[fi - 2])
                blk = self.ds["ws_wdq"][:, fi * 1024:(fi + 1) * 1024]
                self.s_sy.inc(sync.dma_start(out=self.AB[:, wpd:wpd + 1024], in_=blk), 16)
            self.sec("sync", f)
            dldm[fi] = self.s_sy.n

            def f(vector, fi=fi):
                if fi >= 4:
                    vector.wait_ge(self.c_pe.h, dpem[fi - 4])
                dr = w < 128
                guc = self.AR[:, GU + fi * w:GU + fi * w + w]
                vector.tensor_tensor(self.T0[:, 0:w], guc,
                                     self.AR[:, BC + 512:BC + 512 + w], ALU.mult)
                if dr:
                    vector.drain()
                th4 = self.TH if (fi // 2) % 2 == 0 else self.TH2
                tb = th4[:, (fi % 2) * 256:(fi % 2) * 256 + w]
                vector.tensor_scalar(tb, self.T0[:, 0:w], ROFF, None, ALU.add)
                if dr:
                    vector.drain()
                b3 = self.AB[:, self.BL_T + (fi % 4) * 256:self.BL_T + (fi % 4) * 256 + w]
                self.c_ve.inc(vector.tensor_scalar(b3, tb, -ROFF, 127.0,
                                                   ALU.add, ALU.min))
            self.sec("vector", f)
            dvem[fi] = self.c_ve.n

            def f(tensor, fi=fi, wpd=wpd):
                if fi == 0:
                    for b in range(8):
                        self.bank(b, tensor)
                tensor.wait_ge(self.s_sy.h, dldm[fi])
                tensor.wait_ge(self.c_ve.h, dvem[fi])
                b3 = self.AB[:, self.BL_T + (fi % 4) * 256:self.BL_T + (fi % 4) * 256 + w]
                mm = None
                for fo in range(8):
                    ps = self.PS[fo]
                    mm = tensor.matmul(ps[:, 0:w],
                                       self.AB[:, wpd + fo * 128:wpd + (fo + 1) * 128],
                                       b3, start=(fi == 0), stop=(fi == 31))
                self.c_pe.inc(mm)
            self.sec("tensor", f)
            dpem[fi] = self.c_pe.n

        self.wp_pe = dpem[31]

        for h2 in range(2):
            def f(vector, h2=h2):
                vector.wait_ge(self.c_pe.h, self.c_pe.n)
                dr = w < 128
                last = None
                for fo in range(h2 * 4, h2 * 4 + 4):
                    ps = self.PS[fo][:, 0:w]
                    vector.tensor_tensor(self.T0[:, 256:256 + w], ps,
                                         self.AR[:, BC + 768:BC + 768 + w], ALU.mult)
                    if dr:
                        vector.drain()
                    ysrc = self.AR[:, ysrc_sl + fo * ystride + hv * w:
                                   ysrc_sl + fo * ystride + hv * w + w]
                    ydst = self.AR[:, ydst_sl + fo * ystride + hv * w:
                                   ydst_sl + fo * ystride + hv * w + w]
                    last = vector.tensor_tensor(ydst, self.T0[:, 256:256 + w],
                                                ysrc, ALU.add)
                self.c_ve.inc(last)
            self.sec("vector", f)
            for b in range(h2 * 4, h2 * 4 + 4):
                self.set_guard(b, self.c_ve)

    def emit_proj(self, hname, lname, rhsh_off, rhsl_off, w, out_cb, waits=()):
        ldm = {}
        pemk = {}
        wp0 = self.wp_pe
        for fo2 in range(4):
            wph = self.BL_WP + (fo2 % 2) * 2048
            wpl = self.BL_WP + 4096 + (fo2 % 2) * 2048

            def f(sync, fo2=fo2, wph=wph, wpl=wpl):
                if fo2 < 2:
                    sync.wait_ge(self.c_pe.h, wp0)
                else:
                    sync.wait_ge(self.c_pe.h, pemk[(fo2 - 2, 1)])
                blk = self.ds[hname][:, fo2 * 2048:(fo2 + 1) * 2048]
                self.s_sy.inc(sync.dma_start(out=self.AB[:, wph:wph + 2048], in_=blk), 16)
                blk = self.ds[lname][:, fo2 * 2048:(fo2 + 1) * 2048]
                self.s_sy.inc(sync.dma_start(out=self.AB[:, wpl:wpl + 2048], in_=blk), 16)
            self.sec("sync", f)
            ldm[fo2] = self.s_sy.n

            for fop in range(2):
                fo = fo2 * 2 + fop
                bk = fo % 2

                def f(tensor, fo2=fo2, fop=fop, bk=bk, wph=wph, wpl=wpl):
                    ps = self.bank(bk, tensor)
                    tensor.wait_ge(self.s_sy.h, ldm[fo2])
                    if fo2 == 0 and fop == 0:
                        for sm, cnt in waits:
                            tensor.wait_ge(sm.h, cnt)
                    mm = None
                    for fi in range(8):
                        xh = self.AB[:, rhsh_off + fi * w:rhsh_off + fi * w + w]
                        xl = self.AB[:, rhsl_off + fi * w:rhsl_off + fi * w + w]
                        wh = self.AB[:, wph + fi * 256 + fop * 128:
                                     wph + fi * 256 + fop * 128 + 128]
                        wl = self.AB[:, wpl + fi * 256 + fop * 128:
                                     wpl + fi * 256 + fop * 128 + 128]
                        tensor.matmul(ps[:, 0:w], wh, xh, start=(fi == 0), stop=False)
                        tensor.matmul(ps[:, 0:w], wl, xh, start=False, stop=False)
                        mm = tensor.matmul(ps[:, 0:w], wh, xl, start=False,
                                           stop=(fi == 7))
                    self.c_pe.inc(mm)
                self.sec("tensor", f)
                pemk[(fo2, fop)] = self.c_pe.n
                out_cb(fo, bk, self.c_pe.n)
        self.wp_pe = pemk[(3, 1)]

    def emit_ln_apply(self, src_sl, dst_sl, w, rw, gcol=None, bcol=None, nch=8,
                      extra_cb=None):
        nblk = (w + 127) // 128
        self.emit_bcasts([(self.tm(rw + "_rs", nblk), 2),
                          (self.tm(rw + "_mrs", nblk), 3)], w,
                         waits=[(self.c_ve, self.c_ve.n)])
        pem = self.c_pe.n

        def f(vector):
            vector.wait_ge(self.c_pe.h, pem)
            dr = w < 128
            last = None
            for c in range(nch):
                s = self.AR[:, src_sl + c * w:src_sl + c * w + w]
                d = self.AR[:, dst_sl + c * w:dst_sl + c * w + w]
                vector.tensor_tensor(self.T0[:, 0:w], s, self.PS[2][:, 0:w], ALU.mult)
                if dr:
                    vector.drain()
                if gcol is None:
                    last = vector.tensor_tensor(d, self.T0[:, 0:w], self.PS[3][:, 0:w],
                                                ALU.subtract)
                else:
                    vector.tensor_tensor(self.T1[:, 0:w], self.T0[:, 0:w],
                                         self.PS[3][:, 0:w], ALU.subtract)
                    if dr:
                        vector.drain()
                    last = vector.tensor_scalar(d, self.T1[:, 0:w],
                                                gcol[:, c:c + 1], bcol[:, c:c + 1],
                                                ALU.mult, ALU.add)
                if extra_cb is not None:
                    extra_cb(vector, c, d)
            self.c_ve.inc(last)
        self.sec("vector", f)
        self.set_guard(2, self.c_ve)
        self.set_guard(3, self.c_ve)
        return self.c_ve.n

    def emit_split(self, src_sl, hi_off, lo_off, w, nch=8):
        vem = []

        def f(vector):
            for c in range(nch):
                s = self.AR[:, src_sl + c * w:src_sl + c * w + w]
                hb = self.AB[:, hi_off + c * w:hi_off + c * w + w]
                self.c_ve.inc(vector.tensor_copy(hb, s))
                vem.append(self.c_ve.n)
        self.sec("vector", f)

        def f(gpsimd):
            last = None
            for c in range(nch):
                gpsimd.wait_ge(self.c_ve.h, vem[c])
                s = self.AR[:, src_sl + c * w:src_sl + c * w + w]
                hb = self.AB[:, hi_off + c * w:hi_off + c * w + w]
                lb = self.AB[:, lo_off + c * w:lo_off + c * w + w]
                last = gpsimd.tensor_sub(lb, s, hb)
            self.c_gc.inc(last)
        self.sec("gpsimd", f)
        return self.c_gc.n

    def emit_store(self, slot, sname, c0, w, waits=()):
        def f(gpsimd):
            for sm, cnt in waits:
                gpsimd.wait_ge(sm.h, cnt)
            src = self.AR[:, slot:slot + 8 * w].rearrange("p (c t) -> p c t", t=w)
            dst = self.fm(self.ds[sname], c0, w)
            self.s_st.inc(gpsimd.dma_start(out=dst, in_=src), 16)
        self.sec("gpsimd", f)
        return self.s_st.n

    def emit_load(self, sname_or_ap, slot, c0, w, eng="gpsimd", is_input=False,
                  waits=()):
        sem = {"gpsimd": self.s_gp, "sync": self.s_sy, "scalar": self.s_sc}[eng]

        def f(e):
            for sm, cnt in waits:
                e.wait_ge(sm.h, cnt)
            dram = self.di[sname_or_ap] if is_input else self.ds[sname_or_ap]
            src = self.fm(dram, c0, w)
            dst = self.AR[:, slot:slot + 8 * w].rearrange("p (c t) -> p c t", t=w)
            sem.inc(e.dma_start(out=dst, in_=src), 16)
        self.sec(eng, f)
        return sem.n

    def emit_sweep1(self):
        tiles = range(NT + 1) if _S1_TILES is None else _S1_TILES
        for t in tiles:
            mini = (t == NT)
            w = HALO if mini else TT
            c0 = NT * TT if mini else t * TT
            self.emit_s1_tile(t, c0, w, mini)

            def f(vector):
                vector.wait_ge(self.s_st.h, self.s_st.n)
                vector.wait_ge(self.s_gp.h, self.s_gp.n)
            self.sec("vector", f)
        self.barrier()

    def emit_s1_tile(self, t, c0, w, mini):
        SX, SY, SV = self.SL_X, self.SL_Y1, self.SL_V
        RAW, NORM, GU = self.SL_RAW, self.SL_NORM, self.SL_GU
        B = self.BIAS

        ldx = self.emit_load("x", SX, c0, w, eng="scalar", is_input=True,
                             waits=[(self.c_ve, self.c_ve.n),
                                    (self.c_pe, self.c_pe.n)])

        self.emit_sums(SX, w, GU, 6, waits=[(self.s_sc, ldx)])
        self.emit_minmax(SX, w, 7, "ln1_mx", "ln1_mn", waits=[(self.s_sc, ldx)])
        self.emit_ln_stats(6, w, "ln1", minmax=True)
        self.emit_quant(SX, w, self.BL_UQ, "ln1", 2, 3)

        if mini:
            self.emit_ffn(self.BL_UQ, w, 0, w, "ln1", SY, SX, w)
        else:
            for hv in range(2):
                self.emit_ffn(self.BL_UQ, w, hv, HT, "ln1", SY, SX, w)
        if not mini:
            st_y1 = self.emit_store(SY, "s_y1", c0, w, waits=[(self.c_ve, self.c_ve.n)])
        if _S1_FFN_ONLY:
            return

        self.emit_sums(SY, w, GU, 6, waits=[(self.c_ve, self.c_ve.n)])
        self.emit_ln_stats(6, w, "lnv", minmax=False)
        self.emit_ln_apply(SY, SV, w, "lnv")
        if not mini:
            st_v = self.emit_store(SV, "s_v", c0, w, waits=[(self.c_ve, self.c_ve.n)])
        self.emit_split(SV, self.BL_VH, self.BL_VL, w)

        def mkevac(dst_sl, bias_c):
            def cb(fo, bk, pem):
                def f(vector):
                    vector.wait_ge(self.c_pe.h, pem)
                    vector.wait_ge(self.c_gc.h, self.c_gc.n)
                    d = self.AR[:, dst_sl + fo * w:dst_sl + fo * w + w]
                    self.c_ve.inc(vector.tensor_scalar(
                        d, self.PS[bk][:, 0:w], B[:, bias_c + fo:bias_c + fo + 1],
                        None, ALU.add))
                self.sec("vector", f)
                self.set_guard(bk, self.c_ve)
            return cb

        self.emit_proj("wsh_q", "wsl_q", self.BL_VH, self.BL_VL, w, mkevac(RAW, 0),
                       waits=[(self.c_ve, self.c_ve.n), (self.c_gc, self.c_gc.n)])
        self.emit_sums(RAW, w, GU, 6, waits=[(self.c_ve, self.c_ve.n)])
        self.emit_ln_stats(6, w, "lnq", minmax=False)
        self.emit_ln_apply(RAW, NORM, w, "lnq", gcol=B[:, 40:48], bcol=B[:, 48:56])
        if not mini:
            def f(vector, t=t):
                srch = 8 * (t % 2)
                dsth = 8 * ((t + 1) % 2)
                for c in range(8):
                    s = self.AR[:, NORM + c * w:NORM + c * w + w]
                    vector.reduce_sum(self.T0[:, 0:1], s, AX.X)
                    vector.drain()
                    vector.tensor_tensor(self.QG[:, dsth + c:dsth + c + 1],
                                         self.QG[:, srch + c:srch + c + 1],
                                         self.T0[:, 0:1], ALU.add)
                vector.drain()
                self.c_ve.inc(self.vfence(vector))
            self.sec("vector", f)

        if not mini:
            self.emit_proj("wsh_k", "wsl_k", self.BL_VH, self.BL_VL, w, mkevac(RAW, 8))
            self.emit_sums(RAW, w, GU, 6, waits=[(self.c_ve, self.c_ve.n)])
            self.emit_ln_stats(6, w, "lnk", minmax=False)
            self.emit_ln_apply(RAW, GU + 4096, w, "lnk", gcol=B[:, 56:64], bcol=B[:, 64:72])
            st_k = self.emit_store(GU + 4096, "s_kk", c0, w, waits=[(self.c_ve, self.c_ve.n)])

            self.emit_proj("wsh_v", "wsl_v", self.BL_VH, self.BL_VL, w, mkevac(RAW, 16))
            self.emit_sums(RAW, w, GU, 6, waits=[(self.c_ve, self.c_ve.n)])
            self.emit_ln_stats(6, w, "lnu", minmax=False)
            self.emit_ln_apply(RAW, RAW, w, "lnu", gcol=B[:, 72:80], bcol=B[:, 80:88])
            st_vv = self.emit_store(RAW, "s_vv", c0, w, waits=[(self.c_ve, self.c_ve.n)])

        self.emit_split(NORM, self.BL_VH, self.BL_VL, w)
        stv = self.s_st.n

        def mkphi(act_sl, dst_sl, bias_c):
            def cb(fo, bk, pem):
                def f(scalar):
                    scalar.wait_ge(self.c_pe.h, pem)
                    scalar.wait_ge(self.s_st.h, stv)
                    a_ = self.AR[:, act_sl + fo * w:act_sl + fo * w + w]
                    self.c_ac.inc(scalar.activation(
                        a_, self.PS[bk][:, 0:w], AF.Tanh,
                        bias=B[:, bias_c + fo:bias_c + fo + 1], scale=1.0))
                self.sec("scalar", f)
                acm = self.c_ac.n

                def f(vector, acm=acm):
                    vector.wait_ge(self.c_ac.h, acm)
                    a_ = self.AR[:, act_sl + fo * w:act_sl + fo * w + w]
                    d = self.AR[:, dst_sl + fo * w:dst_sl + fo * w + w]
                    self.c_ve.inc(vector.tensor_scalar(d, a_, 1.0, None, ALU.add))
                self.sec("vector", f)
                self.set_guard(bk, self.c_ve)
            return cb

        self.emit_proj("wsh_pq", "wsl_pq", self.BL_VH, self.BL_VL, w,
                       mkphi(NORM, RAW, 24),
                       waits=[(self.c_ve, self.c_ve.n), (self.c_gc, self.c_gc.n)])
        self.emit_store(RAW, "s_pq", c0, w, waits=[(self.c_ve, self.c_ve.n)])

        if not mini:
            self.emit_split(GU + 4096, self.BL_VH, self.BL_VL, w)
            self.emit_proj("wsh_pk", "wsl_pk", self.BL_VH, self.BL_VL, w,
                           mkphi(GU + 4096, NORM, 32),
                           waits=[(self.c_ve, self.c_ve.n), (self.c_gc, self.c_gc.n)])
            self.emit_store(NORM, "s_fk", c0, w, waits=[(self.c_ve, self.c_ve.n)])

    def emit_qg_collective(self):
        groups = [[0, 1, 2, 3], [4, 5, 6, 7]]

        def f(gpsimd):
            self.s_st.inc(gpsimd.dma_start(out=self.ds["cb_qg_i"][:],
                                           in_=self.QG[:, 0:8]), 16)
            gpsimd.wait_ge(self.s_st.h, self.s_st.n)
            self.c_cc.inc(gpsimd.collective_compute(
                "AllReduce", ALU.add, replica_groups=groups,
                ins=[self.ds["cb_qg_i"][:]], outs=[self.ds["cb_qg_o"][:]]))
            gpsimd.wait_ge(self.c_cc.h, self.c_cc.n)
            self.s_gp.inc(gpsimd.dma_start(out=self.QG[:, 0:8],
                                           in_=self.ds["cb_qg_o"][:]), 16)
        self.sec("gpsimd", f)
        gpm = self.s_gp.n

        def f(vector):
            vector.wait_ge(self.s_gp.h, gpm)
            vector.tensor_scalar(self.QG[:, 8:16], self.QG[:, 0:8],
                                 1.0 / (LQ * 8.0), None, ALU.mult)
            vector.drain()
            self.c_ve.inc(self.vfence(vector))
        self.sec("vector", f)
        self.barrier()

    def kvcol(self, s):
        return (0 if s < 4 else 260) + 65 * (s % 4)

    def emit_scores_all(self, ldmark, first, dst_off):
        for c in range(8):
            bk = c % 2

            def f(tensor, c=c, bk=bk):
                ps = self.bank(bk, tensor)
                tensor.wait_ge(self.s_sy.h, ldmark)
                if first and c == 0:
                    tensor.wait_ge(self.c_ve.h, self.c_ve.n)
                self.c_pe.inc(tensor.matmul(
                    ps[0:2, :], self.QGE[:, 2 * c:2 * c + 2],
                    self.AR[:, self.SL_X + 512 * c:self.SL_X + 512 * c + 512],
                    start=True, stop=True))
            self.sec("tensor", f)
            pem = self.c_pe.n

            def f(vector, c=c, bk=bk, pem=pem):
                vector.wait_ge(self.c_pe.h, pem)
                self.c_ve.inc(vector.tensor_copy(
                    self.ESB[0:2, dst_off + 512 * c:dst_off + 512 * c + 512],
                    self.PS[bk][0:2, :]))
            self.sec("vector", f)
            self.set_guard(bk, self.c_ve)
        return self.c_ve.n

    def emit_sweep2(self):
        def f(vector):
            vector.memset(self.QGE[:], 0.0)
            vector.drain()
            for c in range(8):
                vector.tensor_copy(self.QGE[0:64, 2 * c:2 * c + 1],
                                   self.QG[0:64, 8 + c:9 + c])
                vector.tensor_copy(self.QGE[64:128, 2 * c + 1:2 * c + 2],
                                   self.QG[64:128, 8 + c:9 + c])
            vector.drain()
            self.c_ve.inc(self.vfence(vector))
        self.sec("vector", f)

        for t in range(NT):
            ld = self.emit_load("s_kk", self.SL_X, t * TT, TT, eng="sync")
            vem = self.emit_scores_all(ld, t == 0, 0)

            def f(vector, t=t, vem=vem):
                vector.wait_ge(self.c_ve.h, vem)
                srch = (0, 32)[t % 2]
                dsth = (0, 32)[(t + 1) % 2]
                for c in range(8):
                    vector.reduce_max(self.MX[:, 24:25],
                                      self.ESB[0:2, 512 * c:512 * c + 512], AX.X)
                    vector.drain()
                    if t == 0:
                        vector.tensor_copy(self.MX[:, dsth + c:dsth + c + 1],
                                           self.MX[:, 24:25])
                    else:
                        vector.tensor_tensor(self.MX[:, dsth + c:dsth + c + 1],
                                             self.MX[:, srch + c:srch + c + 1],
                                             self.MX[:, 24:25], ALU.max)
                vector.drain()
                self.c_ve.inc(self.vfence(vector))
            self.sec("vector", f)
            self.barrier(("sync", "vector", "tensor"))

        def f(vector):
            vector.tensor_scalar(self.MX[:, 8:16], self.MX[:, 0:8], -1.0, None, ALU.mult)
            vector.memset(self.MX[:, 16:24], 0.0)
            vector.memset(self.MX[:, 40:48], 0.0)
            vector.drain()
            self.c_ve.inc(self.vfence(vector))
        self.sec("vector", f)
        self.barrier()

        VTM = self.SL_NORM
        PKT = self.SL_NORM + 1040

        def f(vector):
            last = None
            for h in range(16):
                last = vector.memset(self.AR[:, VTM + 65 * h + 64:VTM + 65 * h + 65], 1.0)
            self.c_ve.inc(last)
        self.sec("vector", f)

        KVA = self.SL_RAW
        KVB = self.SL_RAW + 520

        def f(vector):
            self.c_ve.inc(vector.memset(self.AR[:, KVA:KVA + 520], 0.0))
        self.sec("vector", f)

        for t in range(NT):
            c0 = t * TT
            ldk = self.emit_load("s_kk", self.SL_X, c0, TT, eng="sync")
            ldf = self.emit_load("s_fk", self.SL_Y1, c0, TT, eng="scalar")
            ldv = self.emit_load("s_vv", self.SL_V, c0, TT, eng="gpsimd")
            vem_s = self.emit_scores_all(ldk, False, 0)

            def f(scalar, vem_s=vem_s):
                scalar.wait_ge(self.c_ve.h, vem_s)
                last = None
                for c in range(8):
                    sl = self.ESB[0:2, 512 * c:512 * c + 512]
                    last = scalar.activation(sl, sl, AF.Exp,
                                             bias=self.MX[:, 8 + c:9 + c], scale=1.0)
                self.c_ac.inc(last)
            self.sec("scalar", f)
            acm = self.c_ac.n

            def f(vector, acm=acm, t=t):
                vector.wait_ge(self.c_ac.h, acm)
                srch = (16, 40)[t % 2]
                dsth = (16, 40)[(t + 1) % 2]
                for c in range(8):
                    vector.reduce_sum(self.MX[:, 24:25],
                                      self.ESB[0:2, 512 * c:512 * c + 512], AX.X)
                    vector.drain()
                    vector.tensor_tensor(self.MX[:, dsth + c:dsth + c + 1],
                                         self.MX[:, srch + c:srch + c + 1],
                                         self.MX[:, 24:25], ALU.add)
                vector.drain()
                self.c_ve.inc(self.vfence(vector))
            self.sec("vector", f)
            vem_e = self.c_ve.n

            def f(tensor, vem_e=vem_e):
                ps = self.bank(1, tensor)
                tensor.wait_ge(self.c_ve.h, vem_e)
                mm = None
                for j in range(4):
                    for c in range(8):
                        mm = tensor.matmul(
                            ps[:, j * 16 + 2 * c:j * 16 + 2 * c + 2],
                            self.ESB[0:2, 512 * c + 128 * j:512 * c + 128 * j + 128],
                            self.EYE[0:2, 0:2], is_transpose=True)
                self.c_pe.inc(mm)
            self.sec("tensor", f)
            pem_t = self.c_pe.n

            def f(vector, pem_t=pem_t):
                vector.wait_ge(self.c_pe.h, pem_t)
                vector.tensor_copy(self.ETM[:], self.PS[1][:, 0:64])
                vector.drain()
                self.c_ve.inc(self.vfence(vector))
            self.sec("vector", f)
            self.set_guard(1, self.c_ve)

            for j in range(4):
                def f(tensor, j=j, ldf=ldf):
                    ps2 = self.bank(2, tensor)
                    ps3 = self.bank(3, tensor)
                    tensor.wait_ge(self.s_sc.h, ldf)
                    mm = None
                    for c in range(8):
                        ps = ps2 if c < 4 else ps3
                        mm = tensor.matmul(
                            ps[:, (c % 4) * 128:(c % 4) * 128 + 128],
                            self.AR[:, self.SL_Y1 + 512 * c + 128 * j:
                                    self.SL_Y1 + 512 * c + 128 * j + 128],
                            self.EYE[:], is_transpose=True)
                    self.c_pe.inc(mm)
                self.sec("tensor", f)
                pem_k = self.c_pe.n

                def f(vector, j=j, pem_k=pem_k):
                    vector.wait_ge(self.c_pe.h, pem_k)
                    last = None
                    for h in range(16):
                        c = h // 2
                        ps = self.PS[2 + c // 4]
                        sl = ps[:, 128 * (c % 4) + 64 * (h % 2):
                                128 * (c % 4) + 64 * (h % 2) + 64]
                        last = vector.tensor_scalar(
                            self.AR[:, PKT + 64 * h:PKT + 64 * h + 64], sl,
                            self.ETM[:, 16 * j + h:16 * j + h + 1], None, ALU.mult)
                    self.c_ve.inc(last)
                self.sec("vector", f)
                self.set_guard(2, self.c_ve)
                self.set_guard(3, self.c_ve)
                vem_pk = self.c_ve.n

                def f(tensor, j=j, ldv=ldv, vem_pk=vem_pk):
                    ps2 = self.bank(2, tensor)
                    ps3 = self.bank(3, tensor)
                    tensor.wait_ge(self.s_gp.h, ldv)
                    mm = None
                    for c in range(8):
                        ps = ps2 if c < 4 else ps3
                        mm = tensor.matmul(
                            ps[:, (c % 4) * 128:(c % 4) * 128 + 128],
                            self.AR[:, self.SL_V + 512 * c + 128 * j:
                                    self.SL_V + 512 * c + 128 * j + 128],
                            self.EYE[:], is_transpose=True)
                    self.c_pe.inc(mm)
                self.sec("tensor", f)
                pem_v = self.c_pe.n

                def f(vector, pem_v=pem_v):
                    vector.wait_ge(self.c_pe.h, pem_v)
                    last = None
                    for h in range(16):
                        c = h // 2
                        ps = self.PS[2 + c // 4]
                        sl = ps[:, 128 * (c % 4) + 64 * (h % 2):
                                128 * (c % 4) + 64 * (h % 2) + 64]
                        last = vector.tensor_copy(
                            self.AR[:, VTM + 65 * h:VTM + 65 * h + 64], sl)
                    self.c_ve.inc(last)
                self.sec("vector", f)
                self.set_guard(2, self.c_ve)
                self.set_guard(3, self.c_ve)
                vem_v = self.c_ve.n

                idx = t * 4 + j
                src_kv = (KVA, KVB)[idx % 2]
                dst_kv = (KVA, KVB)[(idx + 1) % 2]
                for rnd in range(2):
                    def f(tensor, j=j, vem_v=vem_v, rnd=rnd):
                        mm = None
                        for hh in range(8):
                            h = rnd * 8 + hh
                            pb = 64 * (h % 2)
                            ps = self.bank(hh, tensor)
                            if hh == 0 and rnd == 0:
                                tensor.wait_ge(self.c_ve.h, vem_v)
                            mm = tensor.matmul(
                                ps[pb:pb + 64, 0:65],
                                self.AR[:, PKT + 64 * h:PKT + 64 * h + 64],
                                self.AR[:, VTM + 65 * h:VTM + 65 * h + 65],
                                start=True, stop=True)
                        self.c_pe.inc(mm)
                    self.sec("tensor", f)
                    pem_kv = self.c_pe.n

                    def f(vector, rnd=rnd, src_kv=src_kv, dst_kv=dst_kv,
                          pem_kv=pem_kv):
                        vector.wait_ge(self.c_pe.h, pem_kv)
                        last = None
                        for hh in range(8):
                            h = rnd * 8 + hh
                            pb = 64 * (h % 2)
                            a_src = self.AR[pb:pb + 64,
                                            src_kv + self.kvcol(h // 2):
                                            src_kv + self.kvcol(h // 2) + 65]
                            a_dst = self.AR[pb:pb + 64,
                                            dst_kv + self.kvcol(h // 2):
                                            dst_kv + self.kvcol(h // 2) + 65]
                            last = vector.tensor_tensor(a_dst,
                                                        self.PS[hh][pb:pb + 64, 0:65],
                                                        a_src, ALU.add)
                        self.c_ve.inc(last)
                    self.sec("vector", f)
                    for hh in range(8):
                        self.set_guard(hh, self.c_ve)
            self.barrier(("sync", "vector", "scalar", "tensor", "gpsimd"))

        fin_kv = (KVA, KVB)[(NT * 4) % 2]

        def f(vector):
            vector.wait_ge(self.c_pe.h, self.c_pe.n)
            vector.tensor_copy(self.KVSB[:, 0:520], self.AR[:, fin_kv:fin_kv + 520])
            vector.tensor_copy(self.KVSB[0:2, 520:528], self.MX[:, 0:8])
            self.c_ve.inc(vector.tensor_copy(self.KVSB[0:2, 528:536], self.MX[:, 16:24]))
        self.sec("vector", f)

        groups = [[0, 1, 2, 3], [4, 5, 6, 7]]

        def f(gpsimd):
            gpsimd.wait_ge(self.c_ve.h, self.c_ve.n)
            self.s_st.inc(gpsimd.dma_start(out=self.ds["cb_ag_i"][:],
                                           in_=self.KVSB[:, 0:536]), 16)
            gpsimd.wait_ge(self.s_st.h, self.s_st.n)
            self.c_cc.inc(gpsimd.collective_compute(
                "AllGather", ALU.bypass, replica_groups=groups,
                ins=[self.ds["cb_ag_i"][:]], outs=[self.ds["cb_ag_o"][:]]))
            gpsimd.wait_ge(self.c_cc.h, self.c_cc.n)
            K4 = self.AR[:, self.SL_RAW:self.SL_RAW + 4 * 536]
            self.s_gp.inc(gpsimd.dma_start(
                out=K4.rearrange("p (a b) -> p a b", b=536),
                in_=self.ds["cb_ag_o"].rearrange("(a p) b -> p a b", p=128)), 16)
        self.sec("gpsimd", f)
        gpm = self.s_gp.n

        RAWs = self.SL_RAW
        C = self.COMB

        def f(vector):
            vector.wait_ge(self.s_gp.h, gpm)
            for i in range(4):
                vector.tensor_copy(C[:, 8 * i:8 * i + 8],
                                   self.AR[0:2, RAWs + 536 * i + 520:RAWs + 536 * i + 528])
            vector.drain()
            vector.tensor_tensor(self.MX[:, 48:56], C[:, 0:8], C[:, 8:16], ALU.max)
            vector.drain()
            vector.tensor_tensor(self.MX[:, 56:64], self.MX[:, 48:56], C[:, 16:24], ALU.max)
            vector.drain()
            vector.tensor_tensor(self.MX[:, 48:56], self.MX[:, 56:64], C[:, 24:32], ALU.max)
            vector.drain()
            vector.tensor_scalar(C[:, 40:48], self.MX[:, 48:56], 0.0, None, ALU.max)
            vector.drain()
            for i in range(4):
                vector.tensor_tensor(self.MX[:, 32:40], C[:, 8 * i:8 * i + 8],
                                     C[:, 40:48], ALU.subtract)
                vector.drain()
                vector.tensor_copy(C[:, 8 * i:8 * i + 8], self.MX[:, 32:40])
                vector.drain()
            self.c_ve.inc(self.vfence(vector))
        self.sec("vector", f)

        def f(scalar):
            scalar.wait_ge(self.c_ve.h, self.c_ve.n)
            scalar.activation(C[:, 0:32], C[:, 0:32], AF.Exp)
            self.c_ac.inc(scalar.activation(C[:, 48:56], C[:, 40:48], AF.Exp, scale=-1.0))
        self.sec("scalar", f)

        def f(vector):
            vector.wait_ge(self.c_ac.h, self.c_ac.n)
            zp = [C[:, 32:40], self.MX[:, 56:64]]
            vector.tensor_copy(zp[0], C[:, 48:56])
            vector.drain()
            for i in range(4):
                vector.tensor_tensor(self.MX[:, 48:56],
                                     self.AR[0:2, RAWs + 536 * i + 528:RAWs + 536 * i + 536],
                                     C[:, 8 * i:8 * i + 8], ALU.mult)
                vector.drain()
                vector.tensor_tensor(zp[(i + 1) % 2], zp[i % 2], self.MX[:, 48:56], ALU.add)
                vector.drain()
            vector.reciprocal(self.MX[:, 48:56], zp[0])
            vector.drain()
            vector.tensor_scalar(C[:, 56:64], self.MX[:, 48:56], float(LQ), None, ALU.mult)
            vector.drain()
            self.c_ve.inc(self.vfence(vector))
        self.sec("vector", f)

        def f(tensor):
            ps = self.bank(1, tensor)
            tensor.wait_ge(self.c_ve.h, self.c_ve.n)
            sel2 = self.SELP[0:2, 128:256]
            tensor.matmul(ps[:, 0:32], sel2, C[:, 0:32], start=True, stop=True)
            tensor.matmul(ps[:, 32:40], sel2, C[:, 48:56], start=True, stop=True)
            self.c_pe.inc(tensor.matmul(ps[:, 40:48], sel2, C[:, 56:64],
                                        start=True, stop=True))
        self.sec("tensor", f)
        pem_f = self.c_pe.n

        NB0 = self.SL_NORM
        NB1 = self.SL_NORM + 1040

        def f(vector):
            vector.wait_ge(self.c_pe.h, pem_f)
            F = self.PS[1]
            last = None
            for c in range(8):
                col = self.kvcol(c)
                a0 = self.AR[:, NB0 + 65 * c:NB0 + 65 * c + 65]
                a1 = self.AR[:, NB1 + 65 * c:NB1 + 65 * c + 65]
                k4 = [self.AR[:, RAWs + 536 * i + col:RAWs + 536 * i + col + 65]
                      for i in range(4)]
                vector.tensor_scalar(a0, k4[0], F[:, c:c + 1], None, ALU.mult)
                vector.drain()
                vector.scalar_tensor_tensor(a1, k4[1], F[:, 8 + c:9 + c], a0,
                                            ALU.mult, ALU.add)
                vector.drain()
                vector.scalar_tensor_tensor(a0, k4[2], F[:, 16 + c:17 + c], a1,
                                            ALU.mult, ALU.add)
                vector.drain()
                vector.scalar_tensor_tensor(a1, k4[3], F[:, 24 + c:25 + c], a0,
                                            ALU.mult, ALU.add)
                vector.drain()
                vector.scalar_tensor_tensor(a0[:, 64:65], self.PK0[:, c:c + 1],
                                            F[:, 32 + c:33 + c], a1[:, 64:65],
                                            ALU.mult, ALU.add)
                vector.drain()
                vector.tensor_scalar(self.KVSB[:, col:col + 64], a1[:, 0:64],
                                     F[:, 40 + c:41 + c], None, ALU.mult)
                last = vector.tensor_scalar(self.KVSB[:, col + 64:col + 65],
                                            a0[:, 64:65],
                                            F[:, 40 + c:41 + c], None, ALU.mult)
            vector.drain()
            self.c_ve.inc(self.vfence(vector))
        self.sec("vector", f)
        self.set_guard(1, self.c_ve)

        def f(vector):
            vector.memset(self.KSE[:], 0.0)
            vector.drain()
            for c in range(8):
                col = self.kvcol(c) + 64
                vector.tensor_copy(self.KSE[0:64, 2 * c:2 * c + 1],
                                   self.KVSB[0:64, col:col + 1])
                vector.tensor_copy(self.KSE[64:128, 2 * c + 1:2 * c + 2],
                                   self.KVSB[64:128, col:col + 1])
            vector.drain()
            self.c_ve.inc(self.vfence(vector))
        self.sec("vector", f)
        self.barrier()

    def emit_attn(self, pq_sl, pq_stride, pq_off, w, dst_fn, use_mask=False,
                  waits=()):
        for c in range(8):
            bn = c % 2
            bd = 2 + c % 2

            def f(tensor, c=c, bn=bn, bd=bd):
                ps5 = self.bank(5, tensor)
                if c == 0:
                    for sm, cnt in waits:
                        tensor.wait_ge(sm.h, cnt)
                self.c_pe.inc(tensor.matmul(
                    ps5[0:2, 0:w], self.KSE[:, 2 * c:2 * c + 2],
                    self.AR[:, pq_sl + c * pq_stride + pq_off:
                            pq_sl + c * pq_stride + pq_off + w],
                    start=True, stop=True))
            self.sec("tensor", f)
            pem_d = self.c_pe.n

            def f(vector, c=c, pem_d=pem_d):
                vector.wait_ge(self.c_pe.h, pem_d)
                vector.tensor_scalar(self.DEN[:, 0:w], self.PS[5][0:2, 0:w], EPS_ATT,
                                     None, ALU.add)
                if w < 128:
                    vector.drain()
                vector.reciprocal(self.DEN[:, 512:512 + w], self.DEN[:, 0:w])
                vector.drain()
                self.c_ve.inc(self.vfence(vector))
            self.sec("vector", f)
            self.set_guard(5, self.c_ve)
            vem_d = self.c_ve.n

            def f(tensor, c=c, bn=bn, bd=bd, vem_d=vem_d):
                psn = self.bank(bn, tensor)
                psd = self.bank(bd, tensor)
                tensor.wait_ge(self.c_ve.h, vem_d)
                for h in (2 * c, 2 * c + 1):
                    pb = 64 * (h % 2)
                    col = self.kvcol(c)
                    lhs = self.KVSB[pb:pb + 64, col:col + 64]
                    rhs = self.AR[pb:pb + 64, pq_sl + c * pq_stride + pq_off:
                                  pq_sl + c * pq_stride + pq_off + w]
                    tensor.matmul(psn[pb:pb + 64, 0:w], lhs, rhs, start=True, stop=True)
                self.c_pe.inc(tensor.matmul(psd[:, 0:w], self.SELP[0:2, 128:256],
                                            self.DEN[:, 512:512 + w],
                                            start=True, stop=True))
            self.sec("tensor", f)
            pem_n = self.c_pe.n

            def f(vector, c=c, bn=bn, bd=bd, pem_n=pem_n):
                vector.wait_ge(self.c_pe.h, pem_n)
                d = dst_fn(c)
                vector.tensor_copy(self.T1[:, 0:w], self.PS[bd][:, 0:w])
                if w < 128:
                    vector.drain()
                if use_mask:
                    vector.tensor_tensor(self.T0[:, 0:w], self.PS[bn][:, 0:w],
                                         self.T1[:, 0:w], ALU.mult)
                    vector.drain()
                    last = vector.tensor_tensor(d, self.T0[:, 0:w],
                                                self.MASKB[:, 0:w], ALU.mult)
                else:
                    last = vector.tensor_tensor(d, self.PS[bn][:, 0:w],
                                                self.T1[:, 0:w], ALU.mult)
                self.c_ve.inc(last)
            self.sec("vector", f)
            self.set_guard(bn, self.c_ve)
            self.set_guard(bd, self.c_ve)

    def emit_w1(self, w):
        ldm = {}
        pemk = {}
        self.acm_w1 = {}
        wp0 = self.wp_pe
        ve_rdy = self.c_ve.n
        gc_rdy = self.c_gc.n
        for fo2 in range(16):
            wph = self.BL_WP + (fo2 % 2) * 2048
            wpl = self.BL_WP + 4096 + (fo2 % 2) * 2048

            def f(sync, fo2=fo2, wph=wph, wpl=wpl):
                if fo2 < 2:
                    sync.wait_ge(self.c_pe.h, wp0)
                else:
                    sync.wait_ge(self.c_pe.h, pemk[(fo2 - 2, 1)])
                blk = self.ds["ws_w1h"][:, fo2 * 2048:(fo2 + 1) * 2048]
                self.s_sy.inc(sync.dma_start(out=self.AB[:, wph:wph + 2048], in_=blk), 16)
                blk = self.ds["ws_w1l"][:, fo2 * 2048:(fo2 + 1) * 2048]
                self.s_sy.inc(sync.dma_start(out=self.AB[:, wpl:wpl + 2048], in_=blk), 16)
            self.sec("sync", f)
            ldm[fo2] = self.s_sy.n

            for fop in range(2):
                fo = fo2 * 2 + fop
                bk = fo % 8

                def f(tensor, fo2=fo2, fop=fop, bk=bk, wph=wph, wpl=wpl):
                    ps = self.bank(bk, tensor)
                    tensor.wait_ge(self.s_sy.h, ldm[fo2])
                    if fo2 == 0 and fop == 0:
                        tensor.wait_ge(self.c_ve.h, ve_rdy)
                        tensor.wait_ge(self.c_gc.h, gc_rdy)
                    mm = None
                    for fi in range(8):
                        xh = self.AB[:, self.BL_VH + fi * w:self.BL_VH + fi * w + w]
                        xl = self.AB[:, self.BL_VL + fi * w:self.BL_VL + fi * w + w]
                        wh = self.AB[:, wph + fi * 256 + fop * 128:
                                     wph + fi * 256 + fop * 128 + 128]
                        wl = self.AB[:, wpl + fi * 256 + fop * 128:
                                     wpl + fi * 256 + fop * 128 + 128]
                        tensor.matmul(ps[:, 0:w], wh, xh, start=(fi == 0), stop=False)
                        tensor.matmul(ps[:, 0:w], wl, xh, start=False, stop=False)
                        mm = tensor.matmul(ps[:, 0:w], wh, xl, start=False, stop=(fi == 7))
                    self.c_pe.inc(mm)
                self.sec("tensor", f)
                pemk[(fo2, fop)] = self.c_pe.n

            if fo2 % 4 == 3:
                def f(scalar, fo2=fo2):
                    scalar.wait_ge(self.c_pe.h, pemk[(fo2, 1)])
                    last = None
                    for fo in range(fo2 * 2 - 6, fo2 * 2 + 2):
                        d = self.AR[:, self.SL_GU + fo * w:self.SL_GU + fo * w + w]
                        last = scalar.activation(d, self.PS[fo % 8][:, 0:w], AF.Gelu,
                                                 bias=self.B1P[:, fo:fo + 1], scale=1.0)
                    self.c_ac.inc(last)
                self.sec("scalar", f)
                for b in range(8):
                    self.set_guard(b, self.c_ac)
                for fo in range(fo2 * 2 - 6, fo2 * 2 + 2):
                    self.acm_w1[fo] = self.c_ac.n
        self.wp_pe = pemk[(15, 1)]

    def emit_w2(self, w):
        dldm = {}
        dvem = {}
        dpem = {}
        wp0 = self.wp_pe
        for fi in range(32):
            wph = self.BL_WP + (fi % 2) * 1024
            wpl = self.BL_WP + 4096 + (fi % 2) * 1024

            def f(sync, fi=fi, wph=wph, wpl=wpl):
                if fi < 2:
                    sync.wait_ge(self.c_pe.h, wp0)
                else:
                    sync.wait_ge(self.c_pe.h, dpem[fi - 2])
                blk = self.ds["ws_w2h"][:, fi * 1024:(fi + 1) * 1024]
                self.s_sy.inc(sync.dma_start(out=self.AB[:, wph:wph + 1024], in_=blk), 16)
                blk = self.ds["ws_w2l"][:, fi * 1024:(fi + 1) * 1024]
                self.s_sy.inc(sync.dma_start(out=self.AB[:, wpl:wpl + 1024], in_=blk), 16)
            self.sec("sync", f)
            dldm[fi] = self.s_sy.n

            def f(vector, fi=fi):
                if fi >= 2:
                    vector.wait_ge(self.c_pe.h, dpem[fi - 2])
                vector.wait_ge(self.c_ac.h, self.acm_w1[fi])
                hc = self.AR[:, self.SL_GU + fi * w:self.SL_GU + fi * w + w]
                hh = self.AB[:, self.BL_T + (fi % 2) * 512:self.BL_T + (fi % 2) * 512 + w]
                hl = self.AB[:, self.BL_T + 1024 + (fi % 2) * 512:
                             self.BL_T + 1024 + (fi % 2) * 512 + w]
                vector.tensor_copy(hh, hc)
                self.c_ve.inc(vector.tensor_sub(hl, hc, hh))
            self.sec("vector", f)
            dvem[fi] = self.c_ve.n

            def f(tensor, fi=fi, wph=wph, wpl=wpl):
                if fi == 0:
                    for b in range(8):
                        self.bank(b, tensor)
                tensor.wait_ge(self.s_sy.h, dldm[fi])
                tensor.wait_ge(self.c_ve.h, dvem[fi])
                hh = self.AB[:, self.BL_T + (fi % 2) * 512:self.BL_T + (fi % 2) * 512 + w]
                hl = self.AB[:, self.BL_T + 1024 + (fi % 2) * 512:
                             self.BL_T + 1024 + (fi % 2) * 512 + w]
                mm = None
                for fo in range(8):
                    ps = self.PS[fo][:, 0:w]
                    wh = self.AB[:, wph + fo * 128:wph + (fo + 1) * 128]
                    wl = self.AB[:, wpl + fo * 128:wpl + (fo + 1) * 128]
                    tensor.matmul(ps, wh, hh, start=(fi == 0), stop=False)
                    tensor.matmul(ps, wl, hh, start=False, stop=False)
                    mm = tensor.matmul(ps, wh, hl, start=False, stop=(fi == 31))
                self.c_pe.inc(mm)
            self.sec("tensor", f)
            dpem[fi] = self.c_pe.n
        self.wp_pe = dpem[31]

    def emit_finish(self, tau):
        CIN = self.SL_Y1
        w = HT
        B = self.BIAS
        for hv in range(2):
            def f(sync, hv=hv):
                sync.wait_ge(self.c_ve.h, self.c_ve.n)
                sync.wait_ge(self.c_ac.h, self.c_ac.n)
                self.s_sy.inc(sync.dma_start(out=self.AR[:, self.SL_GU:self.SL_GU + 5120],
                                             in_=self.di["convbd"][:]), 16)
            self.sec("sync", f)
            ldcv = self.s_sy.n

            pemc = {}
            for p in range(8):
                bk = 6 + p % 2

                def f(tensor, p=p, bk=bk, hv=hv):
                    ps = self.bank(bk, tensor)
                    tensor.wait_ge(self.s_sy.h, ldcv)
                    if p < 2:
                        tensor.wait_ge(self.c_ve.h, self.c_ve.n)
                    mm = None
                    for k in range(5):
                        lhs = self.AR[:, self.SL_GU + (p * 5 + k) * 128:
                                      self.SL_GU + (p * 5 + k) * 128 + 128]
                        rhs = self.AR[:, CIN + 516 * p + 256 * hv + k:
                                      CIN + 516 * p + 256 * hv + k + w]
                        mm = tensor.matmul(ps[:, 0:w], lhs, rhs,
                                           start=(k == 0), stop=(k == 4))
                    self.c_pe.inc(mm)
                self.sec("tensor", f)
                pemc[p] = self.c_pe.n

                def f(vector, p=p, bk=bk, hv=hv):
                    vector.wait_ge(self.c_pe.h, pemc[p])
                    vector.tensor_scalar(self.T0[:, 0:w], self.PS[bk][:, 0:w],
                                         B[:, 120:121], self.GCB[:, p:p + 1],
                                         ALU.mult, ALU.add)
                    d = self.AR[:, self.SL_V + p * w:self.SL_V + p * w + w]
                    self.c_ve.inc(vector.tensor_tensor(
                        d, self.T0[:, 0:w],
                        self.AR[:, CIN + 516 * p + 2 + 256 * hv:
                                CIN + 516 * p + 2 + 256 * hv + w], ALU.add))
                self.sec("vector", f)
                self.set_guard(bk, self.c_ve)

            self.emit_sums(self.SL_V, w, self.SL_RAW + 2048, 6,
                           waits=[(self.c_ve, self.c_ve.n)])
            self.emit_ln_stats(6, w, "lna", minmax=False)
            self.emit_ln_apply(self.SL_V, self.SL_V, w, "lna",
                               gcol=B[:, 88:96], bcol=B[:, 96:104])
            self.emit_split(self.SL_V, self.BL_VH, self.BL_VL, w)

            ldy1 = self.emit_load("s_y1", self.SL_V + 2048, tau * TT + hv * w, w,
                                  eng="gpsimd", waits=[(self.c_ve, self.c_ve.n)])
            ldv2 = self.emit_load("s_v", self.SL_RAW, tau * TT + hv * w, w,
                                  eng="gpsimd")

            self.emit_w1(w)
            self.emit_w2(w)

            def f(vector, hv=hv, ldy1=ldy1, ldv2=ldv2):
                vector.wait_ge(self.c_pe.h, self.c_pe.n)
                vector.wait_ge(self.s_gp.h, max(ldy1, ldv2))
                last = None
                for fo in range(8):
                    ps = self.PS[fo][:, 0:w]
                    vector.tensor_scalar(self.T0[:, 0:w], ps, B[:, 104 + fo:105 + fo],
                                         None, ALU.add)
                    vector.tensor_tensor(self.T1[:, 0:w], self.T0[:, 0:w],
                                         self.AR[:, self.SL_V + fo * w:
                                                 self.SL_V + fo * w + w], ALU.add)
                    vector.tensor_tensor(self.T0[:, 0:w], self.T1[:, 0:w],
                                         self.AR[:, self.SL_RAW + fo * w:
                                                 self.SL_RAW + fo * w + w], ALU.add)
                    last = vector.tensor_tensor(
                        self.AR[:, self.SL_V + fo * w:self.SL_V + fo * w + w],
                        self.T0[:, 0:w],
                        self.AR[:, self.SL_V + 2048 + fo * w:
                                self.SL_V + 2048 + fo * w + w], ALU.add)
                self.c_ve.inc(last)
            self.sec("vector", f)
            for b in range(8):
                self.set_guard(b, self.c_ve)

            self.emit_sums(self.SL_V, w, self.SL_RAW + 2048, 6,
                           waits=[(self.c_ve, self.c_ve.n)])
            self.emit_minmax(self.SL_V, w, 7, "ln2_mx", "ln2_mn",
                             waits=[(self.c_ve, self.c_ve.n)])
            self.emit_ln_stats(6, w, "ln2", minmax=True)
            self.emit_quant(self.SL_V, w, self.BL_UQ, "ln2", 2, 3)
            self.emit_ffn(self.BL_UQ, w, 0, w, "ln2", self.SL_RAW, self.SL_V, w)

            def f(gpsimd, hv=hv):
                gpsimd.wait_ge(self.c_ve.h, self.c_ve.n)
                src = self.AR[:, self.SL_RAW:self.SL_RAW + 8 * w].rearrange(
                    "p (c t) -> p c t", t=w)
                dst = self.fm(self.out, tau * TT + hv * w, w)
                self.s_st.inc(gpsimd.dma_start(out=dst, in_=src), 16)
            self.sec("gpsimd", f)

    def emit_sweep3(self):
        CIN = self.SL_Y1

        ldm = self.emit_load("s_pq", self.SL_X, TOK, HALO, eng="gpsimd")
        self.emit_attn(self.SL_X, HALO, 0, HALO,
                       lambda s: self.MINIA[:, 4 * s:4 * s + 4], use_mask=True,
                       waits=[(self.s_gp, ldm)])

        def f(vector):
            vector.wait_ge(self.c_ve.h, self.c_ve.n)
            last = None
            for p in range(8):
                last = vector.tensor_copy(self.AR[:, CIN + 516 * p:CIN + 516 * p + 2],
                                          self.MINIA[:, 4 * p:4 * p + 2])
            self.c_ve.inc(last)
        self.sec("vector", f)
        self.barrier()

        for t in range(NT):
            ldp = self.emit_load("s_pq", self.SL_X, t * TT, TT, eng="gpsimd",
                                 waits=[(self.c_pe, self.c_pe.n)])
            if t >= 1:
                self.emit_attn(self.SL_X, TT, 0, 2,
                               lambda s: self.AR[:, CIN + 516 * s + 514:CIN + 516 * s + 516],
                               waits=[(self.s_gp, ldp)])
                self.emit_finish(t - 1)

                def f(vector):
                    vector.wait_ge(self.c_ve.h, self.c_ve.n)
                    vector.wait_ge(self.c_ac.h, self.c_ac.n)
                    last = None
                    for p in range(8):
                        last = vector.tensor_copy(
                            self.AR[:, CIN + 516 * p:CIN + 516 * p + 2],
                            self.AR[:, CIN + 516 * p + 512:CIN + 516 * p + 514])
                    self.c_ve.inc(last)
                self.sec("vector", f)
            self.emit_attn(self.SL_X, TT, 0, TT,
                           lambda s: self.AR[:, CIN + 516 * s + 2:CIN + 516 * s + 514],
                           waits=[(self.s_gp, ldp), (self.c_ve, self.c_ve.n)])

            def f(vector):
                vector.wait_ge(self.s_st.h, self.s_st.n)
                vector.wait_ge(self.s_gp.h, self.s_gp.n)
            self.sec("vector", f)
        self.barrier()

        def f(vector):
            last = None
            for p in range(8):
                last = vector.tensor_copy(
                    self.AR[:, CIN + 516 * p + 514:CIN + 516 * p + 516],
                    self.MINIA[:, 4 * p + 2:4 * p + 4])
            self.c_ve.inc(last)
        self.sec("vector", f)
        self.emit_finish(NT - 1)
        self.barrier()

    def emit_taps_and_close(self):
        if self.taps:
            def f(gpsimd):
                gpsimd.wait_ge(self.c_ve.h, self.c_ve.n)
                gpsimd.wait_ge(self.c_pe.h, self.c_pe.n)
                gpsimd.wait_ge(self.c_ac.h, self.c_ac.n)
                gpsimd.wait_ge(self.c_gc.h, self.c_gc.n)
                gpsimd.wait_ge(self.s_st.h, self.s_st.n)
                gpsimd.wait_ge(self.s_sy.h, self.s_sy.n)
                for nm in ("s_y1", "s_v", "s_pq", "s_kk", "s_fk", "s_vv",
                           "ws_wgq", "ws_wdq", "wsh_q", "wsl_q", "ws_w2h"):
                    self.s_st.inc(gpsimd.dma_start(out=self.tap_t[nm][:],
                                                   in_=self.ds[nm][:]), 16)
                self.s_st.inc(gpsimd.dma_start(out=self.tap_t["qg"][:],
                                               in_=self.QG[:]), 16)
                self.s_st.inc(gpsimd.dma_start(out=self.tap_t["wsr"][:],
                                               in_=self.WS[:]), 16)
                self.s_st.inc(gpsimd.dma_start(out=self.tap_t["const"][:],
                                               in_=self.CONST[:]), 16)
                self.s_st.inc(gpsimd.dma_start(out=self.tap_t["uq"][:],
                                               in_=self.AB[:, self.BL_UQ:self.BL_UQ + 4096]), 16)
                self.s_st.inc(gpsimd.dma_start(out=self.tap_t["gu"][:],
                                               in_=self.AR[:, self.SL_GU:self.SL_GU + 8192]), 16)
                self.s_st.inc(gpsimd.dma_start(out=self.tap_t["bc"][:],
                                               in_=self.AR[:, self.SL_BC:self.SL_BC + 3072]), 16)
                self.s_st.inc(gpsimd.dma_start(out=self.tap_t["st"][:],
                                               in_=self.ST[:]), 16)
                self.s_st.inc(gpsimd.dma_start(out=self.tap_t["kv"][:],
                                               in_=self.KVSB[:]), 16)
            self.sec("gpsimd", f)

        def f(gpsimd):
            gpsimd.wait_ge(self.s_st.h, self.s_st.n)
        self.sec("gpsimd", f)


def _pack_col(v):
    return np.ascontiguousarray(np.asarray(v, dtype=np.float32).reshape(-1, 128).T)


def build_nc(taps=False):
    return Builder(taps=taps, stages=_STAGES).build()


def kernel(L0_attn_x, L0_attn_Wq, L0_attn_bq, L0_attn_lnq_g, L0_attn_lnq_b,
           L0_attn_Wk, L0_attn_bk, L0_attn_lnk_g, L0_attn_lnk_b,
           L0_attn_Wv, L0_attn_bv, L0_attn_lnv_g, L0_attn_lnv_b,
           L0_attn_Wphi_q, L0_attn_bphi_q, L0_attn_Wphi_k, L0_attn_bphi_k,
           L0_attn_lna_g, L0_attn_lna_b, L0_attn_W1, L0_attn_b1,
           L0_attn_W2, L0_attn_b2, L0_attn_gamma, L0_attn_conv_w, L0_attn_conv_b,
           L0_ffn_w_gate, L0_ffn_w_up, L0_ffn_w_down):
    x = np.asarray(L0_attn_x, dtype=np.float32)
    f32 = lambda a: np.ascontiguousarray(np.asarray(a, dtype=np.float32))

    bias = np.zeros((128, 160), dtype=np.float32)
    cols = [L0_attn_bq, L0_attn_bk, L0_attn_bv, L0_attn_bphi_q, L0_attn_bphi_k,
            L0_attn_lnq_g, L0_attn_lnq_b, L0_attn_lnk_g, L0_attn_lnk_b,
            L0_attn_lnv_g, L0_attn_lnv_b, L0_attn_lna_g, L0_attn_lna_b,
            L0_attn_b2, L0_attn_conv_b]
    for i, v in enumerate(cols):
        bias[:, 8 * i:8 * (i + 1)] = _pack_col(v)
    bias[:, 120] = np.float32(L0_attn_gamma)
    b1p = _pack_col(L0_attn_b1)

    convw = np.asarray(L0_attn_conv_w, dtype=np.float32)
    convbd = np.zeros((128, 5120), dtype=np.float32)
    for p in range(8):
        for k in range(5):
            c0 = (p * 5 + k) * 128
            convbd[0:64, c0:c0 + 64] = convw[128 * p:128 * p + 64, :, k].T
            convbd[64:128, c0 + 64:c0 + 128] = convw[128 * p + 64:128 * p + 128, :, k].T

    eye = np.eye(128, dtype=np.float32)
    selp = np.zeros((16, 264), dtype=np.float32)
    for h in range(16):
        for p in range(128):
            selp[h, p] = 1.0 if (p // 64) == (h % 2) else 0.0
    for r in range(2):
        for p in range(128):
            selp[r, 128 + p] = 1.0 if (p // 64) == r else 0.0
    for h in range(16):
        for s in range(8):
            selp[h, 256 + s] = 1.0 if (h // 2) == s else 0.0

    shared = {
        "wq": f32(L0_attn_Wq), "wk": f32(L0_attn_Wk), "wv": f32(L0_attn_Wv),
        "wpq": f32(L0_attn_Wphi_q), "wpk": f32(L0_attn_Wphi_k),
        "w1": f32(L0_attn_W1), "w2": f32(L0_attn_W2),
        "wg": f32(np.asarray(L0_ffn_w_gate).T), "wu": f32(np.asarray(L0_ffn_w_up).T),
        "wd": f32(np.asarray(L0_ffn_w_down).T),
        "bias": bias, "b1p": b1p, "convbd": convbd, "eye": eye, "selp": selp,
    }

    in_maps = []
    for c in range(NCORES):
        b, q = c // 4, c % 4
        s0 = q * TOK
        xcore = np.zeros((E, TOKH), dtype=np.float32)
        xcore[:, 0:TOK] = x[b, s0:s0 + TOK, :].T
        for j, idx in enumerate((s0 - 2, s0 - 1, s0 + TOK, s0 + TOK + 1)):
            if 0 <= idx < LQ:
                xcore[:, TOK + j] = x[b, idx, :]
        mask = np.array([[1.0 if q > 0 else 0.0, 1.0 if q > 0 else 0.0,
                          1.0 if q < 3 else 0.0, 1.0 if q < 3 else 0.0]],
                        dtype=np.float32)
        m = dict(shared)
        m["x"] = xcore
        m["mask"] = mask
        in_maps.append(m)

    nc = build_nc(taps=_TAPS)
    res = run_bass_kernel_spmd(nc, in_maps, core_ids=list(range(NCORES)))

    out = np.empty((2, LQ, E), dtype=np.float32)
    for c in range(NCORES):
        b, q = c // 4, c % 4
        out[b, q * TOK:(q + 1) * TOK, :] = res.results[c]["out"].T
    if _TAPS:
        kernel._last_results = res
    return out
